# version 04
# baseline (speedup 1.0000x reference)
import contextlib
import numpy as np
import ml_dtypes
import concourse.bass as bass
import concourse.mybir as mybir
from concourse.bass_utils import run_bass_kernel_spmd

F32 = mybir.dt.float32
BF16 = mybir.dt.bfloat16
AF = mybir.ActivationFunctionType
ALU = mybir.AluOpType
NPBF16 = ml_dtypes.bfloat16

NCORES = 8
S = 8192
D = 2048
TOK = S // NCORES
NCH = D // 128
LN_EPS = 1e-5
RMS_EPS = float(np.finfo(np.float32).eps)
EXPC = 10.0
NH0, HD0, ROT0 = 32, 64, 32
HPC0 = NH0 // NCORES
SC0 = 0.125
NH1, NKV1, HD1, WIN = 16, 4, 128, 1024
HPC1 = NH1 // NCORES
SC1 = float(1.0 / np.sqrt(HD1))
H0 = 4 * D
H1 = 8192
NKT = S // 128


class SemT:
    __slots__ = ("h", "n")
    def __init__(self, h):
        self.h = h
        self.n = 0


class Prog:
    ENG = ("sync", "gpsimd", "scalar", "vector", "tensor")

    def __init__(self, nc):
        self.nc = nc
        self.stack = contextlib.ExitStack()
        self.scopes = [self.stack]
        self.q = {e: [] for e in self.ENG}
        self._sems = {}
        self.dma_sems = []
        self._uid = 0
        self.c = {e: self.sem("c_" + e) for e in ("tensor", "vector", "scalar", "gpsimd")}

    def sem(self, name):
        if name not in self._sems:
            self._sems[name] = SemT(self.stack.enter_context(self.nc.semaphore(name)))
        return self._sems[name]

    def sbuf(self, name, shape, dt):
        self._uid += 1
        return self.scopes[-1].enter_context(
            self.nc.sbuf_tensor(f"{name}_{self._uid}", shape, dt))

    def psum(self, name, shape, dt=F32):
        self._uid += 1
        return self.scopes[-1].enter_context(
            self.nc.psum_tensor(f"{name}_{self._uid}", shape, dt))

    @contextlib.contextmanager
    def scope(self):
        es = contextlib.ExitStack()
        self.scopes.append(es)
        try:
            yield
        finally:
            self.scopes.pop()
            es.close()

    def barrier(self):
        marks = {e: self.c[e].n for e in ("tensor", "vector", "scalar", "gpsimd")}
        dmarks = [(sm, sm.n) for sm in self.dma_sems]
        for e in self.ENG:
            for o, v in marks.items():
                if o != e:
                    self.wait(e, self.c[o], v)
            for sm, v in dmarks:
                self.wait(e, sm, v)

    def call(self, eng, method, **kw):
        cnt = self.c[eng]
        h = cnt.h

        def f(E, m=method, kw=kw, h=h):
            getattr(E, m)(**kw).then_inc(h, 1)

        self.q[eng].append(f)
        cnt.n += 1
        return cnt.n

    def calln(self, eng, method, **kw):
        def f(E, m=method, kw=kw):
            getattr(E, m)(**kw)

        self.q[eng].append(f)

    def dma(self, eng, out, in_, sem, **kw):
        h = sem.h
        if sem not in self.dma_sems:
            self.dma_sems.append(sem)

        def f(E, kw=kw, h=h, out=out, in_=in_):
            E.dma_start(out=out, in_=in_, **kw).then_inc(h, 16)

        self.q[eng].append(f)
        sem.n += 16
        return sem.n

    def wait(self, eng, sem, val=None):
        v = sem.n if val is None else val
        if v <= 0:
            return
        self.q[eng].append(lambda E, h=sem.h, v=v: E.wait_ge(h, v))

    def finish(self):
        with self.nc.Block() as block:
            for e in self.ENG:
                if self.q[e]:
                    getattr(block, e)(lambda E, lst=tuple(self.q[e]): [f(E) for f in lst])
        self.stack.close()
        return self.nc


def new_nc():
    return bass.Bass("TRN2", target_bir_lowering=False)


def run8(nc, in_maps):
    return run_bass_kernel_spmd(nc, in_maps, core_ids=list(range(NCORES))).results


def make_consts(P, extra=()):
    consts = {}
    tile = P.sbuf("consts", [128, 8], F32)
    ones1 = P.sbuf("ones1", [1, 128], F32)
    onesb = P.sbuf("onesb", [128, 1], BF16)
    P.calln('vector', 'memset', ap=ones1[:, :], constant=1.0)
    P.calln('vector', 'memset', ap=onesb[:, :], constant=1.0)
    consts['ones1_f32'] = ones1
    consts['ones_bf'] = onesb
    zb = P.sbuf("zero_b", [128, 1], F32)
    P.calln('vector', 'memset', ap=zb[:, :], constant=0.0)
    consts['zero_b'] = zb
    col = 0
    for nm, val in extra:
        P.calln('vector', 'memset', ap=tile[:, col:col + 1], constant=float(val))
        consts[nm] = tile[:, col:col + 1]
        col += 1
    consts['_v'] = P.call('vector', 'tensor_copy', out=tile[:, 7:8], in_=tile[:, 7:8])
    return consts


def load_w_cols(P, eng, sem, w_dram, c0, ncols, dest_sb, dest_off, kdim):
    nkc = kdim // 128
    v = sem.n
    for kc0 in range(0, nkc, 4):
        kc1 = min(kc0 + 4, nkc)
        src = w_dram[kc0 * 128:kc1 * 128, c0:c0 + ncols].rearrange(
            "(kc p) n -> p kc n", p=128)
        dst = dest_sb[:, dest_off + kc0 * ncols: dest_off + kc1 * ncols].rearrange(
            "p (kc n) -> p kc n", kc=kc1 - kc0)
        v = P.dma(eng, dst, src, sem)
    return v


def stats_chunk(P, consts, src_f32_or_none, src_bf, sq_bf, ps_s1, ps_s2,
                acc1, acc2, T):
    onesb = consts['ones_bf']
    ventry = P.c['vector'].n
    sentry = P.c['scalar'].n
    tentry = P.c['tensor'].n
    if src_f32_or_none is not None:
        P.wait('vector', P.c['scalar'], sentry)
        vv = P.call('vector', 'tensor_copy', out=src_bf, in_=src_f32_or_none)
        P.wait('scalar', P.c['vector'], vv)
    P.wait('scalar', P.c['tensor'], tentry)
    sv = P.call('scalar', 'square', out=sq_bf[:, 0:T], in_=src_bf)
    P.wait('tensor', P.c['scalar'], sv)
    P.wait('tensor', P.c['vector'], ventry)
    for h0 in range(0, T, 512):
        P.call('tensor', 'matmul', out=ps_s1[0:1, h0:h0 + 512], lhsT=onesb[:, :],
               rhs=src_bf[:, h0:h0 + 512], start=True, stop=True)
        P.call('tensor', 'matmul', out=ps_s2[0:1, h0:h0 + 512], lhsT=onesb[:, :],
               rhs=sq_bf[:, h0:h0 + 512], start=True, stop=True)
    tv = P.c['tensor'].n
    P.wait('vector', P.c['tensor'], tv)
    P.call('vector', 'tensor_add', out=acc1[:, 0:T], in0=acc1[:, 0:T],
           in1=ps_s1[0:1, 0:T])
    vv = P.call('vector', 'tensor_add', out=acc2[:, 0:T], in0=acc2[:, 0:T],
                in1=ps_s2[0:1, 0:T])
    return vv


def ln_finalize(P, consts, nfeat, acc1, acc2, rA, rR, eps_name, T):
    inv_n = 1.0 / nfeat
    P.call('vector', 'tensor_scalar_mul', out=rA[:, 0:T], in0=acc1[:, 0:T], scalar1=inv_n)
    P.call('vector', 'tensor_scalar_mul', out=rR[:, 0:T], in0=acc2[:, 0:T], scalar1=inv_n)
    P.call('vector', 'tensor_mul', out=acc1[:, 0:T], in0=rA[:, 0:T], in1=rA[:, 0:T])
    vw = P.call('vector', 'tensor_sub', out=rR[:, 0:T], in0=rR[:, 0:T], in1=acc1[:, 0:T])
    P.wait('scalar', P.c['vector'], vw)
    sw = P.call('scalar', 'activation', out=acc2[:, 0:T], in_=rR[:, 0:T],
                func=AF.Sqrt, bias=consts[eps_name][0:1, 0:1], scale=1.0)
    P.wait('vector', P.c['scalar'], sw)
    P.call('vector', 'reciprocal', out=rR[:, 0:T], in_=acc2[:, 0:T])
    P.call('vector', 'tensor_mul', out=rA[:, 0:T], in0=rA[:, 0:T], in1=rR[:, 0:T])
    vv = P.call('vector', 'tensor_scalar_mul', out=rA[:, 0:T], in0=rA[:, 0:T], scalar1=-1.0)
    return vv


def bcast2(P, consts, rR, rA, psR, psA, Rb, Ab, T):
    ones1 = consts['ones1_f32']
    for row, ps in ((rR, psR), (rA, psA)):
        for h0 in range(0, T, 512):
            P.call('tensor', 'matmul', out=ps[:, h0:h0 + 512], lhsT=ones1[:, :],
                   rhs=row[:, h0:h0 + 512], start=True, stop=True)
    tw = P.c['tensor'].n
    P.wait('vector', P.c['tensor'], tw)
    P.call('vector', 'tensor_copy', out=Rb[:, 0:T], in_=psR[:, 0:T])
    vv = P.call('vector', 'tensor_copy', out=Ab[:, 0:T], in_=psA[:, 0:T])
    return vv


def _rope_tables_l0():
    inv = 1.0 / (10000.0 ** (np.arange(0, ROT0, 2, dtype=np.float64) / ROT0))
    f = np.outer(np.arange(S, dtype=np.float64), inv)
    c = np.cos(f).T.astype(np.float32)
    s = np.sin(f).T.astype(np.float32)
    C = np.zeros((128, S), np.float32)
    Sx = np.zeros((128, S), np.float32)
    for b in (0, 64):
        C[b:b + 16] = c
        C[b + 16:b + 32] = c
        C[b + 32:b + 64] = 1.0
        Sx[b:b + 16] = s
        Sx[b + 16:b + 32] = s
    P = np.zeros((128, 128), np.float32)
    for b in (0, 64):
        for i in range(16):
            P[b + 16 + i, b + i] = -1.0
            P[b + i, b + 16 + i] = 1.0
    return C.astype(NPBF16), Sx.astype(NPBF16), P.astype(NPBF16)


def _rope_tables_l1():
    inv = 1.0 / (10000.0 ** (np.arange(0, HD1, 2, dtype=np.float64) / HD1))
    f = np.outer(np.arange(S, dtype=np.float64), inv)
    emb = np.concatenate([f, f], axis=1)
    C = np.cos(emb).T.astype(np.float32)
    Sx = np.sin(emb).T.astype(np.float32)
    P = np.zeros((128, 128), np.float32)
    for i in range(64):
        P[64 + i, i] = -1.0
        P[i, 64 + i] = 1.0
    return C.astype(NPBF16), Sx.astype(NPBF16), P.astype(NPBF16)


def _masks_l0():
    q = np.arange(512)[None, :]
    k = np.arange(128)[:, None]
    return np.stack([(q - k - 128 * i >= 0) for i in range(4)]).astype(NPBF16)


def _masks_l1():
    q = np.arange(512)[None, :]
    k = np.arange(128)[:, None]
    out = []
    for t in range(12):
        d = q - k + 1024 - 128 * t
        out.append(((d >= 0) & (d <= WIN)))
    return np.stack(out).astype(NPBF16)


def _vaug_l0(v_t_full):
    out = []
    vt = np.asarray(v_t_full)
    for c in range(NCORES):
        arr = np.ones((HPC0, 128, NKT, 65), NPBF16)
        for hh in range(HPC0):
            h = HPC0 * c + hh
            vh = vt[64 * h:64 * h + 64, :]
            vr = vh.reshape(64, NKT, 128)
            arr[hh, :, :, :64] = vr.transpose(2, 1, 0)
        out.append(arr)
    return out


def _group_mats():
    Ebf = np.zeros((128, 2), np.float32)
    Ebf[:64, 0] = 1.0
    Ebf[64:, 1] = 1.0
    Ebc = np.ascontiguousarray(Ebf.T)
    return Ebf.astype(NPBF16), Ebc.astype(np.float32)


def build_ln0(_P=None, _consts=None, _io=None):
    if _P is None:
        nc = new_nc()
        _io = {
            "x_own": nc.dram_tensor("x_own", [D, TOK], F32, kind="ExternalInput").ap(),
            "xn": nc.dram_tensor("xn", [D, TOK], BF16, kind="ExternalOutput").ap(),
        }
        P = Prog(nc)
        consts = make_consts(P, [("eps_ln", LN_EPS)])
    else:
        P, consts = _P, _consts
    x_in = _io["x_own"]
    xn_out = _io["xn"]

    x_sb = P.sbuf("x_sb", [128, NCH * TOK], F32)
    xb_sb = P.sbuf("xb_sb", [128, NCH * TOK], BF16)
    sq_sb = P.sbuf("sq_sb", [128, TOK], BF16)
    acc1 = P.sbuf("acc1", [1, TOK], F32)
    acc2 = P.sbuf("acc2", [1, TOK], F32)
    rA = P.sbuf("rA", [1, TOK], F32)
    rR = P.sbuf("rR", [1, TOK], F32)
    Rb = P.sbuf("Rb", [128, TOK], F32)
    Ab = P.sbuf("Ab", [128, TOK], F32)
    ps_s1 = P.psum("ps_s1", [1, TOK])
    ps_s2 = P.psum("ps_s2", [1, TOK])
    ps_b1 = P.psum("ps_b1", [128, TOK])
    ps_b2 = P.psum("ps_b2", [128, TOK])
    ld = P.sem("ld")
    st = P.sem("st")

    P.calln('vector', 'memset', ap=acc1[:, :], constant=0.0)
    P.calln('vector', 'memset', ap=acc2[:, :], constant=0.0)
    P.wait('scalar', P.c['vector'], consts['_v'])

    for ch in range(NCH):
        xs = x_sb[:, ch * TOK:(ch + 1) * TOK]
        xbs = xb_sb[:, ch * TOK:(ch + 1) * TOK]
        lv = P.dma('sync', xs, x_in[ch * 128:(ch + 1) * 128, :], ld)
        P.wait('vector', ld, lv)
        stats_chunk(P, consts, xs, xbs, sq_sb, ps_s1, ps_s2, acc1, acc2, TOK)

    vv = ln_finalize(P, consts, D, acc1, acc2, rA, rR, "eps_ln", TOK)
    P.wait('tensor', P.c['vector'], vv)
    vv = bcast2(P, consts, rR, rA, ps_b1, ps_b2, Rb, Ab, TOK)

    for ch in range(NCH):
        xs = x_sb[:, ch * TOK:(ch + 1) * TOK]
        xbs = xb_sb[:, ch * TOK:(ch + 1) * TOK]
        P.call('vector', 'tensor_mul', out=xs, in0=xs, in1=Rb[:, :])
        vv = P.call('vector', 'tensor_add', out=xbs, in0=xs, in1=Ab[:, :])
        P.wait('sync', P.c['vector'], vv)
        P.dma('sync', xn_out[ch * 128:(ch + 1) * 128, :], xbs, st)
    P.wait('sync', st, st.n)
    if _P is None:
        return P.finish()


def qkv_projs(layer):
    if layer == 0:
        return [("q", 256, True, True), ("k", 256, True, True), ("v", 256, True, False)]
    return [("q", 256, False, True), ("k", 128, False, True), ("v", 128, False, False)]


def build_qkv(layer, _P=None, _consts=None, _io=None):
    projs = qkv_projs(layer)
    if _P is None:
        nc = new_nc()
        _io = {}
        xn_full = nc.dram_tensor("xn_full", [D, S], BF16, kind="ExternalInput").ap()
        _io["xn_block"] = (lambda tb, kc0, kc1, _x=xn_full:
                           _x[kc0 * 128:kc1 * 128, tb * 512:(tb + 1) * 512])
        for nm, ncol, has_b, _ in projs:
            _io[f"w{nm}"] = nc.dram_tensor(f"w{nm}", [D, ncol], F32,
                                           kind="ExternalInput").ap()
            if has_b:
                _io[f"b{nm}"] = nc.dram_tensor(f"b{nm}", [128, ncol // 128], F32,
                                               kind="ExternalInput").ap()
            _io[f"{nm}t"] = nc.dram_tensor(f"{nm}t", [ncol, S], BF16,
                                           kind="ExternalOutput").ap()
        _io["Ctab"] = nc.dram_tensor("Ctab", [128, S], BF16, kind="ExternalInput").ap()
        _io["Stab"] = nc.dram_tensor("Stab", [128, S], BF16, kind="ExternalInput").ap()
        _io["Pmat"] = nc.dram_tensor("Pmat", [128, 128], BF16, kind="ExternalInput").ap()
        P = Prog(nc)
        consts = make_consts(P)
    else:
        P, consts = _P, _consts
    w_in = {nm: _io[f"w{nm}"] for nm, _, _, _ in projs}
    b_in = {nm: _io[f"b{nm}"] for nm, _, has_b, _ in projs if has_b}
    outs = {nm: _io[f"{nm}t"] for nm, _, _, _ in projs}
    C_in = _io["Ctab"]
    S_in = _io["Stab"]
    Pm_in = _io["Pmat"]
    xn_block = _io["xn_block"]
    ncols_tot = sum(p[1] for p in projs)
    n_oc = ncols_tot // 128

    w_sb = P.sbuf("w_sb", [128, NCH * ncols_tot], BF16)
    b_sb = P.sbuf("b_sb", [128, 8], F32)
    pm_sb = P.sbuf("pm_sb", [128, 128], BF16)
    ct_sb = P.sbuf("ct_sb", [128, 2 * 512], BF16)
    stt_sb = P.sbuf("stt_sb", [128, 2 * 512], BF16)
    xn_sb = P.sbuf("xn_sb", [128, 2 * NCH * 512], BF16)
    ev_sb = P.sbuf("ev_sb", [128, 4 * 512], BF16)
    rp_sb = P.sbuf("rp_sb", [128, 2 * 512], BF16)
    ps = [P.psum(f"ps{i}", [128, 512]) for i in range(6)]
    ldw = P.sem("ldw")
    ldx = P.sem("ldx")
    ldt = P.sem("ldt")
    sto = P.sem("sto")

    woff, wslices, bslices = 0, {}, {}
    for nm, ncol, has_b, _ in projs:
        load_w_cols(P, 'gpsimd', ldw, w_in[nm], 0, ncol, w_sb, woff, D)
        wslices[nm] = woff
        woff += NCH * ncol
    P.dma('gpsimd', pm_sb[:, :], Pm_in[:, :], ldw)
    boff = 0
    for nm, ncol, has_b, _ in projs:
        if has_b:
            nchk = ncol // 128
            P.dma('gpsimd', b_sb[:, boff:boff + nchk], b_in[nm][:, :], ldw)
            bslices[nm] = boff
            boff += nchk
    wld = ldw.n

    NTB = S // 512
    bank_evac = {}
    ev_dma = {}
    rp_state = [0, {}]
    tb_t = {}
    tb_v = {}
    oc_ctr = [0]

    for tb in range(NTB):
        buf = tb % 2
        xn_tb = xn_sb[:, buf * NCH * 512:(buf + 1) * NCH * 512]
        if tb >= 2:
            P.wait('sync', P.c['tensor'], tb_t[tb - 2])
            P.wait('sync', P.c['vector'], tb_v[tb - 2])
        lv = None
        for kc0 in range(0, NCH, 4):
            src = xn_block(tb, kc0, kc0 + 4).rearrange("(kc p) n -> p kc n", p=128)
            dst = xn_tb[:, kc0 * 512:(kc0 + 4) * 512].rearrange("p (kc n) -> p kc n", kc=4)
            lv = P.dma('sync', dst, src, ldx)
        ctv = P.dma('sync', ct_sb[:, buf * 512:(buf + 1) * 512],
                    C_in[:, tb * 512:(tb + 1) * 512], ldt)
        stv = P.dma('sync', stt_sb[:, buf * 512:(buf + 1) * 512],
                    S_in[:, tb * 512:(tb + 1) * 512], ldt)

        if tb == 0:
            P.wait('tensor', ldw, wld)
            P.wait('scalar', P.c['vector'], consts['_v'])
        P.wait('tensor', ldx, lv)

        for nm, ncol, has_b, rope in projs:
            for oc in range(ncol // 128):
                og = oc_ctr[0]
                key = og % 4
                bank = ps[key]
                if og >= 4:
                    P.wait('tensor', P.c['scalar'], bank_evac[key])
                for kc in range(NCH):
                    P.call('tensor', 'matmul', out=bank[:, :],
                           lhsT=w_sb[:, wslices[nm] + kc * ncol + oc * 128:
                                     wslices[nm] + kc * ncol + oc * 128 + 128],
                           rhs=xn_tb[:, kc * 512:(kc + 1) * 512],
                           start=(kc == 0), stop=(kc == NCH - 1))
                tmm = P.c['tensor'].n
                ev = ev_sb[:, key * 512:(key + 1) * 512]
                P.wait('scalar', P.c['tensor'], tmm)
                if key in ev_dma:
                    P.wait('scalar', sto, ev_dma[key])
                if has_b:
                    sv = P.call('scalar', 'activation', out=ev, in_=bank[:, :],
                                func=AF.Identity, bias=b_sb[:, bslices[nm] + oc:
                                                        bslices[nm] + oc + 1], scale=1.0)
                else:
                    sv = P.call('scalar', 'copy', out=ev, in_=bank[:, :])
                bank_evac[key] = sv

                orow = oc * 128
                if rope:
                    rc = rp_state[0]
                    rkey = rc % 2
                    rbank = ps[4 + rkey]
                    P.wait('tensor', P.c['scalar'], sv)
                    if rkey in rp_state[1]:
                        P.wait('tensor', P.c['vector'], rp_state[1][rkey])
                    tmm2 = P.call('tensor', 'matmul', out=rbank[:, :], lhsT=pm_sb[:, :],
                                  rhs=ev, start=True, stop=True)
                    P.wait('vector', P.c['tensor'], tmm2)
                    P.wait('vector', ldt, stv)
                    rtmp = rp_sb[:, rkey * 512:(rkey + 1) * 512]
                    P.call('vector', 'tensor_mul', out=rtmp, in0=rbank[:, :],
                           in1=stt_sb[:, buf * 512:(buf + 1) * 512])
                    rp_state[1][rkey] = P.c['vector'].n
                    P.call('vector', 'tensor_mul', out=ev, in0=ev,
                           in1=ct_sb[:, buf * 512:(buf + 1) * 512])
                    vv = P.call('vector', 'tensor_add', out=ev, in0=ev, in1=rtmp)
                    rp_state[0] = rc + 1
                    P.wait('sync', P.c['vector'], vv)
                else:
                    P.wait('sync', P.c['scalar'], sv)
                ev_dma[key] = P.dma('sync', outs[nm][orow:orow + 128,
                                                     tb * 512:(tb + 1) * 512], ev, sto)
                oc_ctr[0] += 1
        tb_t[tb] = P.c['tensor'].n
        tb_v[tb] = P.c['vector'].n
    P.wait('sync', sto, sto.n)
    if _P is None:
        return P.finish()


def build_attn0(_P=None, _consts=None, _io=None):
    if _P is None:
        nc = new_nc()
        _io = {
            "qt": nc.dram_tensor("qt", [256, S], BF16, kind="ExternalInput").ap(),
            "kt": nc.dram_tensor("kt", [256, S], BF16, kind="ExternalInput").ap(),
            "m0": nc.dram_tensor("m0", [4, 128, 512], BF16, kind="ExternalInput").ap(),
        }
        v_in = nc.dram_tensor("vh", [HPC0, 128, NKT, 65], BF16, kind="ExternalInput").ap()

        def _load_v(P, consts, v_sb, ldsem):
            for hh in range(HPC0):
                P.dma('gpsimd', v_sb[:, hh * NKT * 65:(hh + 1) * NKT * 65].rearrange(
                    "p (t c) -> p t c", c=65), v_in[hh], ldsem)
            return ldsem.n

        _io["load_v"] = _load_v
        a_out_t = nc.dram_tensor("at", [256, S], BF16, kind="ExternalOutput").ap()
        _io["a_dst"] = (lambda j, hh, _a=a_out_t:
                        _a[hh * 64:hh * 64 + 64, j * 512:(j + 1) * 512])
        P = Prog(nc)
        consts = make_consts(P, [("negC", -EXPC)])
    else:
        P, consts = _P, _consts
    q_in = _io["qt"]
    k_in = _io["kt"]
    m_in = _io["m0"]
    a_dst = _io["a_dst"]

    q_sb = P.sbuf("q_sb", [128, 2 * S], BF16)
    k_sb = P.sbuf("k_sb", [128, 2 * S], BF16)
    v_sb = P.sbuf("v_sb", [128, HPC0 * NKT * 65], BF16)
    m_sb = P.sbuf("m_sb", [128, 4 * 512], BF16)
    p_bf = P.sbuf("p_bf", [128, 4 * 512], BF16)
    b_sb = P.sbuf("b_sb", [128, 512], F32)
    a_ev = P.sbuf("a_ev", [128, 2 * 512], BF16)
    psS = [P.psum(f"psS{i}", [128, 512]) for i in range(3)]
    psO = [P.psum(f"psO{i}", [128, 512]) for i in range(4)]
    psB = P.psum("psB", [128, 512])
    ld = P.sem("ld")
    sto = P.sem("sto")

    for ch in range(2):
        P.dma('sync', q_sb[:, ch * S:(ch + 1) * S], q_in[ch * 128:(ch + 1) * 128, :], ld)
        P.dma('sync', k_sb[:, ch * S:(ch + 1) * S], k_in[ch * 128:(ch + 1) * 128, :], ld)
    _io["load_v"](P, consts, v_sb, ld)
    P.dma('gpsimd', m_sb[:, :].rearrange("p (i n) -> p i n", i=4),
          m_in.rearrange("i p n -> p i n"), ld)
    ldv = ld.n
    P.wait('tensor', ld, ldv)
    P.wait('scalar', ld, ldv)
    P.wait('vector', ld, ldv)
    P.wait('gpsimd', ld, ldv)
    P.wait('scalar', P.c['vector'], consts['_v'])
    P.wait('vector', P.c['vector'], consts['_v'])

    exp_cnt = {}
    pv_cnt = {}
    evac_v = {}
    bco_v = {}
    ev_dma_a = {}
    ti = 0
    g2 = 0
    gev = 0
    for j in range(S // 512):
        nkt = 4 * (j + 1)
        for hp in range(HPC0 // 2):
            accs = (psO[2 * (g2 % 2)], psO[2 * (g2 % 2) + 1])
            for kt in range(nkt):
                for hd in range(2):
                    hh = 2 * hp + hd
                    hch = hh // 2
                    hpp = (hh % 2) * 64
                    sb = psS[ti % 3]
                    if ti >= 3:
                        P.wait('tensor', P.c['scalar'], exp_cnt[ti - 3])
                    P.call('tensor', 'matmul', out=sb[:, :],
                           lhsT=k_sb[hpp:hpp + 64,
                                     hch * S + kt * 128: hch * S + kt * 128 + 128],
                           rhs=q_sb[hpp:hpp + 64,
                                    hch * S + j * 512: hch * S + (j + 1) * 512],
                           start=True, stop=True)
                    tsm = P.c['tensor'].n
                    P.wait('scalar', P.c['tensor'], tsm)
                    if ti >= 4:
                        P.wait('scalar', P.c['tensor'], pv_cnt[ti - 4])
                    pb = p_bf[:, (ti % 4) * 512:(ti % 4 + 1) * 512]
                    ev = P.call('scalar', 'activation', out=pb, in_=sb[:, :],
                                func=AF.Exp, bias=consts['negC'][:, :], scale=SC0)
                    exp_cnt[ti] = ev
                    diag = kt - 4 * j
                    mv = None
                    if diag >= 0:
                        P.wait('gpsimd', P.c['scalar'], ev)
                        mv = P.call('gpsimd', 'tensor_mul', out=pb, in0=pb,
                                    in1=m_sb[:, diag * 512:(diag + 1) * 512])
                    if mv is not None:
                        P.wait('tensor', P.c['gpsimd'], mv)
                    else:
                        P.wait('tensor', P.c['scalar'], ev)
                    if kt == 0 and g2 >= 2:
                        P.wait('tensor', P.c['vector'], evac_v[g2 - 2])
                    P.call('tensor', 'matmul', out=accs[hd][0:65, :],
                           lhsT=v_sb[:, hh * NKT * 65 + kt * 65:
                                     hh * NKT * 65 + kt * 65 + 65],
                           rhs=pb, start=(kt == 0), stop=(kt == nkt - 1))
                    pv_cnt[ti] = P.c['tensor'].n
                    ti += 1
            pvlast = pv_cnt[ti - 1]
            for hd in range(2):
                hh = 2 * hp + hd
                pso = accs[hd]
                P.wait('vector', P.c['tensor'], pvlast)
                P.call('vector', 'reciprocal', out=b_sb[0:1, 0:512], in_=pso[64:65, :])
                rv = P.c['vector'].n
                P.wait('tensor', P.c['vector'], rv)
                if gev >= 1:
                    P.wait('tensor', P.c['vector'], bco_v[gev - 1])
                P.call('tensor', 'matmul', out=psB[:, :], lhsT=consts['ones1_f32'][:, :],
                       rhs=b_sb[0:1, 0:512], start=True, stop=True)
                tbv = P.c['tensor'].n
                P.wait('vector', P.c['tensor'], tbv)
                bco_v[gev] = P.call('vector', 'tensor_copy', out=b_sb[:, :], in_=psB[:, :])
                aev = a_ev[:, (gev % 2) * 512:(gev % 2 + 1) * 512]
                if gev >= 2:
                    P.wait('vector', sto, ev_dma_a[gev - 2])
                vv = P.call('vector', 'tensor_mul', out=aev[0:64, :], in0=pso[0:64, :],
                            in1=b_sb[0:64, :])
                if hd == 1:
                    evac_v[g2] = vv
                P.wait('sync', P.c['vector'], vv)
                ev_dma_a[gev] = P.dma('sync', a_dst(j, hh), aev[0:64, :], sto)
                gev += 1
            g2 += 1
    P.wait('sync', sto, sto.n)
    if _P is None:
        return P.finish()


def build_attn1(_P=None, _consts=None, _io=None):
    if _P is None:
        nc = new_nc()
        _io = {
            "qt": nc.dram_tensor("qt", [256, S], BF16, kind="ExternalInput").ap(),
            "kt": nc.dram_tensor("kt", [128, S], BF16, kind="ExternalInput").ap(),
            "m1": nc.dram_tensor("m1", [12, 128, 512], BF16, kind="ExternalInput").ap(),
        }
        va_in = nc.dram_tensor("va", [128, NKT, 65], BF16, kind="ExternalInput").ap()
        vb_in = nc.dram_tensor("vb", [128, NKT, 64], BF16, kind="ExternalInput").ap()

        def _load_v1(P, consts, va_sb, vb_sb, ldsem):
            P.dma('gpsimd', va_sb[:, :].rearrange("p (t c) -> p t c", c=65),
                  va_in[:, :, :], ldsem)
            P.dma('gpsimd', vb_sb[:, :].rearrange("p (t c) -> p t c", c=64),
                  vb_in[:, :, :], ldsem)
            return ldsem.n

        _io["load_v1"] = _load_v1
        a_out_t = nc.dram_tensor("at", [256, S], BF16, kind="ExternalOutput").ap()
        _io["a_dst"] = (lambda j, qh, _a=a_out_t:
                        _a[qh * 128:qh * 128 + 128, j * 512:(j + 1) * 512])
        P = Prog(nc)
        consts = make_consts(P, [("negC", -EXPC)])
    else:
        P, consts = _P, _consts
    q_in = _io["qt"]
    k_in = _io["kt"]
    m_in = _io["m1"]
    a_dst = _io["a_dst"]

    q_sb = P.sbuf("q_sb", [128, 2 * S], BF16)
    k_sb = P.sbuf("k_sb", [128, S], BF16)
    va_sb = P.sbuf("va_sb", [128, NKT * 65], BF16)
    vb_sb = P.sbuf("vb_sb", [128, NKT * 64], BF16)
    m_sb = P.sbuf("m_sb", [128, 12 * 512], BF16)
    p_bf = P.sbuf("p_bf", [128, 4 * 512], BF16)
    b_sb = P.sbuf("b_sb", [128, 512], F32)
    a_ev = P.sbuf("a_ev", [128, 2 * 512], BF16)
    psS = [P.psum(f"psS{i}", [128, 512]) for i in range(3)]
    psOA = [P.psum("psOA0", [128, 512]), P.psum("psOA1", [128, 512])]
    psOB = [P.psum("psOB0", [128, 512]), P.psum("psOB1", [128, 512])]
    psB = P.psum("psB", [128, 512])
    ld = P.sem("ld")
    sto = P.sem("sto")

    for ch in range(2):
        P.dma('sync', q_sb[:, ch * S:(ch + 1) * S], q_in[ch * 128:(ch + 1) * 128, :], ld)
    P.dma('sync', k_sb[:, :], k_in[:, :], ld)
    _io["load_v1"](P, consts, va_sb, vb_sb, ld)
    P.dma('gpsimd', m_sb[:, :].rearrange("p (i n) -> p i n", i=12),
          m_in.rearrange("i p n -> p i n"), ld)
    ldv = ld.n
    for e in ('tensor', 'scalar', 'vector', 'gpsimd'):
        P.wait(e, ld, ldv)
    P.wait('scalar', P.c['vector'], consts['_v'])
    P.wait('vector', P.c['vector'], consts['_v'])

    MASKED = (0, 1, 2, 3, 8, 9, 10, 11)
    exp_cnt, pv_cnt, evac_v, ev_dma_a = {}, {}, {}, {}
    ti = 0
    for j in range(S // 512):
        tiles = [t for t in range(12) if 512 * j + 128 * t - 1024 >= 0]
        accs = ((psOA[0], psOB[0]), (psOA[1], psOB[1]))
        for it, t in enumerate(tiles):
            kt = (512 * j + 128 * t - 1024) // 128
            for qh in range(HPC1):
                sb = psS[ti % 3]
                if ti >= 3:
                    P.wait('tensor', P.c['scalar'], exp_cnt[ti - 3])
                P.call('tensor', 'matmul', out=sb[:, :],
                       lhsT=k_sb[:, kt * 128:kt * 128 + 128],
                       rhs=q_sb[:, qh * S + j * 512: qh * S + (j + 1) * 512],
                       start=True, stop=True)
                tsm = P.c['tensor'].n
                P.wait('scalar', P.c['tensor'], tsm)
                if ti >= 4:
                    P.wait('scalar', P.c['tensor'], pv_cnt[ti - 4])
                pb = p_bf[:, (ti % 4) * 512:(ti % 4 + 1) * 512]
                ev = P.call('scalar', 'activation', out=pb, in_=sb[:, :], func=AF.Exp,
                            bias=consts['negC'][:, :], scale=SC1)
                exp_cnt[ti] = ev
                mv = None
                if t in MASKED:
                    P.wait('gpsimd', P.c['scalar'], ev)
                    mv = P.call('gpsimd', 'tensor_mul', out=pb, in0=pb,
                                in1=m_sb[:, t * 512:(t + 1) * 512])
                if mv is not None:
                    P.wait('tensor', P.c['gpsimd'], mv)
                else:
                    P.wait('tensor', P.c['scalar'], ev)
                if it == 0 and j >= 1:
                    P.wait('tensor', P.c['vector'], evac_v[j - 1])
                P.call('tensor', 'matmul', out=accs[qh][0][0:65, :],
                       lhsT=va_sb[:, kt * 65:kt * 65 + 65], rhs=pb,
                       start=(it == 0), stop=(it == len(tiles) - 1))
                P.call('tensor', 'matmul', out=accs[qh][1][0:64, :],
                       lhsT=vb_sb[:, kt * 64:kt * 64 + 64], rhs=pb,
                       start=(it == 0), stop=(it == len(tiles) - 1))
                pv_cnt[ti] = P.c['tensor'].n
                ti += 1
        pvlast = pv_cnt[ti - 1]
        for qh in range(HPC1):
            poA, poB = accs[qh]
            P.wait('vector', P.c['tensor'], pvlast)
            P.call('vector', 'reciprocal', out=b_sb[0:1, 0:512], in_=poA[64:65, :])
            rv = P.c['vector'].n
            P.wait('tensor', P.c['vector'], rv)
            P.call('tensor', 'matmul', out=psB[:, :], lhsT=consts['ones1_f32'][:, :],
                   rhs=b_sb[0:1, 0:512], start=True, stop=True)
            tbv = P.c['tensor'].n
            P.wait('vector', P.c['tensor'], tbv)
            P.call('vector', 'tensor_copy', out=b_sb[:, :], in_=psB[:, :])
            aev = a_ev[:, (qh % 2) * 512:(qh % 2 + 1) * 512]
            if j >= 1:
                P.wait('vector', sto, ev_dma_a[(j - 1, qh)])
            P.call('vector', 'tensor_mul', out=aev[0:64, :], in0=poA[0:64, :],
                   in1=b_sb[0:64, :])
            vv = P.call('vector', 'tensor_mul', out=aev[64:128, :], in0=poB[0:64, :],
                        in1=b_sb[0:64, :])
            if qh == 1:
                evac_v[j] = vv
            P.wait('sync', P.c['vector'], vv)
            ev_dma_a[(j, qh)] = P.dma('sync', a_dst(j, qh), aev[:, :], sto)
    P.wait('sync', sto, sto.n)
    if _P is None:
        return P.finish()


def build_proj(has_bias, _P=None, _consts=None, _io=None):
    if _P is None:
        nc = new_nc()
        _io = {
            "a_cols": nc.dram_tensor("a_cols", [D, TOK], BF16, kind="ExternalInput").ap(),
            "w": nc.dram_tensor("w", [D, D], F32, kind="ExternalInput").ap(),
            "x_cols": nc.dram_tensor("x_cols", [D, TOK], F32, kind="ExternalInput").ap(),
            "x1": nc.dram_tensor("x1", [D, TOK], F32, kind="ExternalOutput").ap(),
            "xn": nc.dram_tensor("xn", [D, TOK], BF16, kind="ExternalOutput").ap(),
        }
        if has_bias:
            _io["b"] = nc.dram_tensor("b", [128, NCH], F32, kind="ExternalInput").ap()
        P = Prog(nc)
        consts = make_consts(P, [("eps_ln", LN_EPS)])
    else:
        P, consts = _P, _consts
    a_in = _io["a_cols"]
    w_in = _io["w"]
    b_in = _io.get("b")
    x_in = _io["x_cols"]
    x1_out = _io["x1"]
    xn_out = _io["xn"]

    a_sb = P.sbuf("a_sb", [128, NCH * TOK], BF16)
    x_sb = P.sbuf("x_sb", [128, NCH * TOK], F32)
    y_sb = P.sbuf("y_sb", [128, NCH * TOK], F32)
    w_sb = P.sbuf("w_sb", [128, 2 * NCH * 128], BF16)
    b_sb = P.sbuf("b_sb", [128, NCH], F32)
    yb_sb = P.sbuf("yb_sb", [128, 2 * TOK], BF16)
    sq_sb = P.sbuf("sq_sb", [128, TOK], BF16)
    acc1 = P.sbuf("acc1", [1, TOK], F32)
    acc2 = P.sbuf("acc2", [1, TOK], F32)
    rA = P.sbuf("rA", [1, TOK], F32)
    rR = P.sbuf("rR", [1, TOK], F32)
    Rb = P.sbuf("Rb", [128, TOK], F32)
    Ab = P.sbuf("Ab", [128, TOK], F32)
    ps = [P.psum(f"ps{i}", [128, 512]) for i in range(4)]
    ps_s1 = P.psum("ps_s1", [1, TOK])
    ps_s2 = P.psum("ps_s2", [1, TOK])
    ldw = P.sem("ldw")
    ld = P.sem("ld")
    sto = P.sem("sto")

    P.calln('vector', 'memset', ap=acc1[:, :], constant=0.0)
    P.calln('vector', 'memset', ap=acc2[:, :], constant=0.0)
    P.wait('scalar', P.c['vector'], consts['_v'])

    for kc0 in range(0, NCH, 4):
        src = a_in[kc0 * 128:(kc0 + 4) * 128, :].rearrange("(kc p) n -> p kc n", p=128)
        dst = a_sb[:, kc0 * TOK:(kc0 + 4) * TOK].rearrange("p (kc n) -> p kc n", kc=4)
        av = P.dma('sync', dst, src, ld)
    for kc0 in range(0, NCH, 4):
        src = x_in[kc0 * 128:(kc0 + 4) * 128, :].rearrange("(kc p) n -> p kc n", p=128)
        dst = x_sb[:, kc0 * TOK:(kc0 + 4) * TOK].rearrange("p (kc n) -> p kc n", kc=4)
        xv = P.dma('sync', dst, src, ld)
    if has_bias:
        P.dma('gpsimd', b_sb[:, :], b_in[:, :], ldw)
    wv = {}
    wv[0] = load_w_cols(P, 'gpsimd', ldw, w_in, 0, 128, w_sb, 0, D)

    ev_cnt = {}
    wmm = {}
    xnb_dma = {}
    for oc in range(NCH):
        if oc + 1 < NCH:
            if oc >= 1:
                P.wait('gpsimd', P.c['tensor'], wmm[oc - 1])
            wv[oc + 1] = load_w_cols(P, 'gpsimd', ldw, w_in, (oc + 1) * 128, 128,
                                     w_sb, ((oc + 1) % 2) * NCH * 128, D)
        P.wait('tensor', ldw, wv[oc])
        if oc == 0:
            P.wait('tensor', ld, av)
        woff = (oc % 2) * NCH * 128
        for half in range(2):
            bank = ps[(2 * oc + half) % 4]
            if 2 * oc + half >= 4:
                P.wait('tensor', P.c['scalar'], ev_cnt[(2 * oc + half) % 4])
            for kc in range(NCH):
                P.call('tensor', 'matmul', out=bank[:, :],
                       lhsT=w_sb[:, woff + kc * 128:woff + (kc + 1) * 128],
                       rhs=a_sb[:, kc * TOK + half * 512: kc * TOK + half * 512 + 512],
                       start=(kc == 0), stop=(kc == NCH - 1))
            tmm = P.c['tensor'].n
            wmm[oc] = tmm
            P.wait('scalar', P.c['tensor'], tmm)
            ydst = y_sb[:, oc * TOK + half * 512: oc * TOK + half * 512 + 512]
            if has_bias:
                sv = P.call('scalar', 'activation', out=ydst, in_=bank[:, :],
                            func=AF.Identity, bias=b_sb[:, oc:oc + 1], scale=1.0)
            else:
                sv = P.call('scalar', 'copy', out=ydst, in_=bank[:, :])
            ev_cnt[(2 * oc + half) % 4] = sv
        P.wait('vector', P.c['scalar'], P.c['scalar'].n)
        stats_chunk(P, consts, y_sb[:, oc * TOK:(oc + 1) * TOK],
                    yb_sb[:, 0:TOK], sq_sb, ps_s1, ps_s2, acc1, acc2, TOK)

    vv = ln_finalize(P, consts, D, acc1, acc2, rA, rR, "eps_ln", TOK)
    P.wait('tensor', P.c['vector'], vv)
    ones1 = consts['ones1_f32']
    for row, banks, dst in ((rR, (ps[0], ps[1]), Rb), (rA, (ps[2], ps[3]), Ab)):
        for h in range(2):
            P.call('tensor', 'matmul', out=banks[h][:, :], lhsT=ones1[:, :],
                   rhs=row[:, h * 512:(h + 1) * 512], start=True, stop=True)
    tw = P.c['tensor'].n
    P.wait('vector', P.c['tensor'], tw)
    for banks, dst in (((ps[0], ps[1]), Rb), ((ps[2], ps[3]), Ab)):
        for h in range(2):
            P.call('vector', 'tensor_copy', out=dst[:, h * 512:(h + 1) * 512],
                   in_=banks[h][:, :])
    P.calln('vector', 'memset', ap=acc1[:, :], constant=0.0)
    P.calln('vector', 'memset', ap=acc2[:, :], constant=0.0)
    P.wait('vector', ld, xv)
    for oc in range(NCH):
        ys = y_sb[:, oc * TOK:(oc + 1) * TOK]
        xs = x_sb[:, oc * TOK:(oc + 1) * TOK]
        P.call('vector', 'tensor_mul', out=ys, in0=ys, in1=Rb[:, :])
        P.call('vector', 'tensor_add', out=ys, in0=ys, in1=Ab[:, :])
        vv = P.call('vector', 'tensor_add', out=xs, in0=xs, in1=ys)
        P.wait('sync', P.c['vector'], vv)
        P.dma('sync', x1_out[oc * 128:(oc + 1) * 128, :], xs, sto)
        stats_chunk(P, consts, xs, yb_sb[:, 0:TOK], sq_sb, ps_s1, ps_s2, acc1, acc2, TOK)

    vv = ln_finalize(P, consts, D, acc1, acc2, rA, rR, "eps_ln", TOK)
    P.wait('tensor', P.c['vector'], vv)
    for row, banks in ((rR, (ps[0], ps[1])), (rA, (ps[2], ps[3]))):
        for h in range(2):
            P.call('tensor', 'matmul', out=banks[h][:, :], lhsT=ones1[:, :],
                   rhs=row[:, h * 512:(h + 1) * 512], start=True, stop=True)
    tw = P.c['tensor'].n
    P.wait('vector', P.c['tensor'], tw)
    for banks, dst in (((ps[0], ps[1]), Rb), ((ps[2], ps[3]), Ab)):
        for h in range(2):
            P.call('vector', 'tensor_copy', out=dst[:, h * 512:(h + 1) * 512],
                   in_=banks[h][:, :])
    for oc in range(NCH):
        xs = x_sb[:, oc * TOK:(oc + 1) * TOK]
        xnb = yb_sb[:, (oc % 2) * TOK:(oc % 2) * TOK + TOK]
        if oc >= 2:
            P.wait('vector', sto, xnb_dma[oc - 2])
        P.call('vector', 'tensor_mul', out=xs, in0=xs, in1=Rb[:, :])
        vv = P.call('vector', 'tensor_add', out=xnb, in0=xs, in1=Ab[:, :])
        P.wait('sync', P.c['vector'], vv)
        xnb_dma[oc] = P.dma('sync', xn_out[oc * 128:(oc + 1) * 128, :], xnb, sto)
    P.wait('sync', sto, sto.n)
    if _P is None:
        return P.finish()


def build_quant(_P=None, _consts=None, _io=None):
    if _P is None:
        nc = new_nc()
        _io = {
            "wgu_sh": nc.dram_tensor("wgu_sh", [D // 8, 2 * H0], F32, kind="ExternalInput").ap(),
            "wp_sh": nc.dram_tensor("wp_sh", [H0 // 8, D], F32, kind="ExternalInput").ap(),
            "up_sh": nc.dram_tensor("up_sh", [D // 8, H1], F32, kind="ExternalInput").ap(),
            "dn_sh": nc.dram_tensor("dn_sh", [H1 // 8, D], F32, kind="ExternalInput").ap(),
            "wgu_q": nc.dram_tensor("wgu_q", [D // 8, 2 * H0], BF16, kind="ExternalOutput").ap(),
            "wp_q": nc.dram_tensor("wp_q", [H0 // 8, D], BF16, kind="ExternalOutput").ap(),
            "up_q": nc.dram_tensor("up_q", [D // 8, H1], BF16, kind="ExternalOutput").ap(),
            "dn_q": nc.dram_tensor("dn_q", [H1 // 8, D], BF16, kind="ExternalOutput").ap(),
            "Ebf": nc.dram_tensor("Ebf", [128, 2], BF16, kind="ExternalInput").ap(),
            "Ebc": nc.dram_tensor("Ebc", [2, 128], F32, kind="ExternalInput").ap(),
        }
        P = Prog(nc)
        consts = make_consts(P, [("mhalf", -0.5)])
    else:
        P, consts = _P, _consts
    ins = {k: _io[k] for k in ("wgu_sh", "wp_sh", "up_sh", "dn_sh")}
    outs = {k: _io[k] for k in ("wgu_q", "wp_q", "up_q", "dn_q")}
    Ebf_in = _io["Ebf"]
    Ebc_in = _io["Ebc"]
    w_sb = P.sbuf("w_sb", [128, 2 * 512], F32)
    wb_sb = P.sbuf("wb_sb", [128, 512], BF16)
    a_sb = P.sbuf("a_sb", [128, 512], BF16)
    r_sb = P.sbuf("r_sb", [128, 512], BF16)
    t1_sb = P.sbuf("t1_sb", [128, 512], F32)
    t2_sb = P.sbuf("t2_sb", [128, 512], F32)
    wh_sb = P.sbuf("wh_sb", [128, 2 * 512], BF16)
    srow = P.sbuf("srow", [2, 512], F32)
    sbro = P.sbuf("sbro", [2, 512], BF16)
    s32 = P.sbuf("s32", [2, 512], F32)
    invs = P.sbuf("invs", [2, 512], F32)
    halfs = P.sbuf("halfs", [2, 512], F32)
    ebf_sb = P.sbuf("ebf_sb", [128, 2], BF16)
    ebc_sb = P.sbuf("ebc_sb", [2, 128], F32)
    ps_g = P.psum("ps_g", [2, 512])
    ps_r = P.psum("ps_r", [128, 512])
    ps_h = P.psum("ps_h", [128, 512])
    ld = P.sem("ld")
    sto = P.sem("sto")

    P.dma('sync', ebf_sb[:, :], Ebf_in[:, :], ld)
    P.dma('sync', ebc_sb[:, :], Ebc_in[:, :], ld)
    P.wait('tensor', ld, ld.n)
    P.wait('scalar', P.c['vector'], consts['_v'])

    def tern_block(w_in, wh_out, r0, c0, ib):
        buf = ib % 2
        ws = w_sb[:, buf * 512:(buf + 1) * 512]
        whs = wh_sb[:, buf * 512:(buf + 1) * 512]
        if ib >= 2:
            P.wait('sync', P.c['vector'], blk_v[ib - 2])
            P.wait('sync', sto, blk_st[ib - 2])
        lv = P.dma('sync', ws, w_in[r0:r0 + 128, c0:c0 + 512], ld)
        P.wait('vector', ld, lv)
        if ib >= 1:
            P.wait('vector', P.c['scalar'], blk_s[ib - 1])
        vv = P.call('vector', 'tensor_copy', out=wb_sb[:, :], in_=ws)
        P.wait('scalar', P.c['vector'], vv)
        sv = P.call('scalar', 'activation', out=a_sb[:, :], in_=wb_sb[:, :],
                    func=AF.Abs, bias=consts['zero_b'][:, :], scale=1.0)
        P.wait('tensor', P.c['scalar'], sv)
        if ib >= 1:
            P.wait('tensor', P.c['vector'], blk_gv[ib - 1])
        P.call('tensor', 'matmul', out=ps_g[:, :], lhsT=ebf_sb[:, :], rhs=a_sb[:, :],
               start=True, stop=True)
        tg = P.c['tensor'].n
        P.wait('vector', P.c['tensor'], tg)
        P.call('vector', 'tensor_scalar_mul', out=srow[:, :], in0=ps_g[:, :],
               scalar1=1.0 / 64.0)
        blk_gv[ib] = P.c['vector'].n
        P.call('vector', 'tensor_scalar_max', out=srow[:, :], in0=srow[:, :], scalar1=1e-8)
        P.call('vector', 'tensor_copy', out=sbro[:, :], in_=srow[:, :])
        P.call('vector', 'tensor_copy', out=s32[:, :], in_=sbro[:, :])
        vh = P.call('vector', 'reciprocal', out=invs[:, :], in_=s32[:, :])
        P.wait('tensor', P.c['vector'], vh)
        if ib >= 1:
            P.wait('tensor', P.c['vector'], blk_bv[ib - 1])
        P.call('tensor', 'matmul', out=ps_r[:, :], lhsT=ebc_sb[:, :], rhs=invs[:, :],
               start=True, stop=True)
        P.call('tensor', 'matmul', out=ps_h[:, :], lhsT=ebc_sb[:, :], rhs=s32[:, :],
               start=True, stop=True)
        tb = P.c['tensor'].n
        P.wait('vector', P.c['tensor'], tb)
        vr = P.call('vector', 'tensor_mul', out=r_sb[:, :], in0=wb_sb[:, :], in1=ps_r[:, :])
        blk_v[ib] = vr
        P.wait('scalar', P.c['vector'], vr)
        P.call('scalar', 'activation', out=t1_sb[:, :], in_=r_sb[:, :], func=AF.Sign,
               bias=consts['mhalf'][:, :], scale=1.0)
        ss = P.call('scalar', 'activation', out=t2_sb[:, :], in_=r_sb[:, :], func=AF.Sign,
                    bias=consts['mhalf'][:, :], scale=-1.0)
        blk_s[ib] = ss
        P.wait('vector', P.c['scalar'], ss)
        P.call('vector', 'tensor_scalar_max', out=t1_sb[:, :], in0=t1_sb[:, :], scalar1=0.0)
        P.call('vector', 'tensor_scalar_max', out=t2_sb[:, :], in0=t2_sb[:, :], scalar1=0.0)
        P.call('vector', 'tensor_sub', out=t1_sb[:, :], in0=t1_sb[:, :], in1=t2_sb[:, :])
        vv = P.call('vector', 'tensor_mul', out=whs, in0=t1_sb[:, :], in1=ps_h[:, :])
        blk_bv[ib] = vv
        P.wait('sync', P.c['vector'], vv)
        blk_st[ib] = P.dma('sync', wh_out[r0:r0 + 128, c0:c0 + 512], whs, sto)

    blk_v, blk_st, blk_s, blk_gv, blk_bv = {}, {}, {}, {}, {}
    ib = 0
    for nm_in, nm_out, rows, cols in (("wgu_sh", "wgu_q", D // 8, 2 * H0),
                                      ("wp_sh", "wp_q", H0 // 8, D)):
        for r0 in range(0, rows, 128):
            for c0 in range(0, cols, 512):
                tern_block(ins[nm_in], outs[nm_out], r0, c0, ib)
                ib += 1

    sg_v, sg_st = {}, {}
    jb = 0
    for nm_in, nm_out, rows, cols in (("up_sh", "up_q", D // 8, H1),
                                      ("dn_sh", "dn_q", H1 // 8, D)):
        for r0 in range(0, rows, 128):
            for c0 in range(0, cols, 512):
                buf = jb % 2
                ws = w_sb[:, buf * 512:(buf + 1) * 512]
                whs = wh_sb[:, buf * 512:(buf + 1) * 512]
                if jb == 0:
                    P.wait('sync', P.c['vector'], P.c['vector'].n)
                    P.wait('sync', sto, sto.n)
                if jb >= 2:
                    P.wait('sync', P.c['scalar'], sg_v[jb - 2])
                    P.wait('sync', sto, sg_st[jb - 2])
                lv = P.dma('sync', ws, ins[nm_in][r0:r0 + 128, c0:c0 + 512], ld)
                P.wait('scalar', ld, lv)
                sv = P.call('scalar', 'activation', out=whs, in_=ws, func=AF.Sign,
                            bias=consts['zero_b'][:, :], scale=1.0)
                sg_v[jb] = sv
                P.wait('sync', P.c['scalar'], sv)
                sg_st[jb] = P.dma('sync', outs[nm_out][r0:r0 + 128, c0:c0 + 512], whs, sto)
                jb += 1
    P.wait('sync', sto, sto.n)
    if _P is None:
        return P.finish()


def build_ffn0(_P=None, _consts=None, _io=None):
    if _P is None:
        nc = new_nc()
        _io = {
            "xn2": nc.dram_tensor("xn2", [D, TOK], BF16, kind="ExternalInput").ap(),
            "wgu": nc.dram_tensor("wgu", [D, 2 * H0], BF16, kind="ExternalInput").ap(),
            "wp": nc.dram_tensor("wp", [H0, D], BF16, kind="ExternalInput").ap(),
            "x1": nc.dram_tensor("x1", [D, TOK], F32, kind="ExternalInput").ap(),
            "x2": nc.dram_tensor("x2", [D, TOK], F32, kind="ExternalOutput").ap(),
            "xn3": nc.dram_tensor("xn3", [D, TOK], BF16, kind="ExternalOutput").ap(),
        }
        P = Prog(nc)
        consts = make_consts(P, [("eps_ln", LN_EPS), ("eps_rms", RMS_EPS)])
    else:
        P, consts = _P, _consts
    xn_in = _io["xn2"]
    wgu_in = _io["wgu"]
    wp_in = _io["wp"]
    x1_in = _io["x1"]
    x2_out = _io["x2"]
    xn3_out = _io["xn3"]
    NJ = H0 // 128

    xn_sb = P.sbuf("xn_sb", [128, NCH * 512], BF16)
    h_sb = P.sbuf("h_sb", [128, NJ * 512], BF16)
    wg_sb = P.sbuf("wg_sb", [128, 2 * 2 * NCH * 128], BF16)
    wp_sb = P.sbuf("wp_sb", [128, 2 * 32 * 128], BF16)
    x1_sb = P.sbuf("x1_sb", [128, NCH * 512], F32)
    y_sb = P.sbuf("y_sb", [128, NCH * 512], F32)
    sg_sb = P.sbuf("sg_sb", [128, 2 * 512], F32)
    sq_sb = P.sbuf("sq_sb", [128, 512], BF16)
    yb_sb = P.sbuf("yb_sb", [128, 2 * 512], BF16)
    racc = P.sbuf("racc", [1, 512], F32)
    acc1 = P.sbuf("acc1", [1, 512], F32)
    acc2 = P.sbuf("acc2", [1, 512], F32)
    rA = P.sbuf("rA", [1, 512], F32)
    rR = P.sbuf("rR", [1, 512], F32)
    Rb = P.sbuf("Rb", [128, 512], F32)
    Ab = P.sbuf("Ab", [128, 512], F32)
    bR = P.sbuf("bR", [128, 512], F32)
    psGs = [P.psum("psG", [128, 512]), P.psum("psGb", [128, 512])]
    psUs = [P.psum("psU", [128, 512]), P.psum("psUb", [128, 512])]
    psG = psGs[0]
    psU = psUs[0]
    psY = [P.psum("psY0", [128, 512]), P.psum("psY1", [128, 512])]
    ps_s1 = P.psum("ps_s1", [1, 512])
    ps_s2 = P.psum("ps_s2", [1, 512])
    ps_r = ps_s1
    ps_b = psGs[1]
    ldw = P.sem("ldw")
    ld = P.sem("ld")
    sto = P.sem("sto")

    P.wait('scalar', P.c['vector'], consts['_v'])

    for half in range(2):
        t0 = half * 512
        if half == 1:
            P.wait('sync', sto, prev_sto)
        lv = None
        for kc0 in range(0, NCH, 4):
            src = xn_in[kc0 * 128:(kc0 + 4) * 128, t0:t0 + 512].rearrange(
                "(kc p) n -> p kc n", p=128)
            dst = xn_sb[:, kc0 * 512:(kc0 + 4) * 512].rearrange("p (kc n) -> p kc n", kc=4)
            lv = P.dma('sync', dst, src, ld)
        xv = None
        for kc0 in range(0, NCH, 4):
            src = x1_in[kc0 * 128:(kc0 + 4) * 128, t0:t0 + 512].rearrange(
                "(kc p) n -> p kc n", p=128)
            dst = x1_sb[:, kc0 * 512:(kc0 + 4) * 512].rearrange("p (kc n) -> p kc n", kc=4)
            xv = P.dma('sync', dst, src, ld)
        if half == 1:
            P.wait('tensor', ld, lv)
            P.wait('tensor', P.c['vector'], P.c['vector'].n)
            P.wait('tensor', P.c['scalar'], P.c['scalar'].n)
        P.calln('vector', 'memset', ap=racc[:, :], constant=0.0)
        P.calln('vector', 'memset', ap=acc1[:, :], constant=0.0)
        P.calln('vector', 'memset', ap=acc2[:, :], constant=0.0)

        wv = {}
        wv[0] = load_w_cols(P, 'gpsimd', ldw, wgu_in, 0, 128, wg_sb, 0, D)
        wv[0] = load_w_cols(P, 'gpsimd', ldw, wgu_in, H0, 128, wg_sb, NCH * 128, D)
        gmm = {}
        hv = {}
        sqv = {}
        silu_j = {}
        for j in range(NJ):
            woff = (j % 2) * 2 * NCH * 128
            if j + 1 < NJ:
                woff2 = ((j + 1) % 2) * 2 * NCH * 128
                if j >= 1:
                    P.wait('gpsimd', P.c['tensor'], gmm[j - 1])
                wv[j + 1] = load_w_cols(P, 'gpsimd', ldw, wgu_in, (j + 1) * 128,
                                        128, wg_sb, woff2, D)
                wv[j + 1] = load_w_cols(P, 'gpsimd', ldw, wgu_in, H0 + (j + 1) * 128,
                                        128, wg_sb, woff2 + NCH * 128, D)
            P.wait('tensor', ldw, wv[j])
            if half == 0 and j == 0:
                P.wait('tensor', ld, lv)
            psGj = psGs[j % 2]
            psUj = psUs[j % 2]
            if j >= 2:
                P.wait('tensor', P.c['scalar'], silu_j[j - 2])
                P.wait('tensor', P.c['vector'], hv[j - 2])
            for kc in range(NCH):
                P.call('tensor', 'matmul', out=psGj[:, :],
                       lhsT=wg_sb[:, woff + kc * 128:woff + (kc + 1) * 128],
                       rhs=xn_sb[:, kc * 512:(kc + 1) * 512],
                       start=(kc == 0), stop=(kc == NCH - 1))
            for kc in range(NCH):
                P.call('tensor', 'matmul', out=psUj[:, :],
                       lhsT=wg_sb[:, woff + NCH * 128 + kc * 128:
                                  woff + NCH * 128 + (kc + 1) * 128],
                       rhs=xn_sb[:, kc * 512:(kc + 1) * 512],
                       start=(kc == 0), stop=(kc == NCH - 1))
            gmm[j] = P.c['tensor'].n
            P.wait('scalar', P.c['tensor'], gmm[j])
            if j >= 2:
                P.wait('scalar', P.c['vector'], hv[j - 2])
            silu_s = P.call('scalar', 'activation',
                            out=sg_sb[:, (j % 2) * 512:(j % 2) * 512 + 512],
                            in_=psGj[:, :],
                            func=AF.Silu, bias=consts['zero_b'][:, :], scale=1.0)
            silu_j[j] = silu_s
            P.wait('vector', P.c['scalar'], silu_s)
            hs = h_sb[:, j * 512:(j + 1) * 512]
            hv[j] = P.call('vector', 'tensor_mul', out=hs,
                           in0=sg_sb[:, (j % 2) * 512:(j % 2) * 512 + 512], in1=psUj[:, :])
            P.wait('scalar', P.c['vector'], hv[j])
            if j >= 1:
                P.wait('scalar', P.c['tensor'], rmm_t)
            sqv[j] = P.call('scalar', 'square', out=sq_sb[:, :], in_=hs)
            P.wait('tensor', P.c['scalar'], sqv[j])
            if j >= 1:
                P.wait('tensor', P.c['vector'], racc_v)
            P.call('tensor', 'matmul', out=ps_r[:, :], lhsT=consts['ones_bf'][:, :],
                   rhs=sq_sb[:, :], start=True, stop=True)
            rmm_t = P.c['tensor'].n
            P.wait('vector', P.c['tensor'], rmm_t)
            racc_v = P.call('vector', 'tensor_add', out=racc[:, :], in0=racc[:, :],
                            in1=ps_r[:, :])

        P.call('vector', 'tensor_scalar_mul', out=rR[:, :], in0=racc[:, :],
               scalar1=1.0 / H0)
        vw = P.c['vector'].n
        P.wait('scalar', P.c['vector'], vw)
        sw = P.call('scalar', 'activation', out=rA[:, :], in_=rR[:, :], func=AF.Sqrt,
                    bias=consts['eps_rms'][0:1, 0:1], scale=1.0)
        P.wait('vector', P.c['scalar'], sw)
        rv = P.call('vector', 'reciprocal', out=rR[:, :], in_=rA[:, :])
        P.wait('tensor', P.c['vector'], rv)
        P.call('tensor', 'matmul', out=ps_b[:, :], lhsT=consts['ones1_f32'][:, :],
               rhs=rR[0:1, :], start=True, stop=True)
        tb = P.c['tensor'].n
        P.wait('vector', P.c['tensor'], tb)
        brv = P.call('vector', 'tensor_copy', out=bR[:, :], in_=ps_b[:, :])

        dmm = {}
        yv = {}
        pieces = {}
        P.wait('tensor', P.c['vector'], hv[NJ - 1])

        def load_piece(oc, pc):
            dst_off = ((2 * oc + pc) % 2) * 32 * 128
            v = None
            for kk in range(0, 32, 4):
                ss = wp_in[pc * 32 * 128 + kk * 128: pc * 32 * 128 + (kk + 4) * 128,
                           oc * 128:(oc + 1) * 128].rearrange("(kc p) n -> p kc n", p=128)
                dd = wp_sb[:, dst_off + kk * 128:dst_off + (kk + 4) * 128].rearrange(
                    "p (kc n) -> p kc n", kc=4)
                v = P.dma('gpsimd', dd, ss, ldw)
            return v

        pieces[(0, 0)] = load_piece(0, 0)
        pseq = [(oc, pc) for oc in range(NCH) for pc in range(2)]
        for idx, (oc, pc) in enumerate(pseq):
            if idx + 1 < len(pseq):
                noc, npc = pseq[idx + 1]
                if idx >= 1:
                    P.wait('gpsimd', P.c['tensor'], dmm[pseq[idx - 1]])
                pieces[(noc, npc)] = load_piece(noc, npc)
            P.wait('tensor', ldw, pieces[(oc, pc)])
            if pc == 0 and oc >= 2:
                P.wait('tensor', P.c['vector'], yv[oc - 2])
            woff = (idx % 2) * 32 * 128
            bank = psY[oc % 2]
            for kk in range(32):
                j = pc * 32 + kk
                P.call('tensor', 'matmul', out=bank[:, :],
                       lhsT=wp_sb[:, woff + kk * 128:woff + (kk + 1) * 128],
                       rhs=h_sb[:, j * 512:(j + 1) * 512],
                       start=(j == 0), stop=(j == NJ - 1))
            dmm[(oc, pc)] = P.c['tensor'].n
            if pc == 1:
                P.wait('vector', P.c['tensor'], dmm[(oc, pc)])
                if oc == 0:
                    P.wait('vector', P.c['vector'], brv)
                ys = y_sb[:, oc * 512:(oc + 1) * 512]
                yv[oc] = P.call('vector', 'tensor_mul', out=ys, in0=bank[:, :], in1=bR[:, :])
                stats_chunk(P, consts, ys, yb_sb[:, 0:512], sq_sb, ps_s1, ps_s2,
                            acc1, acc2, 512)

        vv = ln_finalize(P, consts, D, acc1, acc2, rA, rR, "eps_ln", 512)
        P.wait('tensor', P.c['vector'], vv)
        P.call('tensor', 'matmul', out=psG[:, :], lhsT=consts['ones1_f32'][:, :],
               rhs=rR[0:1, :], start=True, stop=True)
        P.call('tensor', 'matmul', out=psU[:, :], lhsT=consts['ones1_f32'][:, :],
               rhs=rA[0:1, :], start=True, stop=True)
        tb = P.c['tensor'].n
        P.wait('vector', P.c['tensor'], tb)
        P.call('vector', 'tensor_copy', out=Rb[:, :], in_=psG[:, :])
        P.call('vector', 'tensor_copy', out=Ab[:, :], in_=psU[:, :])
        P.calln('vector', 'memset', ap=acc1[:, :], constant=0.0)
        P.calln('vector', 'memset', ap=acc2[:, :], constant=0.0)
        P.wait('vector', ld, xv)
        x2v = {}
        for oc in range(NCH):
            ys = y_sb[:, oc * 512:(oc + 1) * 512]
            xs = x1_sb[:, oc * 512:(oc + 1) * 512]
            P.call('vector', 'tensor_mul', out=ys, in0=ys, in1=Rb[:, :])
            P.call('vector', 'tensor_add', out=ys, in0=ys, in1=Ab[:, :])
            x2v[oc] = P.call('vector', 'tensor_add', out=xs, in0=xs, in1=ys)
            P.wait('sync', P.c['vector'], x2v[oc])
            P.dma('sync', x2_out[oc * 128:(oc + 1) * 128, t0:t0 + 512], xs, sto)
            stats_chunk(P, consts, xs, yb_sb[:, 0:512], sq_sb, ps_s1, ps_s2,
                        acc1, acc2, 512)
        vv = ln_finalize(P, consts, D, acc1, acc2, rA, rR, "eps_ln", 512)
        P.wait('tensor', P.c['vector'], vv)
        P.call('tensor', 'matmul', out=psG[:, :], lhsT=consts['ones1_f32'][:, :],
               rhs=rR[0:1, :], start=True, stop=True)
        P.call('tensor', 'matmul', out=psU[:, :], lhsT=consts['ones1_f32'][:, :],
               rhs=rA[0:1, :], start=True, stop=True)
        tb = P.c['tensor'].n
        P.wait('vector', P.c['tensor'], tb)
        P.call('vector', 'tensor_copy', out=Rb[:, :], in_=psG[:, :])
        P.call('vector', 'tensor_copy', out=Ab[:, :], in_=psU[:, :])
        xnb_d = {}
        for oc in range(NCH):
            xs = x1_sb[:, oc * 512:(oc + 1) * 512]
            xnb = yb_sb[:, (oc % 2) * 512:(oc % 2) * 512 + 512]
            if oc >= 2:
                P.wait('vector', sto, xnb_d[oc - 2])
            P.call('vector', 'tensor_mul', out=xs, in0=xs, in1=Rb[:, :])
            vv = P.call('vector', 'tensor_add', out=xnb, in0=xs, in1=Ab[:, :])
            P.wait('sync', P.c['vector'], vv)
            xnb_d[oc] = P.dma('sync', xn3_out[oc * 128:(oc + 1) * 128, t0:t0 + 512],
                              xnb, sto)
        prev_sto = sto.n
    P.wait('sync', sto, sto.n)
    if _P is None:
        return P.finish()


def build_ffn1(dbg=False, _P=None, _consts=None, _io=None):
    if _P is None:
        nc = new_nc()
        _io = {
            "xn4": nc.dram_tensor("xn4", [D, TOK], BF16, kind="ExternalInput").ap(),
            "up": nc.dram_tensor("up", [D, H1], BF16, kind="ExternalInput").ap(),
            "dn": nc.dram_tensor("dn", [H1, D], BF16, kind="ExternalInput").ap(),
            "upsc": nc.dram_tensor("upsc", [128, H1 // 128], F32, kind="ExternalInput").ap(),
            "dsc": nc.dram_tensor("dsc", [128, NCH], F32, kind="ExternalInput").ap(),
            "osc": nc.dram_tensor("osc", [1, 1], F32, kind="ExternalInput").ap(),
            "x3": nc.dram_tensor("x3", [D, TOK], F32, kind="ExternalInput").ap(),
            "x4": nc.dram_tensor("x4", [D, TOK], F32, kind="ExternalOutput").ap(),
        }
        if dbg:
            _io["h_dbg"] = nc.dram_tensor("h_dbg", [H1, 512], BF16, kind="ExternalOutput").ap()
            _io["y_dbg"] = nc.dram_tensor("y_dbg", [D, 512], F32, kind="ExternalOutput").ap()
            _io["s_dbg"] = nc.dram_tensor("s_dbg", [128, 17], F32, kind="ExternalOutput").ap()
        P = Prog(nc)
        consts = make_consts(P, [("eps_ln", LN_EPS)])
    else:
        P, consts = _P, _consts
        dbg = False
    xn_in = _io["xn4"]
    up_in = _io["up"]
    dn_in = _io["dn"]
    upsc_in = _io["upsc"]
    dsc_in = _io["dsc"]
    osc_in = _io["osc"]
    x3_in = _io["x3"]
    x4_out = _io["x4"]
    if dbg:
        h_dbg = _io["h_dbg"]
        y_dbg = _io["y_dbg"]
        s_dbg = _io["s_dbg"]
    NJ = H1 // 128

    xn_sb = P.sbuf("xn_sb", [128, NCH * 512], BF16)
    h_sb = P.sbuf("h_sb", [128, NJ * 512], BF16)
    wu_sb = P.sbuf("wu_sb", [128, 2 * NCH * 128], BF16)
    wd_sb = P.sbuf("wd_sb", [128, 2 * 32 * 128], BF16)
    x3_sb = P.sbuf("x3_sb", [128, NCH * 512], F32)
    y_sb = P.sbuf("y_sb", [128, NCH * 512], F32)
    sq_sb = P.sbuf("sq_sb", [128, 512], BF16)
    yb_sb = P.sbuf("yb_sb", [128, 2 * 512], BF16)
    upsc_sb = P.sbuf("upsc_sb", [128, NJ], F32)
    dsc_sb = P.sbuf("dsc_sb", [128, NCH], F32)
    osc_sb = P.sbuf("osc_sb", [1, 1], F32)
    osb = P.sbuf("osb", [128, 1], F32)
    dscm = P.sbuf("dscm", [128, NCH], F32)
    acc1 = P.sbuf("acc1", [1, 512], F32)
    acc2 = P.sbuf("acc2", [1, 512], F32)
    rA = P.sbuf("rA", [1, 512], F32)
    rR = P.sbuf("rR", [1, 512], F32)
    Rb = P.sbuf("Rb", [128, 512], F32)
    Ab = P.sbuf("Ab", [128, 512], F32)
    psU = P.psum("psU", [128, 512])
    psY = [P.psum("psY0", [128, 512]), P.psum("psY1", [128, 512])]
    ps_s1 = P.psum("ps_s1", [1, 512])
    ps_s2 = P.psum("ps_s2", [1, 512])
    ps_b = P.psum("ps_b", [128, 512])
    ps_b2 = P.psum("ps_b2", [128, 512])
    ldw = P.sem("ldw")
    ld = P.sem("ld")
    sto = P.sem("sto")

    P.wait('scalar', P.c['vector'], consts['_v'])

    P.dma('sync', upsc_sb[:, :], upsc_in[:, :], ld)
    P.dma('sync', dsc_sb[:, :], dsc_in[:, :], ld)
    scv = P.dma('sync', osc_sb[:, :], osc_in[:, :], ld)
    P.wait('tensor', ld, scv)
    P.wait('tensor', P.c['vector'], consts['_v'])
    P.call('tensor', 'matmul', out=ps_b[:, 0:1], lhsT=consts['ones1_f32'][:, :],
           rhs=osc_sb[:, :], start=True, stop=True)
    tb = P.c['tensor'].n
    P.wait('vector', P.c['tensor'], tb)
    P.wait('vector', ld, scv)
    cv = P.call('vector', 'tensor_copy', out=osb[:, :], in_=ps_b[:, 0:1])
    P.wait('vector', P.c['vector'], cv)
    dscmv = P.call('vector', 'tensor_scalar', out=dscm[:, :], in0=dsc_sb[:, :],
                   scalar1=osb[:, 0:1], scalar2=None, op0=ALU.mult)
    if dbg:
        P.wait('sync', P.c['vector'], dscmv)
        P.dma('sync', s_dbg[:, 0:1], osb[:, :], sto, allow_slow_non_contiguous=True)
        P.dma('sync', s_dbg[:, 1:17], dscm[:, :], sto, allow_slow_non_contiguous=True)

    for half in range(2):
        t0 = half * 512
        if half == 1:
            P.wait('sync', sto, prev_sto)
        lv = None
        for kc0 in range(0, NCH, 4):
            src = xn_in[kc0 * 128:(kc0 + 4) * 128, t0:t0 + 512].rearrange(
                "(kc p) n -> p kc n", p=128)
            dst = xn_sb[:, kc0 * 512:(kc0 + 4) * 512].rearrange("p (kc n) -> p kc n", kc=4)
            lv = P.dma('sync', dst, src, ld)
        xv = None
        for kc0 in range(0, NCH, 4):
            src = x3_in[kc0 * 128:(kc0 + 4) * 128, t0:t0 + 512].rearrange(
                "(kc p) n -> p kc n", p=128)
            dst = x3_sb[:, kc0 * 512:(kc0 + 4) * 512].rearrange("p (kc n) -> p kc n", kc=4)
            xv = P.dma('sync', dst, src, ld)
        if half == 1:
            P.wait('tensor', ld, lv)
            P.wait('tensor', P.c['vector'], P.c['vector'].n)
            P.wait('tensor', P.c['scalar'], P.c['scalar'].n)
        P.calln('vector', 'memset', ap=acc1[:, :], constant=0.0)
        P.calln('vector', 'memset', ap=acc2[:, :], constant=0.0)

        wv = {}
        wv[0] = load_w_cols(P, 'gpsimd', ldw, up_in, 0, 128, wu_sb, 0, D)
        umm = {}
        hs_s = {}
        for j in range(NJ):
            woff = (j % 2) * NCH * 128
            if j + 1 < NJ:
                if j >= 1:
                    P.wait('gpsimd', P.c['tensor'], umm[j - 1])
                wv[j + 1] = load_w_cols(P, 'gpsimd', ldw, up_in, (j + 1) * 128, 128,
                                        wu_sb, ((j + 1) % 2) * NCH * 128, D)
            P.wait('tensor', ldw, wv[j])
            if half == 0 and j == 0:
                P.wait('tensor', ld, lv)
            if j >= 1:
                P.wait('tensor', P.c['scalar'], hs_s[j - 1])
            for kc in range(NCH):
                P.call('tensor', 'matmul', out=psU[:, :],
                       lhsT=wu_sb[:, woff + kc * 128:woff + (kc + 1) * 128],
                       rhs=xn_sb[:, kc * 512:(kc + 1) * 512],
                       start=(kc == 0), stop=(kc == NCH - 1))
            umm[j] = P.c['tensor'].n
            P.wait('scalar', P.c['tensor'], umm[j])
            hs_s[j] = P.call('scalar', 'activation', out=h_sb[:, j * 512:(j + 1) * 512],
                             in_=psU[:, :], func=AF.Relu,
                             bias=consts['zero_b'][:, :], scale=upsc_sb[:, j:j + 1])
            if dbg and half == 0:
                P.wait('sync', P.c['scalar'], hs_s[j])
                P.dma('sync', h_dbg[j * 128:(j + 1) * 128, :],
                      h_sb[:, j * 512:(j + 1) * 512], sto)
        wpv = {}

        def load_piece1(oc, pc):
            dst_off = ((2 * oc + pc) % 2) * 32 * 128
            v = None
            for kk in range(0, 32, 4):
                ss = dn_in[pc * 32 * 128 + kk * 128: pc * 32 * 128 + (kk + 4) * 128,
                           oc * 128:(oc + 1) * 128].rearrange("(kc p) n -> p kc n", p=128)
                dd = wd_sb[:, dst_off + kk * 128:dst_off + (kk + 4) * 128].rearrange(
                    "p (kc n) -> p kc n", kc=4)
                v = P.dma('gpsimd', dd, ss, ldw)
            return v

        dmm = {}
        yv = {}
        wpv[(0, 0)] = load_piece1(0, 0)
        P.wait('tensor', P.c['scalar'], hs_s[NJ - 1])
        pseq = [(oc, pc) for oc in range(NCH) for pc in range(2)]
        for idx, (oc, pc) in enumerate(pseq):
            if idx + 1 < len(pseq):
                if idx >= 1:
                    P.wait('gpsimd', P.c['tensor'], dmm[pseq[idx - 1]])
                wpv[pseq[idx + 1]] = load_piece1(*pseq[idx + 1])
            P.wait('tensor', ldw, wpv[(oc, pc)])
            if pc == 0 and oc >= 2:
                P.wait('tensor', P.c['vector'], yv[oc - 2])
            woff = (idx % 2) * 32 * 128
            bank = psY[oc % 2]
            for kk in range(32):
                j = pc * 32 + kk
                P.call('tensor', 'matmul', out=bank[:, :],
                       lhsT=wd_sb[:, woff + kk * 128:woff + (kk + 1) * 128],
                       rhs=h_sb[:, j * 512:(j + 1) * 512],
                       start=(j == 0), stop=(j == NJ - 1))
            dmm[(oc, pc)] = P.c['tensor'].n
            if pc == 1:
                P.wait('vector', P.c['tensor'], dmm[(oc, pc)])
                if half == 0 and oc == 0:
                    P.wait('vector', P.c['vector'], dscmv)
                ys = y_sb[:, oc * 512:(oc + 1) * 512]
                yv[oc] = P.call('vector', 'tensor_scalar', out=ys, in0=bank[:, :],
                                scalar1=dscm[:, oc:oc + 1], scalar2=None, op0=ALU.mult)
                if dbg and half == 0:
                    P.wait('sync', P.c['vector'], yv[oc])
                    P.dma('sync', y_dbg[oc * 128:(oc + 1) * 128, :], ys, sto)
                stats_chunk(P, consts, ys, yb_sb[:, 0:512], sq_sb, ps_s1, ps_s2,
                            acc1, acc2, 512)

        vv = ln_finalize(P, consts, D, acc1, acc2, rA, rR, "eps_ln", 512)
        P.wait('tensor', P.c['vector'], vv)
        P.call('tensor', 'matmul', out=ps_b[:, :], lhsT=consts['ones1_f32'][:, :],
               rhs=rR[0:1, :], start=True, stop=True)
        P.call('tensor', 'matmul', out=ps_b2[:, :], lhsT=consts['ones1_f32'][:, :],
               rhs=rA[0:1, :], start=True, stop=True)
        tb = P.c['tensor'].n
        P.wait('vector', P.c['tensor'], tb)
        P.call('vector', 'tensor_copy', out=Rb[:, :], in_=ps_b[:, :])
        P.call('vector', 'tensor_copy', out=Ab[:, :], in_=ps_b2[:, :])
        P.wait('vector', ld, xv)
        for oc in range(NCH):
            ys = y_sb[:, oc * 512:(oc + 1) * 512]
            xs = x3_sb[:, oc * 512:(oc + 1) * 512]
            P.call('vector', 'tensor_mul', out=ys, in0=ys, in1=Rb[:, :])
            P.call('vector', 'tensor_add', out=ys, in0=ys, in1=Ab[:, :])
            vv = P.call('vector', 'tensor_add', out=xs, in0=xs, in1=ys)
            P.wait('sync', P.c['vector'], vv)
            P.dma('sync', x4_out[oc * 128:(oc + 1) * 128, t0:t0 + 512], xs, sto)
        prev_sto = sto.n
    P.wait('sync', sto, sto.n)
    if _P is None:
        return P.finish()


_BUILD_CACHE = {}


def _built(key, fn):
    if key not in _BUILD_CACHE:
        _BUILD_CACHE[key] = fn()
    return _BUILD_CACHE[key]


_CONST_CACHE = {}


def _host_consts():
    if _CONST_CACHE:
        return _CONST_CACHE
    C0, S0t, P0 = _rope_tables_l0()
    C1, S1t, P1 = _rope_tables_l1()
    _CONST_CACHE.update(dict(C0=C0, S0=S0t, P0=P0, C1=C1, S1=S1t, P1=P1,
                             M0=_masks_l0(), M1=_masks_l1()))
    Ebf, Ebc = _group_mats()
    _CONST_CACHE.update(dict(Ebf=Ebf, Ebc=Ebc))
    return _CONST_CACHE


def _ct(a):
    return np.ascontiguousarray(a)


FUSED = True


def kernel(**inp):
    if FUSED:
        return kernel_fused(inp)
    f32 = np.float32
    HC = _host_consts()
    xT = _ct(np.asarray(inp["L0_attn_x"], f32)[0].T)
    Wqkv = np.asarray(inp["L0_attn_Wqkv"], f32)
    bqkv = np.asarray(inp["L0_attn_bqkv"], f32)
    Wout = _ct(np.asarray(inp["L0_attn_Wout"], f32))
    bout = np.asarray(inp["L0_attn_bout"], f32)
    wgu_t = _ct(np.asarray(inp["L0_ffn_w_gate_up"], f32).T)
    wp_t = _ct(np.asarray(inp["L0_ffn_w_proj"], f32).T)
    w1q = np.asarray(inp["L1_attn_w_q"], f32)
    w1k = np.asarray(inp["L1_attn_w_k"], f32)
    w1v = np.asarray(inp["L1_attn_w_v"], f32)
    w1o = _ct(np.asarray(inp["L1_attn_w_o"], f32))
    up_t = _ct(np.asarray(inp["L1_ffn_up_weight"], f32).T)
    dn_t = _ct(np.asarray(inp["L1_ffn_down_weight"], f32).T)
    upsc = np.asarray(inp["L1_ffn_up_scale"], f32)
    dsc = np.asarray(inp["L1_ffn_down_scale"], f32)
    osc = np.asarray(inp["L1_ffn_output_scale"], f32).reshape(1, 1)

    cols = [slice(TOK * c, TOK * (c + 1)) for c in range(NCORES)]

    nc = _built("quant", build_quant)
    qres = run8(nc, [{
        "wgu_sh": _ct(wgu_t[256 * c:256 * (c + 1)]),
        "wp_sh": _ct(wp_t[1024 * c:1024 * (c + 1)]),
        "up_sh": _ct(up_t[256 * c:256 * (c + 1)]),
        "dn_sh": _ct(dn_t[1024 * c:1024 * (c + 1)]),
        "Ebf": HC["Ebf"], "Ebc": HC["Ebc"],
    } for c in range(NCORES)])
    WGUQ = np.concatenate([r["wgu_q"] for r in qres], axis=0)
    WPQ = np.concatenate([r["wp_q"] for r in qres], axis=0)
    UPQ = np.concatenate([r["up_q"] for r in qres], axis=0)
    DNQ = np.concatenate([r["dn_q"] for r in qres], axis=0)

    nc = _built("ln0", build_ln0)
    res = run8(nc, [{"x_own": _ct(xT[:, cols[c]])} for c in range(NCORES)])
    Xn1 = np.concatenate([r["xn"] for r in res], axis=1)

    nc = _built("qkv0", lambda: build_qkv(0))
    res = run8(nc, [{
        "xn_full": Xn1,
        "wq": _ct(Wqkv[:, 256 * c:256 * (c + 1)]),
        "wk": _ct(Wqkv[:, 2048 + 256 * c:2048 + 256 * (c + 1)]),
        "wv": _ct(Wqkv[:, 4096 + 256 * c:4096 + 256 * (c + 1)]),
        "bq": _ct(bqkv[256 * c:256 * (c + 1)].reshape(2, 128).T),
        "bk": _ct(bqkv[2048 + 256 * c:2048 + 256 * (c + 1)].reshape(2, 128).T),
        "bv": _ct(bqkv[4096 + 256 * c:4096 + 256 * (c + 1)].reshape(2, 128).T),
        "Ctab": HC["C0"], "Stab": HC["S0"], "Pmat": HC["P0"],
    } for c in range(NCORES)])
    q0 = [r["qt"] for r in res]
    k0 = [r["kt"] for r in res]
    Vt0 = np.concatenate([r["vt"] for r in res], axis=0)
    VH0 = _vaug_l0(Vt0)
    K0full = np.concatenate(k0, axis=0)

    nc = _built("attn0", build_attn0)
    res = run8(nc, [{
        "qt": q0[c], "kt": _ct(K0full[256 * c:256 * (c + 1)]),
        "vh": VH0[c], "m0": HC["M0"],
    } for c in range(NCORES)])
    A0 = np.concatenate([r["at"] for r in res], axis=0)

    nc = _built("proj_b", lambda: build_proj(True))
    res = run8(nc, [{
        "a_cols": _ct(A0[:, cols[c]]),
        "w": Wout,
        "b": _ct(bout.reshape(NCH, 128).T),
        "x_cols": _ct(xT[:, cols[c]]),
    } for c in range(NCORES)])
    x1 = [r["x1"] for r in res]
    xn2 = [r["xn"] for r in res]

    nc = _built("ffn0", build_ffn0)
    res = run8(nc, [{
        "xn2": xn2[c], "wgu": WGUQ, "wp": WPQ, "x1": x1[c],
    } for c in range(NCORES)])
    x2 = [r["x2"] for r in res]
    Xn3 = np.concatenate([r["xn3"] for r in res], axis=1)

    nc = _built("qkv1", lambda: build_qkv(1))
    res = run8(nc, [{
        "xn_full": Xn3,
        "wq": _ct(w1q[:, 256 * c:256 * (c + 1)]),
        "wk": _ct(w1k[:, 128 * (c // 2):128 * (c // 2) + 128]),
        "wv": _ct(w1v[:, 128 * (c // 2):128 * (c // 2) + 128]),
        "Ctab": HC["C1"], "Stab": HC["S1"], "Pmat": HC["P1"],
    } for c in range(NCORES)])
    q1 = [r["qt"] for r in res]
    k1 = [r["kt"] for r in res]
    v1 = [r["vt"] for r in res]

    nc = _built("attn1", build_attn1)
    in_maps = []
    for c in range(NCORES):
        vh = np.asarray(v1[c])
        vr = vh.reshape(128, NKT, 128)
        VA = np.ones((128, NKT, 65), NPBF16)
        VA[:, :, :64] = vr[:64].transpose(2, 1, 0)
        VB = _ct(vr[64:].transpose(2, 1, 0))
        in_maps.append({"qt": q1[c], "kt": k1[c], "va": VA, "vb": VB, "m1": HC["M1"]})
    res = run8(nc, in_maps)
    A1 = np.concatenate([r["at"] for r in res], axis=0)

    nc = _built("proj_nb", lambda: build_proj(False))
    res = run8(nc, [{
        "a_cols": _ct(A1[:, cols[c]]),
        "w": w1o,
        "x_cols": x2[c],
    } for c in range(NCORES)])
    x3 = [r["x1"] for r in res]
    xn4 = [r["xn"] for r in res]

    nc = _built("ffn1", build_ffn1)
    res = run8(nc, [{
        "xn4": xn4[c], "up": UPQ, "dn": DNQ,
        "upsc": _ct(upsc.reshape(H1 // 128, 128).T),
        "dsc": _ct(dsc.reshape(NCH, 128).T),
        "osc": osc,
        "x3": x3[c],
    } for c in range(NCORES)])

    out = np.empty((1, S, D), f32)
    for c in range(NCORES):
        out[0, TOK * c:TOK * (c + 1), :] = np.asarray(res[c]["x4"]).T
    return out


RG8 = [[0, 1, 2, 3, 4, 5, 6, 7]]


def _collective(P, kind, ins, outs, sem):
    h = sem.h

    def f(E, kind=kind, ins=ins, outs=outs, h=h):
        from concourse.aot_env import allow_ici_all_to_all
        with allow_ici_all_to_all():
            E.collective_compute(kind, ALU.bypass, replica_groups=RG8,
                                 ins=ins, outs=outs).then_inc(h, 1)

    P.q['gpsimd'].append(f)
    sem.n += 1
    return sem.n


def _load_v0_fused(v0t):
    def f(P, consts, v_sb, ldsem):
        vtok = P.sbuf("vtok", [128, NKT * 256], BF16)
        P.calln('vector', 'memset', ap=v_sb[:, :], constant=1.0)
        lv = ldsem.n
        for kt in range(NKT):
            lv = P.dma('sync', vtok[:, kt * 256:(kt + 1) * 256],
                       v0t[:, kt * 128:(kt + 1) * 128], ldsem, transpose=True)
        P.wait('vector', ldsem, lv)
        vv = None
        for hh in range(HPC0):
            dst = v_sb[:, hh * NKT * 65:(hh + 1) * NKT * 65].rearrange(
                "p (t c) -> p t c", c=65)[:, :, 0:64]
            src = vtok[:, :].rearrange("p (t c) -> p t c", c=256)[:, :, hh * 64:(hh + 1) * 64]
            vv = P.call('vector', 'tensor_copy', out=dst, in_=src)
        P.wait('tensor', P.c['vector'], vv)
        P.wait('scalar', P.c['vector'], vv)
        return ldsem.n
    return f


def _load_v1_fused(v1t):
    def f(P, consts, va_sb, vb_sb, ldsem):
        vtok = P.sbuf("vtok1", [128, NKT * 128], BF16)
        P.calln('vector', 'memset', ap=va_sb[:, :], constant=1.0)
        lv = ldsem.n
        for kt in range(NKT):
            lv = P.dma('sync', vtok[:, kt * 128:(kt + 1) * 128],
                       v1t[:, kt * 128:(kt + 1) * 128], ldsem, transpose=True)
        P.wait('vector', ldsem, lv)
        dst = va_sb[:, :].rearrange("p (t c) -> p t c", c=65)[:, :, 0:64]
        src = vtok[:, :].rearrange("p (t c) -> p t c", c=128)[:, :, 0:64]
        P.call('vector', 'tensor_copy', out=dst, in_=src)
        dstb = vb_sb[:, :].rearrange("p (t c) -> p t c", c=64)
        srcb = vtok[:, :].rearrange("p (t c) -> p t c", c=128)[:, :, 64:128]
        vv = P.call('vector', 'tensor_copy', out=dstb, in_=srcb)
        P.wait('tensor', P.c['vector'], vv)
        P.wait('scalar', P.c['vector'], vv)
        return ldsem.n
    return f


def build_fused(dbg=False):
    nc = new_nc()

    def EI(name, shape, dt):
        return nc.dram_tensor(name, shape, dt, kind="ExternalInput").ap()

    def INT(name, shape, dt):
        return nc.dram_tensor(name, shape, dt, kind="Internal").ap()

    def SHD(name, shape, dt):
        return nc.dram_tensor(name, shape, dt, kind="Internal", addr_space="Shared").ap()

    ei = {}
    for nm, shape, dt in [
        ("x_own", [D, TOK], F32),
        ("w0q", [D, 256], F32), ("w0k", [D, 256], F32), ("w0v", [D, 256], F32),
        ("b0q", [128, 2], F32), ("b0k", [128, 2], F32), ("b0v", [128, 2], F32),
        ("C0", [128, S], BF16), ("S0", [128, S], BF16), ("P0", [128, 128], BF16),
        ("C1", [128, S], BF16), ("S1", [128, S], BF16), ("P1", [128, 128], BF16),
        ("m0", [4, 128, 512], BF16), ("m1", [12, 128, 512], BF16),
        ("wout", [D, D], F32), ("bout", [128, NCH], F32),
        ("w1q", [D, 256], F32), ("w1k", [D, 128], F32), ("w1v", [D, 128], F32),
        ("w1o", [D, D], F32),
        ("wgu_sh", [D // 8, 2 * H0], F32), ("wp_sh", [H0 // 8, D], F32),
        ("up_sh", [D // 8, H1], F32), ("dn_sh", [H1 // 8, D], F32),
        ("Ebf", [128, 2], BF16), ("Ebc", [2, 128], F32),
        ("upsc", [128, H1 // 128], F32), ("dsc", [128, NCH], F32), ("osc", [1, 1], F32),
    ]:
        ei[nm] = EI(nm, shape, dt)
    x4_out = nc.dram_tensor("x4", [D, TOK], F32, kind="ExternalOutput").ap()

    DBG = (lambda name, shape, dt: nc.dram_tensor(name, shape, dt,
           kind="ExternalOutput").ap()) if dbg else INT
    xn1_own = INT("xn1_own", [D, TOK], BF16)
    Xn1g = SHD("Xn1g", [8 * D, TOK], BF16)
    q0t = DBG("q0t_d", [256, S], BF16)
    k0t = DBG("k0t_d", [256, S], BF16)
    v0t = DBG("v0t_d", [256, S], BF16)
    a2a0_in = INT("a2a0_in", [D, TOK], BF16)
    a2a0_out = INT("a2a0_out", [D, TOK], BF16)
    x1d = DBG("x1_d", [D, TOK], F32)
    xn2d = DBG("xn2_d", [D, TOK], BF16)
    wguq_sh = INT("wguq_sh", [D // 8, 2 * H0], BF16)
    wpq_sh = INT("wpq_sh", [H0 // 8, D], BF16)
    upq_sh = INT("upq_sh", [D // 8, H1], BF16)
    dnq_sh = INT("dnq_sh", [H1 // 8, D], BF16)
    WGUQg = SHD("WGUQg", [D, 2 * H0], BF16)
    WPQg = SHD("WPQg", [H0, D], BF16)
    UPQg = SHD("UPQg", [D, H1], BF16)
    DNQg = SHD("DNQg", [H1, D], BF16)
    x2d = DBG("x2_d", [D, TOK], F32)
    xn3_own = INT("xn3_own", [D, TOK], BF16)
    Xn3g = SHD("Xn3g", [8 * D, TOK], BF16)
    q1t = INT("q1t_d", [256, S], BF16)
    k1t = INT("k1t_d", [128, S], BF16)
    v1t = INT("v1t_d", [128, S], BF16)
    a2a1_in = INT("a2a1_in", [D, TOK], BF16)
    a2a1_out = INT("a2a1_out", [D, TOK], BF16)
    x3d = DBG("x3_d", [D, TOK], F32)
    xn4d = DBG("xn4_d", [D, TOK], BF16)
    if dbg:
        globals()['_dbg_extra'] = True

    P = Prog(nc)
    consts = make_consts(P, [("eps_ln", LN_EPS), ("eps_rms", RMS_EPS),
                             ("negC", -EXPC), ("mhalf", -0.5)])
    cc = {nm: P.sem("cc_" + nm) for nm in
          ("xn1", "a0", "xn3", "a1", "wgu", "wp", "up", "dn")}

    def xn_block_g(gbuf):
        def f(tb, kc0, kc1, _g=gbuf):
            r = tb // 2
            col = (tb % 2) * 512
            return _g[r * D + kc0 * 128: r * D + kc1 * 128, col:col + 512]
        return f

    sto = P.sem("sto")
    st = P.sem("st")
    with P.scope():
        build_quant(_P=P, _consts=consts, _io={
            "wgu_sh": ei["wgu_sh"], "wp_sh": ei["wp_sh"],
            "up_sh": ei["up_sh"], "dn_sh": ei["dn_sh"],
            "wgu_q": wguq_sh, "wp_q": wpq_sh, "up_q": upq_sh, "dn_q": dnq_sh,
            "Ebf": ei["Ebf"], "Ebc": ei["Ebc"],
        })
    P.wait('gpsimd', sto, sto.n)
    _collective(P, "AllGather", [wguq_sh[:, :]], [WGUQg[:, :]], cc["wgu"])
    _collective(P, "AllGather", [wpq_sh[:, :]], [WPQg[:, :]], cc["wp"])
    _collective(P, "AllGather", [upq_sh[:, :]], [UPQg[:, :]], cc["up"])
    _collective(P, "AllGather", [dnq_sh[:, :]], [DNQg[:, :]], cc["dn"])
    P.barrier()

    with P.scope():
        build_ln0(_P=P, _consts=consts, _io={"x_own": ei["x_own"], "xn": xn1_own})
    P.wait('gpsimd', st, st.n)
    _collective(P, "AllGather", [xn1_own[:, :]], [Xn1g[:, :]], cc["xn1"])
    P.barrier()

    P.wait('sync', cc["xn1"], 1)
    with P.scope():
        build_qkv(0, _P=P, _consts=consts, _io={
            "xn_block": xn_block_g(Xn1g),
            "wq": ei["w0q"], "wk": ei["w0k"], "wv": ei["w0v"],
            "bq": ei["b0q"], "bk": ei["b0k"], "bv": ei["b0v"],
            "qt": q0t, "kt": k0t, "vt": v0t,
            "Ctab": ei["C0"], "Stab": ei["S0"], "Pmat": ei["P0"],
        })
    P.barrier()

    with P.scope():
        build_attn0(_P=P, _consts=consts, _io={
            "qt": q0t, "kt": k0t, "m0": ei["m0"],
            "load_v": _load_v0_fused(v0t),
            "a_dst": (lambda j, hh, _a=a2a0_in:
                      _a[(j // 2) * 256 + hh * 64:(j // 2) * 256 + hh * 64 + 64,
                         (j % 2) * 512:(j % 2) * 512 + 512]),
        })
    P.wait('gpsimd', sto, sto.n)
    _collective(P, "AllToAll", [a2a0_in[:, :]], [a2a0_out[:, :]], cc["a0"])
    P.barrier()

    P.wait('sync', cc["a0"], 1)
    with P.scope():
        build_proj(True, _P=P, _consts=consts, _io={
            "a_cols": a2a0_out, "w": ei["wout"], "b": ei["bout"],
            "x_cols": ei["x_own"], "x1": x1d, "xn": xn2d,
        })
    P.barrier()

    P.wait('gpsimd', cc["wgu"], 1)
    P.wait('gpsimd', cc["wp"], 1)
    with P.scope():
        build_ffn0(_P=P, _consts=consts, _io={
            "xn2": xn2d, "wgu": WGUQg, "wp": WPQg, "x1": x1d,
            "x2": x2d, "xn3": xn3_own,
        })
    P.wait('gpsimd', sto, sto.n)
    _collective(P, "AllGather", [xn3_own[:, :]], [Xn3g[:, :]], cc["xn3"])
    P.barrier()

    P.wait('sync', cc["xn3"], 1)
    with P.scope():
        build_qkv(1, _P=P, _consts=consts, _io={
            "xn_block": xn_block_g(Xn3g),
            "wq": ei["w1q"], "wk": ei["w1k"], "wv": ei["w1v"],
            "qt": q1t, "kt": k1t, "vt": v1t,
            "Ctab": ei["C1"], "Stab": ei["S1"], "Pmat": ei["P1"],
        })
    P.barrier()

    with P.scope():
        build_attn1(_P=P, _consts=consts, _io={
            "qt": q1t, "kt": k1t, "m1": ei["m1"],
            "load_v1": _load_v1_fused(v1t),
            "a_dst": (lambda j, qh, _a=a2a1_in:
                      _a[(j // 2) * 256 + qh * 128:(j // 2) * 256 + qh * 128 + 128,
                         (j % 2) * 512:(j % 2) * 512 + 512]),
        })
    P.wait('gpsimd', sto, sto.n)
    _collective(P, "AllToAll", [a2a1_in[:, :]], [a2a1_out[:, :]], cc["a1"])
    P.barrier()

    P.wait('sync', cc["a1"], 1)
    with P.scope():
        build_proj(False, _P=P, _consts=consts, _io={
            "a_cols": a2a1_out, "w": ei["w1o"],
            "x_cols": x2d, "x1": x3d, "xn": xn4d,
        })
    P.barrier()

    P.wait('gpsimd', cc["up"], 1)
    P.wait('gpsimd', cc["dn"], 1)
    with P.scope():
        build_ffn1(_P=P, _consts=consts, _io={
            "xn4": xn4d, "up": UPQg, "dn": DNQg,
            "upsc": ei["upsc"], "dsc": ei["dsc"], "osc": ei["osc"],
            "x3": x3d, "x4": x4_out,
        })
    return P.finish()


def kernel_fused(inp):
    f32 = np.float32
    HC = _host_consts()
    xT = _ct(np.asarray(inp["L0_attn_x"], f32)[0].T)
    Wqkv = np.asarray(inp["L0_attn_Wqkv"], f32)
    bqkv = np.asarray(inp["L0_attn_bqkv"], f32)
    Wout = _ct(np.asarray(inp["L0_attn_Wout"], f32))
    bout = np.asarray(inp["L0_attn_bout"], f32)
    wgu_t = _ct(np.asarray(inp["L0_ffn_w_gate_up"], f32).T)
    wp_t = _ct(np.asarray(inp["L0_ffn_w_proj"], f32).T)
    w1q = np.asarray(inp["L1_attn_w_q"], f32)
    w1k = np.asarray(inp["L1_attn_w_k"], f32)
    w1v = np.asarray(inp["L1_attn_w_v"], f32)
    w1o = _ct(np.asarray(inp["L1_attn_w_o"], f32))
    up_t = _ct(np.asarray(inp["L1_ffn_up_weight"], f32).T)
    dn_t = _ct(np.asarray(inp["L1_ffn_down_weight"], f32).T)
    upsc = np.asarray(inp["L1_ffn_up_scale"], f32)
    dsc = np.asarray(inp["L1_ffn_down_scale"], f32)
    osc = np.asarray(inp["L1_ffn_output_scale"], f32).reshape(1, 1)

    nc = _built("fused", build_fused)
    in_maps = []
    for c in range(NCORES):
        g = c // 2
        in_maps.append({
            "x_own": _ct(xT[:, TOK * c:TOK * (c + 1)]),
            "w0q": _ct(Wqkv[:, 256 * c:256 * (c + 1)]),
            "w0k": _ct(Wqkv[:, 2048 + 256 * c:2048 + 256 * (c + 1)]),
            "w0v": _ct(Wqkv[:, 4096 + 256 * c:4096 + 256 * (c + 1)]),
            "b0q": _ct(bqkv[256 * c:256 * (c + 1)].reshape(2, 128).T),
            "b0k": _ct(bqkv[2048 + 256 * c:2048 + 256 * (c + 1)].reshape(2, 128).T),
            "b0v": _ct(bqkv[4096 + 256 * c:4096 + 256 * (c + 1)].reshape(2, 128).T),
            "C0": HC["C0"], "S0": HC["S0"], "P0": HC["P0"],
            "C1": HC["C1"], "S1": HC["S1"], "P1": HC["P1"],
            "m0": HC["M0"], "m1": HC["M1"],
            "wout": Wout, "bout": _ct(bout.reshape(NCH, 128).T),
            "w1q": _ct(w1q[:, 256 * c:256 * (c + 1)]),
            "w1k": _ct(w1k[:, 128 * g:128 * g + 128]),
            "w1v": _ct(w1v[:, 128 * g:128 * g + 128]),
            "w1o": w1o,
            "wgu_sh": _ct(wgu_t[256 * c:256 * (c + 1)]),
            "wp_sh": _ct(wp_t[1024 * c:1024 * (c + 1)]),
            "up_sh": _ct(up_t[256 * c:256 * (c + 1)]),
            "dn_sh": _ct(dn_t[1024 * c:1024 * (c + 1)]),
            "Ebf": HC["Ebf"], "Ebc": HC["Ebc"],
            "upsc": _ct(upsc.reshape(H1 // 128, 128).T),
            "dsc": _ct(dsc.reshape(NCH, 128).T),
            "osc": osc,
        })
    res = run8(nc, in_maps)
    globals()["_last_res"] = res
    out = np.empty((1, S, D), f32)
    for c in range(NCORES):
        out[0, TOK * c:TOK * (c + 1), :] = np.asarray(res[c]["x4"]).T
    return out


# version 05
# speedup vs baseline: 1.4360x; 1.4360x over previous
import contextlib
import numpy as np
import ml_dtypes
import concourse.bass as bass
import concourse.mybir as mybir
from concourse.bass_utils import run_bass_kernel_spmd

F32 = mybir.dt.float32
BF16 = mybir.dt.bfloat16
AF = mybir.ActivationFunctionType
ALU = mybir.AluOpType
NPBF16 = ml_dtypes.bfloat16

NCORES = 8
S = 8192
D = 2048
TOK = S // NCORES
NCH = D // 128
LN_EPS = 1e-5
RMS_EPS = float(np.finfo(np.float32).eps)
EXPC = 10.0
NH0, HD0, ROT0 = 32, 64, 32
HPC0 = NH0 // NCORES
SC0 = 0.125
NH1, NKV1, HD1, WIN = 16, 4, 128, 1024
HPC1 = NH1 // NCORES
SC1 = float(1.0 / np.sqrt(HD1))
H0 = 4 * D
H1 = 8192
NKT = S // 128


class SemT:
    __slots__ = ("h", "n")
    def __init__(self, h):
        self.h = h
        self.n = 0


class Prog:
    ENG = ("sync", "gpsimd", "scalar", "vector", "tensor")

    def __init__(self, nc):
        self.nc = nc
        self.stack = contextlib.ExitStack()
        self.scopes = [self.stack]
        self.q = {e: [] for e in self.ENG}
        self._sems = {}
        self.dma_sems = []
        self._uid = 0
        self.c = {e: self.sem("c_" + e) for e in ("tensor", "vector", "scalar", "gpsimd")}

    def sem(self, name):
        if name not in self._sems:
            self._sems[name] = SemT(self.stack.enter_context(self.nc.semaphore(name)))
        return self._sems[name]

    def sbuf(self, name, shape, dt):
        self._uid += 1
        return self.scopes[-1].enter_context(
            self.nc.sbuf_tensor(f"{name}_{self._uid}", shape, dt))

    def psum(self, name, shape, dt=F32):
        self._uid += 1
        return self.scopes[-1].enter_context(
            self.nc.psum_tensor(f"{name}_{self._uid}", shape, dt))

    @contextlib.contextmanager
    def scope(self):
        es = contextlib.ExitStack()
        self.scopes.append(es)
        try:
            yield
        finally:
            self.scopes.pop()
            es.close()

    def barrier(self):
        marks = {e: self.c[e].n for e in ("tensor", "vector", "scalar", "gpsimd")}
        dmarks = [(sm, sm.n) for sm in self.dma_sems]
        for e in self.ENG:
            for o, v in marks.items():
                if o != e:
                    self.wait(e, self.c[o], v)
            for sm, v in dmarks:
                self.wait(e, sm, v)

    def call(self, eng, method, **kw):
        cnt = self.c[eng]
        h = cnt.h

        def f(E, m=method, kw=kw, h=h):
            getattr(E, m)(**kw).then_inc(h, 1)

        self.q[eng].append(f)
        cnt.n += 1
        return cnt.n

    def calln(self, eng, method, **kw):
        def f(E, m=method, kw=kw):
            getattr(E, m)(**kw)

        self.q[eng].append(f)

    def dma(self, eng, out, in_, sem, **kw):
        h = sem.h
        if sem not in self.dma_sems:
            self.dma_sems.append(sem)

        def f(E, kw=kw, h=h, out=out, in_=in_):
            E.dma_start(out=out, in_=in_, **kw).then_inc(h, 16)

        self.q[eng].append(f)
        sem.n += 16
        return sem.n

    def wait(self, eng, sem, val=None):
        v = sem.n if val is None else val
        if v <= 0:
            return
        self.q[eng].append(lambda E, h=sem.h, v=v: E.wait_ge(h, v))

    def finish(self):
        with self.nc.Block() as block:
            for e in self.ENG:
                if self.q[e]:
                    getattr(block, e)(lambda E, lst=tuple(self.q[e]): [f(E) for f in lst])
        self.stack.close()
        return self.nc


def new_nc():
    return bass.Bass("TRN2", target_bir_lowering=False)


def run8(nc, in_maps):
    return run_bass_kernel_spmd(nc, in_maps, core_ids=list(range(NCORES))).results


def make_consts(P, extra=()):
    consts = {}
    tile = P.sbuf("consts", [128, 8], F32)
    ones1 = P.sbuf("ones1", [1, 128], F32)
    onesb = P.sbuf("onesb", [128, 1], BF16)
    P.calln('vector', 'memset', ap=ones1[:, :], constant=1.0)
    P.calln('vector', 'memset', ap=onesb[:, :], constant=1.0)
    consts['ones1_f32'] = ones1
    consts['ones_bf'] = onesb
    zb = P.sbuf("zero_b", [128, 1], F32)
    P.calln('vector', 'memset', ap=zb[:, :], constant=0.0)
    consts['zero_b'] = zb
    col = 0
    for nm, val in extra:
        P.calln('vector', 'memset', ap=tile[:, col:col + 1], constant=float(val))
        consts[nm] = tile[:, col:col + 1]
        col += 1
    consts['_v'] = P.call('vector', 'tensor_copy', out=tile[:, 7:8], in_=tile[:, 7:8])
    return consts


def load_w_cols(P, eng, sem, w_dram, c0, ncols, dest_sb, dest_off, kdim):
    nkc = kdim // 128
    v = sem.n
    for kc0 in range(0, nkc, 4):
        kc1 = min(kc0 + 4, nkc)
        src = w_dram[kc0 * 128:kc1 * 128, c0:c0 + ncols].rearrange(
            "(kc p) n -> p kc n", p=128)
        dst = dest_sb[:, dest_off + kc0 * ncols: dest_off + kc1 * ncols].rearrange(
            "p (kc n) -> p kc n", kc=kc1 - kc0)
        v = P.dma(eng, dst, src, sem)
    return v


def stats_chunk(P, consts, src_f32_or_none, src_bf, sq_bf, ps_s1, ps_s2,
                acc1, acc2, T):
    onesb = consts['ones_bf']
    ventry = P.c['vector'].n
    sentry = P.c['scalar'].n
    tentry = P.c['tensor'].n
    if src_f32_or_none is not None:
        P.wait('vector', P.c['scalar'], sentry)
        vv = P.call('vector', 'tensor_copy', out=src_bf, in_=src_f32_or_none)
        P.wait('scalar', P.c['vector'], vv)
    P.wait('scalar', P.c['tensor'], tentry)
    sv = P.call('scalar', 'square', out=sq_bf[:, 0:T], in_=src_bf)
    P.wait('tensor', P.c['scalar'], sv)
    P.wait('tensor', P.c['vector'], ventry)
    for h0 in range(0, T, 512):
        P.call('tensor', 'matmul', out=ps_s1[0:1, h0:h0 + 512], lhsT=onesb[:, :],
               rhs=src_bf[:, h0:h0 + 512], start=True, stop=True)
        P.call('tensor', 'matmul', out=ps_s2[0:1, h0:h0 + 512], lhsT=onesb[:, :],
               rhs=sq_bf[:, h0:h0 + 512], start=True, stop=True)
    tv = P.c['tensor'].n
    P.wait('vector', P.c['tensor'], tv)
    P.call('vector', 'tensor_add', out=acc1[:, 0:T], in0=acc1[:, 0:T],
           in1=ps_s1[0:1, 0:T])
    vv = P.call('vector', 'tensor_add', out=acc2[:, 0:T], in0=acc2[:, 0:T],
                in1=ps_s2[0:1, 0:T])
    return vv


def ln_finalize(P, consts, nfeat, acc1, acc2, rA, rR, eps_name, T):
    inv_n = 1.0 / nfeat
    P.call('vector', 'tensor_scalar_mul', out=rA[:, 0:T], in0=acc1[:, 0:T], scalar1=inv_n)
    P.call('vector', 'tensor_scalar_mul', out=rR[:, 0:T], in0=acc2[:, 0:T], scalar1=inv_n)
    P.call('vector', 'tensor_mul', out=acc1[:, 0:T], in0=rA[:, 0:T], in1=rA[:, 0:T])
    vw = P.call('vector', 'tensor_sub', out=rR[:, 0:T], in0=rR[:, 0:T], in1=acc1[:, 0:T])
    P.wait('scalar', P.c['vector'], vw)
    sw = P.call('scalar', 'activation', out=acc2[:, 0:T], in_=rR[:, 0:T],
                func=AF.Sqrt, bias=consts[eps_name][0:1, 0:1], scale=1.0)
    P.wait('vector', P.c['scalar'], sw)
    P.call('vector', 'reciprocal', out=rR[:, 0:T], in_=acc2[:, 0:T])
    P.call('vector', 'tensor_mul', out=rA[:, 0:T], in0=rA[:, 0:T], in1=rR[:, 0:T])
    vv = P.call('vector', 'tensor_scalar_mul', out=rA[:, 0:T], in0=rA[:, 0:T], scalar1=-1.0)
    return vv


def bcast2(P, consts, rR, rA, psR, psA, Rb, Ab, T):
    ones1 = consts['ones1_f32']
    for row, ps in ((rR, psR), (rA, psA)):
        for h0 in range(0, T, 512):
            P.call('tensor', 'matmul', out=ps[:, h0:h0 + 512], lhsT=ones1[:, :],
                   rhs=row[:, h0:h0 + 512], start=True, stop=True)
    tw = P.c['tensor'].n
    P.wait('vector', P.c['tensor'], tw)
    P.call('vector', 'tensor_copy', out=Rb[:, 0:T], in_=psR[:, 0:T])
    vv = P.call('vector', 'tensor_copy', out=Ab[:, 0:T], in_=psA[:, 0:T])
    return vv


def _rope_tables_l0():
    inv = 1.0 / (10000.0 ** (np.arange(0, ROT0, 2, dtype=np.float64) / ROT0))
    f = np.outer(np.arange(S, dtype=np.float64), inv)
    c = np.cos(f).T.astype(np.float32)
    s = np.sin(f).T.astype(np.float32)
    C = np.zeros((128, S), np.float32)
    Sx = np.zeros((128, S), np.float32)
    for b in (0, 64):
        C[b:b + 16] = c
        C[b + 16:b + 32] = c
        C[b + 32:b + 64] = 1.0
        Sx[b:b + 16] = s
        Sx[b + 16:b + 32] = s
    P = np.zeros((128, 128), np.float32)
    for b in (0, 64):
        for i in range(16):
            P[b + 16 + i, b + i] = -1.0
            P[b + i, b + 16 + i] = 1.0
    return C.astype(NPBF16), Sx.astype(NPBF16), P.astype(NPBF16)


def _rope_tables_l1():
    inv = 1.0 / (10000.0 ** (np.arange(0, HD1, 2, dtype=np.float64) / HD1))
    f = np.outer(np.arange(S, dtype=np.float64), inv)
    emb = np.concatenate([f, f], axis=1)
    C = np.cos(emb).T.astype(np.float32)
    Sx = np.sin(emb).T.astype(np.float32)
    P = np.zeros((128, 128), np.float32)
    for i in range(64):
        P[64 + i, i] = -1.0
        P[i, 64 + i] = 1.0
    return C.astype(NPBF16), Sx.astype(NPBF16), P.astype(NPBF16)


def _masks_l0():
    q = np.arange(512)[None, :]
    k = np.arange(128)[:, None]
    return np.stack([(q - k - 128 * i >= 0) for i in range(4)]).astype(NPBF16)


def _masks_l1():
    q = np.arange(512)[None, :]
    k = np.arange(128)[:, None]
    out = []
    for t in range(12):
        d = q - k + 1024 - 128 * t
        out.append(((d >= 0) & (d <= WIN)))
    return np.stack(out).astype(NPBF16)


def _vaug_l0(v_t_full):
    out = []
    vt = np.asarray(v_t_full)
    for c in range(NCORES):
        arr = np.ones((HPC0, 128, NKT, 65), NPBF16)
        for hh in range(HPC0):
            h = HPC0 * c + hh
            vh = vt[64 * h:64 * h + 64, :]
            vr = vh.reshape(64, NKT, 128)
            arr[hh, :, :, :64] = vr.transpose(2, 1, 0)
        out.append(arr)
    return out


def _group_mats():
    Ebf = np.zeros((128, 2), np.float32)
    Ebf[:64, 0] = 1.0
    Ebf[64:, 1] = 1.0
    Ebc = np.ascontiguousarray(Ebf.T)
    return Ebf.astype(NPBF16), Ebc.astype(np.float32)


def build_ln0(_P=None, _consts=None, _io=None):
    if _P is None:
        nc = new_nc()
        _io = {
            "x_own": nc.dram_tensor("x_own", [D, TOK], F32, kind="ExternalInput").ap(),
            "xn": nc.dram_tensor("xn", [D, TOK], BF16, kind="ExternalOutput").ap(),
        }
        P = Prog(nc)
        consts = make_consts(P, [("eps_ln", LN_EPS)])
    else:
        P, consts = _P, _consts
    x_in = _io["x_own"]
    xn_out = _io["xn"]

    x_sb = P.sbuf("x_sb", [128, NCH * TOK], F32)
    xb_sb = P.sbuf("xb_sb", [128, NCH * TOK], BF16)
    sq_sb = P.sbuf("sq_sb", [128, TOK], BF16)
    acc1 = P.sbuf("acc1", [1, TOK], F32)
    acc2 = P.sbuf("acc2", [1, TOK], F32)
    rA = P.sbuf("rA", [1, TOK], F32)
    rR = P.sbuf("rR", [1, TOK], F32)
    Rb = P.sbuf("Rb", [128, TOK], F32)
    Ab = P.sbuf("Ab", [128, TOK], F32)
    ps_s1 = P.psum("ps_s1", [1, TOK])
    ps_s2 = P.psum("ps_s2", [1, TOK])
    ps_b1 = P.psum("ps_b1", [128, TOK])
    ps_b2 = P.psum("ps_b2", [128, TOK])
    ld = P.sem("ld")
    st = P.sem("st")

    P.calln('vector', 'memset', ap=acc1[:, :], constant=0.0)
    P.calln('vector', 'memset', ap=acc2[:, :], constant=0.0)
    P.wait('scalar', P.c['vector'], consts['_v'])

    for ch in range(NCH):
        xs = x_sb[:, ch * TOK:(ch + 1) * TOK]
        xbs = xb_sb[:, ch * TOK:(ch + 1) * TOK]
        lv = P.dma('sync', xs, x_in[ch * 128:(ch + 1) * 128, :], ld)
        P.wait('vector', ld, lv)
        stats_chunk(P, consts, xs, xbs, sq_sb, ps_s1, ps_s2, acc1, acc2, TOK)

    vv = ln_finalize(P, consts, D, acc1, acc2, rA, rR, "eps_ln", TOK)
    P.wait('tensor', P.c['vector'], vv)
    vv = bcast2(P, consts, rR, rA, ps_b1, ps_b2, Rb, Ab, TOK)

    for ch in range(NCH):
        xs = x_sb[:, ch * TOK:(ch + 1) * TOK]
        xbs = xb_sb[:, ch * TOK:(ch + 1) * TOK]
        P.call('vector', 'tensor_mul', out=xs, in0=xs, in1=Rb[:, :])
        vv = P.call('vector', 'tensor_add', out=xbs, in0=xs, in1=Ab[:, :])
        P.wait('sync', P.c['vector'], vv)
        P.dma('sync', xn_out[ch * 128:(ch + 1) * 128, :], xbs, st)
    P.wait('sync', st, st.n)
    if _P is None:
        return P.finish()


def qkv_projs(layer):
    if layer == 0:
        return [("q", 256, True, True), ("k", 256, True, True), ("v", 256, True, False)]
    return [("q", 256, False, True), ("k", 128, False, True), ("v", 128, False, False)]


def build_qkv(layer, _P=None, _consts=None, _io=None):
    projs = qkv_projs(layer)
    if _P is None:
        nc = new_nc()
        _io = {}
        xn_full = nc.dram_tensor("xn_full", [D, S], BF16, kind="ExternalInput").ap()
        _io["xn_block"] = (lambda tb, kc0, kc1, _x=xn_full:
                           _x[kc0 * 128:kc1 * 128, tb * 512:(tb + 1) * 512])
        for nm, ncol, has_b, _ in projs:
            _io[f"w{nm}"] = nc.dram_tensor(f"w{nm}", [D, ncol], F32,
                                           kind="ExternalInput").ap()
            if has_b:
                _io[f"b{nm}"] = nc.dram_tensor(f"b{nm}", [128, ncol // 128], F32,
                                               kind="ExternalInput").ap()
            _io[f"{nm}t"] = nc.dram_tensor(f"{nm}t", [ncol, S], BF16,
                                           kind="ExternalOutput").ap()
        _io["Ctab"] = nc.dram_tensor("Ctab", [128, S], BF16, kind="ExternalInput").ap()
        _io["Stab"] = nc.dram_tensor("Stab", [128, S], BF16, kind="ExternalInput").ap()
        _io["Pmat"] = nc.dram_tensor("Pmat", [128, 128], BF16, kind="ExternalInput").ap()
        P = Prog(nc)
        consts = make_consts(P)
    else:
        P, consts = _P, _consts
    w_in = {nm: _io[f"w{nm}"] for nm, _, _, _ in projs}
    b_in = {nm: _io[f"b{nm}"] for nm, _, has_b, _ in projs if has_b}
    outs = {nm: _io[f"{nm}t"] for nm, _, _, _ in projs}
    C_in = _io["Ctab"]
    S_in = _io["Stab"]
    Pm_in = _io["Pmat"]
    xn_block = _io["xn_block"]
    ncols_tot = sum(p[1] for p in projs)
    n_oc = ncols_tot // 128

    w_sb = P.sbuf("w_sb", [128, NCH * ncols_tot], BF16)
    b_sb = P.sbuf("b_sb", [128, 8], F32)
    pm_sb = P.sbuf("pm_sb", [128, 128], BF16)
    ct_sb = P.sbuf("ct_sb", [128, 2 * 512], BF16)
    stt_sb = P.sbuf("stt_sb", [128, 2 * 512], BF16)
    xn_sb = P.sbuf("xn_sb", [128, 2 * NCH * 512], BF16)
    ev_sb = P.sbuf("ev_sb", [128, 4 * 512], BF16)
    rp_sb = P.sbuf("rp_sb", [128, 2 * 512], BF16)
    ps = [P.psum(f"ps{i}", [128, 512]) for i in range(6)]
    ldw = P.sem("ldw")
    ldx = P.sem("ldx")
    ldt = P.sem("ldt")
    sto = P.sem("sto")

    woff, wslices, bslices = 0, {}, {}
    for nm, ncol, has_b, _ in projs:
        load_w_cols(P, 'gpsimd', ldw, w_in[nm], 0, ncol, w_sb, woff, D)
        wslices[nm] = woff
        woff += NCH * ncol
    P.dma('gpsimd', pm_sb[:, :], Pm_in[:, :], ldw)
    boff = 0
    for nm, ncol, has_b, _ in projs:
        if has_b:
            nchk = ncol // 128
            P.dma('gpsimd', b_sb[:, boff:boff + nchk], b_in[nm][:, :], ldw)
            bslices[nm] = boff
            boff += nchk
    wld = ldw.n

    NTB = S // 512
    bank_evac = {}
    ev_dma = {}
    rp_state = [0, {}]
    tb_t = {}
    tb_v = {}
    oc_ctr = [0]

    for tb in range(NTB):
        buf = tb % 2
        xn_tb = xn_sb[:, buf * NCH * 512:(buf + 1) * NCH * 512]
        if tb >= 2:
            P.wait('sync', P.c['tensor'], tb_t[tb - 2])
            P.wait('sync', P.c['vector'], tb_v[tb - 2])
        lv = None
        for kc0 in range(0, NCH, 4):
            src = xn_block(tb, kc0, kc0 + 4).rearrange("(kc p) n -> p kc n", p=128)
            dst = xn_tb[:, kc0 * 512:(kc0 + 4) * 512].rearrange("p (kc n) -> p kc n", kc=4)
            lv = P.dma('sync', dst, src, ldx)
        ctv = P.dma('sync', ct_sb[:, buf * 512:(buf + 1) * 512],
                    C_in[:, tb * 512:(tb + 1) * 512], ldt)
        stv = P.dma('sync', stt_sb[:, buf * 512:(buf + 1) * 512],
                    S_in[:, tb * 512:(tb + 1) * 512], ldt)

        if tb == 0:
            P.wait('tensor', ldw, wld)
            P.wait('scalar', P.c['vector'], consts['_v'])
        P.wait('tensor', ldx, lv)

        for nm, ncol, has_b, rope in projs:
            for oc in range(ncol // 128):
                og = oc_ctr[0]
                key = og % 4
                bank = ps[key]
                if og >= 4:
                    P.wait('tensor', P.c['scalar'], bank_evac[key])
                for kc in range(NCH):
                    P.call('tensor', 'matmul', out=bank[:, :],
                           lhsT=w_sb[:, wslices[nm] + kc * ncol + oc * 128:
                                     wslices[nm] + kc * ncol + oc * 128 + 128],
                           rhs=xn_tb[:, kc * 512:(kc + 1) * 512],
                           start=(kc == 0), stop=(kc == NCH - 1))
                tmm = P.c['tensor'].n
                ev = ev_sb[:, key * 512:(key + 1) * 512]
                P.wait('scalar', P.c['tensor'], tmm)
                if key in ev_dma:
                    P.wait('scalar', sto, ev_dma[key])
                if has_b:
                    sv = P.call('scalar', 'activation', out=ev, in_=bank[:, :],
                                func=AF.Identity, bias=b_sb[:, bslices[nm] + oc:
                                                        bslices[nm] + oc + 1], scale=1.0)
                else:
                    sv = P.call('scalar', 'copy', out=ev, in_=bank[:, :])
                bank_evac[key] = sv

                orow = oc * 128
                if rope:
                    rc = rp_state[0]
                    rkey = rc % 2
                    rbank = ps[4 + rkey]
                    P.wait('tensor', P.c['scalar'], sv)
                    if rkey in rp_state[1]:
                        P.wait('tensor', P.c['vector'], rp_state[1][rkey])
                    tmm2 = P.call('tensor', 'matmul', out=rbank[:, :], lhsT=pm_sb[:, :],
                                  rhs=ev, start=True, stop=True)
                    P.wait('vector', P.c['tensor'], tmm2)
                    P.wait('vector', ldt, stv)
                    rtmp = rp_sb[:, rkey * 512:(rkey + 1) * 512]
                    P.call('vector', 'tensor_mul', out=rtmp, in0=rbank[:, :],
                           in1=stt_sb[:, buf * 512:(buf + 1) * 512])
                    rp_state[1][rkey] = P.c['vector'].n
                    P.call('vector', 'tensor_mul', out=ev, in0=ev,
                           in1=ct_sb[:, buf * 512:(buf + 1) * 512])
                    vv = P.call('vector', 'tensor_add', out=ev, in0=ev, in1=rtmp)
                    rp_state[0] = rc + 1
                    P.wait('sync', P.c['vector'], vv)
                else:
                    P.wait('sync', P.c['scalar'], sv)
                ev_dma[key] = P.dma('sync', outs[nm][orow:orow + 128,
                                                     tb * 512:(tb + 1) * 512], ev, sto)
                oc_ctr[0] += 1
        tb_t[tb] = P.c['tensor'].n
        tb_v[tb] = P.c['vector'].n
    P.wait('sync', sto, sto.n)
    if _P is None:
        return P.finish()


def build_attn0(_P=None, _consts=None, _io=None):
    if _P is None:
        nc = new_nc()
        _io = {
            "qt": nc.dram_tensor("qt", [256, S], BF16, kind="ExternalInput").ap(),
            "kt": nc.dram_tensor("kt", [256, S], BF16, kind="ExternalInput").ap(),
            "m0": nc.dram_tensor("m0", [4, 128, 512], BF16, kind="ExternalInput").ap(),
        }
        v_in = nc.dram_tensor("vh", [HPC0, 128, NKT, 65], BF16, kind="ExternalInput").ap()

        def _load_v(P, consts, v_sb, ldsem):
            for hh in range(HPC0):
                P.dma('gpsimd', v_sb[:, hh * NKT * 65:(hh + 1) * NKT * 65].rearrange(
                    "p (t c) -> p t c", c=65), v_in[hh], ldsem)
            return ldsem.n

        _io["load_v"] = _load_v
        a_out_t = nc.dram_tensor("at", [256, S], BF16, kind="ExternalOutput").ap()
        _io["a_dst"] = (lambda j, hh, _a=a_out_t:
                        _a[hh * 64:hh * 64 + 64, j * 512:(j + 1) * 512])
        P = Prog(nc)
        consts = make_consts(P, [("negC", -EXPC)])
    else:
        P, consts = _P, _consts
    q_in = _io["qt"]
    k_in = _io["kt"]
    m_in = _io["m0"]
    a_dst = _io["a_dst"]

    q_sb = P.sbuf("q_sb", [128, 2 * S], BF16)
    k_sb = P.sbuf("k_sb", [128, 2 * S], BF16)
    v_sb = P.sbuf("v_sb", [128, HPC0 * NKT * 65], BF16)
    m_sb = P.sbuf("m_sb", [128, 4 * 512], BF16)
    p_bf = P.sbuf("p_bf", [128, 4 * 512], BF16)
    b_sb = P.sbuf("b_sb", [128, 512], F32)
    a_ev = P.sbuf("a_ev", [128, 2 * 512], BF16)
    psS = [P.psum(f"psS{i}", [128, 512]) for i in range(3)]
    psO = [P.psum(f"psO{i}", [128, 512]) for i in range(4)]
    psB = P.psum("psB", [128, 512])
    ld = P.sem("ld")
    sto = P.sem("sto")

    for ch in range(2):
        P.dma('sync', q_sb[:, ch * S:(ch + 1) * S], q_in[ch * 128:(ch + 1) * 128, :], ld)
        P.dma('sync', k_sb[:, ch * S:(ch + 1) * S], k_in[ch * 128:(ch + 1) * 128, :], ld)
    _io["load_v"](P, consts, v_sb, ld)
    P.dma('gpsimd', m_sb[:, :].rearrange("p (i n) -> p i n", i=4),
          m_in.rearrange("i p n -> p i n"), ld)
    ldv = ld.n
    P.wait('tensor', ld, ldv)
    P.wait('scalar', ld, ldv)
    P.wait('vector', ld, ldv)
    P.wait('gpsimd', ld, ldv)
    P.wait('scalar', P.c['vector'], consts['_v'])
    P.wait('vector', P.c['vector'], consts['_v'])

    exp_cnt = {}
    pv_cnt = {}
    evac_v = {}
    bco_v = {}
    ev_dma_a = {}
    ti = 0
    g2 = 0
    gev = 0
    for j in range(S // 512):
        nkt = 4 * (j + 1)
        for hp in range(HPC0 // 2):
            accs = (psO[2 * (g2 % 2)], psO[2 * (g2 % 2) + 1])

            def emit_pv(item):
                (myti, acc, hh_, kt_, pb_, mv_, ev_, start_, stop_, gwait) = item
                if mv_ is not None:
                    P.wait('tensor', P.c['gpsimd'], mv_)
                else:
                    P.wait('tensor', P.c['scalar'], ev_)
                if gwait is not None:
                    P.wait('tensor', P.c['vector'], gwait)
                P.call('tensor', 'matmul', out=acc[0:65, :],
                       lhsT=v_sb[:, hh_ * NKT * 65 + kt_ * 65:
                                 hh_ * NKT * 65 + kt_ * 65 + 65],
                       rhs=pb_, start=start_, stop=stop_)
                pv_cnt[myti] = P.c['tensor'].n

            pend = []
            for kt in range(nkt):
                for hd in range(2):
                    hh = 2 * hp + hd
                    hch = hh // 2
                    hpp = (hh % 2) * 64
                    sb = psS[ti % 3]
                    if ti >= 3:
                        P.wait('tensor', P.c['scalar'], exp_cnt[ti - 3])
                    P.call('tensor', 'matmul', out=sb[:, :],
                           lhsT=k_sb[hpp:hpp + 64,
                                     hch * S + kt * 128: hch * S + kt * 128 + 128],
                           rhs=q_sb[hpp:hpp + 64,
                                    hch * S + j * 512: hch * S + (j + 1) * 512],
                           start=True, stop=True)
                    tsm = P.c['tensor'].n
                    P.wait('scalar', P.c['tensor'], tsm)
                    if ti >= 4:
                        P.wait('scalar', P.c['tensor'],
                               pv_cnt.get(ti - 4, 0) or P.c['tensor'].n - 1
                               if False else pv_cnt[ti - 4] if (ti - 4) in pv_cnt
                               else tsm - 1)
                    pb = p_bf[:, (ti % 4) * 512:(ti % 4 + 1) * 512]
                    ev = P.call('scalar', 'activation', out=pb, in_=sb[:, :],
                                func=AF.Exp, bias=consts['negC'][:, :], scale=SC0)
                    exp_cnt[ti] = ev
                    diag = kt - 4 * j
                    mv = None
                    if diag >= 0:
                        P.wait('gpsimd', P.c['scalar'], ev)
                        mv = P.call('gpsimd', 'tensor_mul', out=pb, in0=pb,
                                    in1=m_sb[:, diag * 512:(diag + 1) * 512])
                    gwait = evac_v[g2 - 2] if (kt == 0 and g2 >= 2) else None
                    pend.append((ti, accs[hd], hh, kt, pb, mv, ev,
                                 kt == 0, kt == nkt - 1, gwait))
                    if len(pend) > 2:
                        emit_pv(pend.pop(0))
                    ti += 1
            for item in pend:
                emit_pv(item)
            pend = []
            pvlast = pv_cnt[ti - 1]
            for hd in range(2):
                hh = 2 * hp + hd
                pso = accs[hd]
                P.wait('vector', P.c['tensor'], pvlast)
                P.call('vector', 'reciprocal', out=b_sb[0:1, 0:512], in_=pso[64:65, :])
                rv = P.c['vector'].n
                P.wait('tensor', P.c['vector'], rv)
                if gev >= 1:
                    P.wait('tensor', P.c['vector'], bco_v[gev - 1])
                P.call('tensor', 'matmul', out=psB[:, :], lhsT=consts['ones1_f32'][:, :],
                       rhs=b_sb[0:1, 0:512], start=True, stop=True)
                tbv = P.c['tensor'].n
                P.wait('vector', P.c['tensor'], tbv)
                bco_v[gev] = P.call('vector', 'tensor_copy', out=b_sb[:, :], in_=psB[:, :])
                aev = a_ev[:, (gev % 2) * 512:(gev % 2 + 1) * 512]
                if gev >= 2:
                    P.wait('vector', sto, ev_dma_a[gev - 2])
                vv = P.call('vector', 'tensor_mul', out=aev[0:64, :], in0=pso[0:64, :],
                            in1=b_sb[0:64, :])
                if hd == 1:
                    evac_v[g2] = vv
                P.wait('sync', P.c['vector'], vv)
                ev_dma_a[gev] = P.dma('sync', a_dst(j, hh), aev[0:64, :], sto)
                gev += 1
            g2 += 1
    P.wait('sync', sto, sto.n)
    if _P is None:
        return P.finish()


def build_attn1(_P=None, _consts=None, _io=None):
    if _P is None:
        nc = new_nc()
        _io = {
            "qt": nc.dram_tensor("qt", [256, S], BF16, kind="ExternalInput").ap(),
            "kt": nc.dram_tensor("kt", [128, S], BF16, kind="ExternalInput").ap(),
            "m1": nc.dram_tensor("m1", [12, 128, 512], BF16, kind="ExternalInput").ap(),
        }
        va_in = nc.dram_tensor("va", [128, NKT, 65], BF16, kind="ExternalInput").ap()
        vb_in = nc.dram_tensor("vb", [128, NKT, 64], BF16, kind="ExternalInput").ap()

        def _load_v1(P, consts, va_sb, vb_sb, ldsem):
            P.dma('gpsimd', va_sb[:, :].rearrange("p (t c) -> p t c", c=65),
                  va_in[:, :, :], ldsem)
            P.dma('gpsimd', vb_sb[:, :].rearrange("p (t c) -> p t c", c=64),
                  vb_in[:, :, :], ldsem)
            return ldsem.n

        _io["load_v1"] = _load_v1
        a_out_t = nc.dram_tensor("at", [256, S], BF16, kind="ExternalOutput").ap()
        _io["a_dst"] = (lambda j, qh, _a=a_out_t:
                        _a[qh * 128:qh * 128 + 128, j * 512:(j + 1) * 512])
        P = Prog(nc)
        consts = make_consts(P, [("negC", -EXPC)])
    else:
        P, consts = _P, _consts
    q_in = _io["qt"]
    k_in = _io["kt"]
    m_in = _io["m1"]
    a_dst = _io["a_dst"]

    q_sb = P.sbuf("q_sb", [128, 2 * S], BF16)
    k_sb = P.sbuf("k_sb", [128, S], BF16)
    va_sb = P.sbuf("va_sb", [128, NKT * 65], BF16)
    vb_sb = P.sbuf("vb_sb", [128, NKT * 64], BF16)
    m_sb = P.sbuf("m_sb", [128, 12 * 512], BF16)
    p_bf = P.sbuf("p_bf", [128, 4 * 512], BF16)
    b_sb = P.sbuf("b_sb", [128, 512], F32)
    a_ev = P.sbuf("a_ev", [128, 2 * 512], BF16)
    psS = [P.psum(f"psS{i}", [128, 512]) for i in range(3)]
    psOA = [P.psum("psOA0", [128, 512]), P.psum("psOA1", [128, 512])]
    psOB = [P.psum("psOB0", [128, 512]), P.psum("psOB1", [128, 512])]
    psB = P.psum("psB", [128, 512])
    ld = P.sem("ld")
    sto = P.sem("sto")

    for ch in range(2):
        P.dma('sync', q_sb[:, ch * S:(ch + 1) * S], q_in[ch * 128:(ch + 1) * 128, :], ld)
    P.dma('sync', k_sb[:, :], k_in[:, :], ld)
    _io["load_v1"](P, consts, va_sb, vb_sb, ld)
    P.dma('gpsimd', m_sb[:, :].rearrange("p (i n) -> p i n", i=12),
          m_in.rearrange("i p n -> p i n"), ld)
    ldv = ld.n
    for e in ('tensor', 'scalar', 'vector', 'gpsimd'):
        P.wait(e, ld, ldv)
    P.wait('scalar', P.c['vector'], consts['_v'])
    P.wait('vector', P.c['vector'], consts['_v'])

    MASKED = (0, 1, 2, 3, 8, 9, 10, 11)
    exp_cnt, pv_cnt, evac_v, ev_dma_a = {}, {}, {}, {}
    ti = 0
    for j in range(S // 512):
        tiles = [t for t in range(12) if 512 * j + 128 * t - 1024 >= 0]
        accs = ((psOA[0], psOB[0]), (psOA[1], psOB[1]))

        def emit_pv(item):
            (myti, qh_, kt_, pb_, mv_, ev_, start_, stop_, gwait) = item
            if mv_ is not None:
                P.wait('tensor', P.c['gpsimd'], mv_)
            else:
                P.wait('tensor', P.c['scalar'], ev_)
            if gwait is not None:
                P.wait('tensor', P.c['vector'], gwait)
            P.call('tensor', 'matmul', out=accs[qh_][0][0:65, :],
                   lhsT=va_sb[:, kt_ * 65:kt_ * 65 + 65], rhs=pb_,
                   start=start_, stop=stop_)
            P.call('tensor', 'matmul', out=accs[qh_][1][0:64, :],
                   lhsT=vb_sb[:, kt_ * 64:kt_ * 64 + 64], rhs=pb_,
                   start=start_, stop=stop_)
            pv_cnt[myti] = P.c['tensor'].n

        pend = []
        for it, t in enumerate(tiles):
            kt = (512 * j + 128 * t - 1024) // 128
            for qh in range(HPC1):
                sb = psS[ti % 3]
                if ti >= 3:
                    P.wait('tensor', P.c['scalar'], exp_cnt[ti - 3])
                P.call('tensor', 'matmul', out=sb[:, :],
                       lhsT=k_sb[:, kt * 128:kt * 128 + 128],
                       rhs=q_sb[:, qh * S + j * 512: qh * S + (j + 1) * 512],
                       start=True, stop=True)
                tsm = P.c['tensor'].n
                P.wait('scalar', P.c['tensor'], tsm)
                if ti >= 4 and (ti - 4) in pv_cnt:
                    P.wait('scalar', P.c['tensor'], pv_cnt[ti - 4])
                pb = p_bf[:, (ti % 4) * 512:(ti % 4 + 1) * 512]
                ev = P.call('scalar', 'activation', out=pb, in_=sb[:, :], func=AF.Exp,
                            bias=consts['negC'][:, :], scale=SC1)
                exp_cnt[ti] = ev
                mv = None
                if t in MASKED:
                    P.wait('gpsimd', P.c['scalar'], ev)
                    mv = P.call('gpsimd', 'tensor_mul', out=pb, in0=pb,
                                in1=m_sb[:, t * 512:(t + 1) * 512])
                gwait = evac_v[j - 1] if (it == 0 and j >= 1) else None
                pend.append((ti, qh, kt, pb, mv, ev, it == 0,
                             it == len(tiles) - 1, gwait))
                if len(pend) > 2:
                    emit_pv(pend.pop(0))
                ti += 1
        for item in pend:
            emit_pv(item)
        pend = []
        pvlast = pv_cnt[ti - 1]
        for qh in range(HPC1):
            poA, poB = accs[qh]
            P.wait('vector', P.c['tensor'], pvlast)
            P.call('vector', 'reciprocal', out=b_sb[0:1, 0:512], in_=poA[64:65, :])
            rv = P.c['vector'].n
            P.wait('tensor', P.c['vector'], rv)
            P.call('tensor', 'matmul', out=psB[:, :], lhsT=consts['ones1_f32'][:, :],
                   rhs=b_sb[0:1, 0:512], start=True, stop=True)
            tbv = P.c['tensor'].n
            P.wait('vector', P.c['tensor'], tbv)
            P.call('vector', 'tensor_copy', out=b_sb[:, :], in_=psB[:, :])
            aev = a_ev[:, (qh % 2) * 512:(qh % 2 + 1) * 512]
            if j >= 1:
                P.wait('vector', sto, ev_dma_a[(j - 1, qh)])
            P.call('vector', 'tensor_mul', out=aev[0:64, :], in0=poA[0:64, :],
                   in1=b_sb[0:64, :])
            vv = P.call('vector', 'tensor_mul', out=aev[64:128, :], in0=poB[0:64, :],
                        in1=b_sb[0:64, :])
            if qh == 1:
                evac_v[j] = vv
            P.wait('sync', P.c['vector'], vv)
            ev_dma_a[(j, qh)] = P.dma('sync', a_dst(j, qh), aev[:, :], sto)
    P.wait('sync', sto, sto.n)
    if _P is None:
        return P.finish()


def build_proj(has_bias, _P=None, _consts=None, _io=None):
    if _P is None:
        nc = new_nc()
        _io = {
            "a_cols": nc.dram_tensor("a_cols", [D, TOK], BF16, kind="ExternalInput").ap(),
            "w": nc.dram_tensor("w", [D, D], F32, kind="ExternalInput").ap(),
            "x_cols": nc.dram_tensor("x_cols", [D, TOK], F32, kind="ExternalInput").ap(),
            "x1": nc.dram_tensor("x1", [D, TOK], F32, kind="ExternalOutput").ap(),
            "xn": nc.dram_tensor("xn", [D, TOK], BF16, kind="ExternalOutput").ap(),
        }
        if has_bias:
            _io["b"] = nc.dram_tensor("b", [128, NCH], F32, kind="ExternalInput").ap()
        P = Prog(nc)
        consts = make_consts(P, [("eps_ln", LN_EPS)])
    else:
        P, consts = _P, _consts
    a_in = _io["a_cols"]
    w_in = _io["w"]
    b_in = _io.get("b")
    x_in = _io["x_cols"]
    x1_out = _io["x1"]
    xn_out = _io["xn"]

    a_sb = P.sbuf("a_sb", [128, NCH * TOK], BF16)
    x_sb = P.sbuf("x_sb", [128, NCH * TOK], F32)
    y_sb = P.sbuf("y_sb", [128, NCH * TOK], F32)
    w_sb = P.sbuf("w_sb", [128, 2 * NCH * 128], BF16)
    b_sb = P.sbuf("b_sb", [128, NCH], F32)
    yb_sb = P.sbuf("yb_sb", [128, 2 * TOK], BF16)
    sq_sb = P.sbuf("sq_sb", [128, TOK], BF16)
    acc1 = P.sbuf("acc1", [1, TOK], F32)
    acc2 = P.sbuf("acc2", [1, TOK], F32)
    rA = P.sbuf("rA", [1, TOK], F32)
    rR = P.sbuf("rR", [1, TOK], F32)
    Rb = P.sbuf("Rb", [128, TOK], F32)
    Ab = P.sbuf("Ab", [128, TOK], F32)
    ps = [P.psum(f"ps{i}", [128, 512]) for i in range(4)]
    ps_s1 = P.psum("ps_s1", [1, TOK])
    ps_s2 = P.psum("ps_s2", [1, TOK])
    ldw = P.sem("ldw")
    ld = P.sem("ld")
    sto = P.sem("sto")

    P.calln('vector', 'memset', ap=acc1[:, :], constant=0.0)
    P.calln('vector', 'memset', ap=acc2[:, :], constant=0.0)
    P.wait('scalar', P.c['vector'], consts['_v'])

    for kc0 in range(0, NCH, 4):
        src = a_in[kc0 * 128:(kc0 + 4) * 128, :].rearrange("(kc p) n -> p kc n", p=128)
        dst = a_sb[:, kc0 * TOK:(kc0 + 4) * TOK].rearrange("p (kc n) -> p kc n", kc=4)
        av = P.dma('sync', dst, src, ld)
    for kc0 in range(0, NCH, 4):
        src = x_in[kc0 * 128:(kc0 + 4) * 128, :].rearrange("(kc p) n -> p kc n", p=128)
        dst = x_sb[:, kc0 * TOK:(kc0 + 4) * TOK].rearrange("p (kc n) -> p kc n", kc=4)
        xv = P.dma('sync', dst, src, ld)
    if has_bias:
        P.dma('gpsimd', b_sb[:, :], b_in[:, :], ldw)
    wv = {}
    wv[0] = load_w_cols(P, 'gpsimd', ldw, w_in, 0, 128, w_sb, 0, D)

    ev_cnt = {}
    wmm = {}
    xnb_dma = {}
    for oc in range(NCH):
        if oc + 1 < NCH:
            if oc >= 1:
                P.wait('gpsimd', P.c['tensor'], wmm[oc - 1])
            wv[oc + 1] = load_w_cols(P, 'gpsimd', ldw, w_in, (oc + 1) * 128, 128,
                                     w_sb, ((oc + 1) % 2) * NCH * 128, D)
        P.wait('tensor', ldw, wv[oc])
        if oc == 0:
            P.wait('tensor', ld, av)
        woff = (oc % 2) * NCH * 128
        for half in range(2):
            bank = ps[(2 * oc + half) % 4]
            if 2 * oc + half >= 4:
                P.wait('tensor', P.c['scalar'], ev_cnt[(2 * oc + half) % 4])
            for kc in range(NCH):
                P.call('tensor', 'matmul', out=bank[:, :],
                       lhsT=w_sb[:, woff + kc * 128:woff + (kc + 1) * 128],
                       rhs=a_sb[:, kc * TOK + half * 512: kc * TOK + half * 512 + 512],
                       start=(kc == 0), stop=(kc == NCH - 1))
            tmm = P.c['tensor'].n
            wmm[oc] = tmm
            P.wait('scalar', P.c['tensor'], tmm)
            ydst = y_sb[:, oc * TOK + half * 512: oc * TOK + half * 512 + 512]
            if has_bias:
                sv = P.call('scalar', 'activation', out=ydst, in_=bank[:, :],
                            func=AF.Identity, bias=b_sb[:, oc:oc + 1], scale=1.0)
            else:
                sv = P.call('scalar', 'copy', out=ydst, in_=bank[:, :])
            ev_cnt[(2 * oc + half) % 4] = sv
        P.wait('vector', P.c['scalar'], P.c['scalar'].n)
        stats_chunk(P, consts, y_sb[:, oc * TOK:(oc + 1) * TOK],
                    yb_sb[:, 0:TOK], sq_sb, ps_s1, ps_s2, acc1, acc2, TOK)

    vv = ln_finalize(P, consts, D, acc1, acc2, rA, rR, "eps_ln", TOK)
    P.wait('tensor', P.c['vector'], vv)
    ones1 = consts['ones1_f32']
    for row, banks, dst in ((rR, (ps[0], ps[1]), Rb), (rA, (ps[2], ps[3]), Ab)):
        for h in range(2):
            P.call('tensor', 'matmul', out=banks[h][:, :], lhsT=ones1[:, :],
                   rhs=row[:, h * 512:(h + 1) * 512], start=True, stop=True)
    tw = P.c['tensor'].n
    P.wait('vector', P.c['tensor'], tw)
    for banks, dst in (((ps[0], ps[1]), Rb), ((ps[2], ps[3]), Ab)):
        for h in range(2):
            P.call('vector', 'tensor_copy', out=dst[:, h * 512:(h + 1) * 512],
                   in_=banks[h][:, :])
    P.calln('vector', 'memset', ap=acc1[:, :], constant=0.0)
    P.calln('vector', 'memset', ap=acc2[:, :], constant=0.0)
    P.wait('vector', ld, xv)
    for oc in range(NCH):
        ys = y_sb[:, oc * TOK:(oc + 1) * TOK]
        xs = x_sb[:, oc * TOK:(oc + 1) * TOK]
        P.call('vector', 'tensor_mul', out=ys, in0=ys, in1=Rb[:, :])
        P.call('vector', 'tensor_add', out=ys, in0=ys, in1=Ab[:, :])
        vv = P.call('vector', 'tensor_add', out=xs, in0=xs, in1=ys)
        P.wait('sync', P.c['vector'], vv)
        P.dma('sync', x1_out[oc * 128:(oc + 1) * 128, :], xs, sto)
        stats_chunk(P, consts, xs, yb_sb[:, 0:TOK], sq_sb, ps_s1, ps_s2, acc1, acc2, TOK)

    vv = ln_finalize(P, consts, D, acc1, acc2, rA, rR, "eps_ln", TOK)
    P.wait('tensor', P.c['vector'], vv)
    for row, banks in ((rR, (ps[0], ps[1])), (rA, (ps[2], ps[3]))):
        for h in range(2):
            P.call('tensor', 'matmul', out=banks[h][:, :], lhsT=ones1[:, :],
                   rhs=row[:, h * 512:(h + 1) * 512], start=True, stop=True)
    tw = P.c['tensor'].n
    P.wait('vector', P.c['tensor'], tw)
    for banks, dst in (((ps[0], ps[1]), Rb), ((ps[2], ps[3]), Ab)):
        for h in range(2):
            P.call('vector', 'tensor_copy', out=dst[:, h * 512:(h + 1) * 512],
                   in_=banks[h][:, :])
    for oc in range(NCH):
        xs = x_sb[:, oc * TOK:(oc + 1) * TOK]
        xnb = yb_sb[:, (oc % 2) * TOK:(oc % 2) * TOK + TOK]
        if oc >= 2:
            P.wait('vector', sto, xnb_dma[oc - 2])
        P.call('vector', 'tensor_mul', out=xs, in0=xs, in1=Rb[:, :])
        vv = P.call('vector', 'tensor_add', out=xnb, in0=xs, in1=Ab[:, :])
        P.wait('sync', P.c['vector'], vv)
        xnb_dma[oc] = P.dma('sync', xn_out[oc * 128:(oc + 1) * 128, :], xnb, sto)
    P.wait('sync', sto, sto.n)
    if _P is None:
        return P.finish()


def build_quant(_P=None, _consts=None, _io=None):
    if _P is None:
        nc = new_nc()
        _io = {
            "wgu_sh": nc.dram_tensor("wgu_sh", [D // 8, 2 * H0], F32, kind="ExternalInput").ap(),
            "wp_sh": nc.dram_tensor("wp_sh", [H0 // 8, D], F32, kind="ExternalInput").ap(),
            "up_sh": nc.dram_tensor("up_sh", [D // 8, H1], F32, kind="ExternalInput").ap(),
            "dn_sh": nc.dram_tensor("dn_sh", [H1 // 8, D], F32, kind="ExternalInput").ap(),
            "wgu_q": nc.dram_tensor("wgu_q", [D // 8, 2 * H0], BF16, kind="ExternalOutput").ap(),
            "wp_q": nc.dram_tensor("wp_q", [H0 // 8, D], BF16, kind="ExternalOutput").ap(),
            "up_q": nc.dram_tensor("up_q", [D // 8, H1], BF16, kind="ExternalOutput").ap(),
            "dn_q": nc.dram_tensor("dn_q", [H1 // 8, D], BF16, kind="ExternalOutput").ap(),
            "Ebf": nc.dram_tensor("Ebf", [128, 2], BF16, kind="ExternalInput").ap(),
            "Ebc": nc.dram_tensor("Ebc", [2, 128], F32, kind="ExternalInput").ap(),
        }
        P = Prog(nc)
        consts = make_consts(P, [("mhalf", -0.5)])
    else:
        P, consts = _P, _consts
    ins = {k: _io[k] for k in ("wgu_sh", "wp_sh", "up_sh", "dn_sh")}
    outs = {k: _io[k] for k in ("wgu_q", "wp_q", "up_q", "dn_q")}
    Ebf_in = _io["Ebf"]
    Ebc_in = _io["Ebc"]
    w_sb = P.sbuf("w_sb", [128, 2 * 512], F32)
    wb_sb = P.sbuf("wb_sb", [128, 512], BF16)
    a_sb = P.sbuf("a_sb", [128, 512], BF16)
    r_sb = P.sbuf("r_sb", [128, 512], BF16)
    t1_sb = P.sbuf("t1_sb", [128, 512], F32)
    t2_sb = P.sbuf("t2_sb", [128, 512], F32)
    wh_sb = P.sbuf("wh_sb", [128, 2 * 512], BF16)
    srow = P.sbuf("srow", [2, 512], F32)
    sbro = P.sbuf("sbro", [2, 512], BF16)
    s32 = P.sbuf("s32", [2, 512], F32)
    invs = P.sbuf("invs", [2, 512], F32)
    halfs = P.sbuf("halfs", [2, 512], F32)
    ebf_sb = P.sbuf("ebf_sb", [128, 2], BF16)
    ebc_sb = P.sbuf("ebc_sb", [2, 128], F32)
    ps_g = P.psum("ps_g", [2, 512])
    ps_r = P.psum("ps_r", [128, 512])
    ps_h = P.psum("ps_h", [128, 512])
    ld = P.sem("ld")
    sto = P.sem("sto")

    P.dma('sync', ebf_sb[:, :], Ebf_in[:, :], ld)
    P.dma('sync', ebc_sb[:, :], Ebc_in[:, :], ld)
    P.wait('tensor', ld, ld.n)
    P.wait('scalar', P.c['vector'], consts['_v'])

    def tern_block(w_in, wh_out, r0, c0, ib):
        buf = ib % 2
        ws = w_sb[:, buf * 512:(buf + 1) * 512]
        whs = wh_sb[:, buf * 512:(buf + 1) * 512]
        if ib >= 2:
            P.wait('sync', P.c['vector'], blk_v[ib - 2])
            P.wait('sync', sto, blk_st[ib - 2])
        lv = P.dma('sync', ws, w_in[r0:r0 + 128, c0:c0 + 512], ld)
        P.wait('vector', ld, lv)
        if ib >= 1:
            P.wait('vector', P.c['scalar'], blk_s[ib - 1])
        vv = P.call('vector', 'tensor_copy', out=wb_sb[:, :], in_=ws)
        P.wait('scalar', P.c['vector'], vv)
        sv = P.call('scalar', 'activation', out=a_sb[:, :], in_=wb_sb[:, :],
                    func=AF.Abs, bias=consts['zero_b'][:, :], scale=1.0)
        P.wait('tensor', P.c['scalar'], sv)
        if ib >= 1:
            P.wait('tensor', P.c['vector'], blk_gv[ib - 1])
        P.call('tensor', 'matmul', out=ps_g[:, :], lhsT=ebf_sb[:, :], rhs=a_sb[:, :],
               start=True, stop=True)
        tg = P.c['tensor'].n
        P.wait('vector', P.c['tensor'], tg)
        P.call('vector', 'tensor_scalar_mul', out=srow[:, :], in0=ps_g[:, :],
               scalar1=1.0 / 64.0)
        blk_gv[ib] = P.c['vector'].n
        P.call('vector', 'tensor_scalar_max', out=srow[:, :], in0=srow[:, :], scalar1=1e-8)
        P.call('vector', 'tensor_copy', out=sbro[:, :], in_=srow[:, :])
        P.call('vector', 'tensor_copy', out=s32[:, :], in_=sbro[:, :])
        vh = P.call('vector', 'reciprocal', out=invs[:, :], in_=s32[:, :])
        P.wait('tensor', P.c['vector'], vh)
        if ib >= 1:
            P.wait('tensor', P.c['vector'], blk_bv[ib - 1])
        P.call('tensor', 'matmul', out=ps_r[:, :], lhsT=ebc_sb[:, :], rhs=invs[:, :],
               start=True, stop=True)
        P.call('tensor', 'matmul', out=ps_h[:, :], lhsT=ebc_sb[:, :], rhs=s32[:, :],
               start=True, stop=True)
        tb = P.c['tensor'].n
        P.wait('vector', P.c['tensor'], tb)
        vr = P.call('vector', 'tensor_mul', out=r_sb[:, :], in0=wb_sb[:, :], in1=ps_r[:, :])
        blk_v[ib] = vr
        P.wait('scalar', P.c['vector'], vr)
        P.call('scalar', 'activation', out=t1_sb[:, :], in_=r_sb[:, :], func=AF.Sign,
               bias=consts['mhalf'][:, :], scale=1.0)
        ss = P.call('scalar', 'activation', out=t2_sb[:, :], in_=r_sb[:, :], func=AF.Sign,
                    bias=consts['mhalf'][:, :], scale=-1.0)
        blk_s[ib] = ss
        P.wait('vector', P.c['scalar'], ss)
        P.call('vector', 'tensor_scalar_max', out=t1_sb[:, :], in0=t1_sb[:, :], scalar1=0.0)
        P.call('vector', 'tensor_scalar_max', out=t2_sb[:, :], in0=t2_sb[:, :], scalar1=0.0)
        P.call('vector', 'tensor_sub', out=t1_sb[:, :], in0=t1_sb[:, :], in1=t2_sb[:, :])
        vv = P.call('vector', 'tensor_mul', out=whs, in0=t1_sb[:, :], in1=ps_h[:, :])
        blk_bv[ib] = vv
        P.wait('sync', P.c['vector'], vv)
        blk_st[ib] = P.dma('sync', wh_out[r0:r0 + 128, c0:c0 + 512], whs, sto)

    blk_v, blk_st, blk_s, blk_gv, blk_bv = {}, {}, {}, {}, {}
    ib = 0
    for nm_in, nm_out, rows, cols in (("wgu_sh", "wgu_q", D // 8, 2 * H0),
                                      ("wp_sh", "wp_q", H0 // 8, D)):
        for r0 in range(0, rows, 128):
            for c0 in range(0, cols, 512):
                tern_block(ins[nm_in], outs[nm_out], r0, c0, ib)
                ib += 1

    sg_v, sg_st = {}, {}
    jb = 0
    for nm_in, nm_out, rows, cols in (("up_sh", "up_q", D // 8, H1),
                                      ("dn_sh", "dn_q", H1 // 8, D)):
        for r0 in range(0, rows, 128):
            for c0 in range(0, cols, 512):
                buf = jb % 2
                ws = w_sb[:, buf * 512:(buf + 1) * 512]
                whs = wh_sb[:, buf * 512:(buf + 1) * 512]
                if jb == 0:
                    P.wait('sync', P.c['vector'], P.c['vector'].n)
                    P.wait('sync', sto, sto.n)
                if jb >= 2:
                    P.wait('sync', P.c['scalar'], sg_v[jb - 2])
                    P.wait('sync', sto, sg_st[jb - 2])
                lv = P.dma('sync', ws, ins[nm_in][r0:r0 + 128, c0:c0 + 512], ld)
                P.wait('scalar', ld, lv)
                sv = P.call('scalar', 'activation', out=whs, in_=ws, func=AF.Sign,
                            bias=consts['zero_b'][:, :], scale=1.0)
                sg_v[jb] = sv
                P.wait('sync', P.c['scalar'], sv)
                sg_st[jb] = P.dma('sync', outs[nm_out][r0:r0 + 128, c0:c0 + 512], whs, sto)
                jb += 1
    P.wait('sync', sto, sto.n)
    if _P is None:
        return P.finish()


def build_ffn0(_P=None, _consts=None, _io=None):
    if _P is None:
        nc = new_nc()
        _io = {
            "xn2": nc.dram_tensor("xn2", [D, TOK], BF16, kind="ExternalInput").ap(),
            "wgu": nc.dram_tensor("wgu", [D, 2 * H0], BF16, kind="ExternalInput").ap(),
            "wp": nc.dram_tensor("wp", [H0, D], BF16, kind="ExternalInput").ap(),
            "x1": nc.dram_tensor("x1", [D, TOK], F32, kind="ExternalInput").ap(),
            "x2": nc.dram_tensor("x2", [D, TOK], F32, kind="ExternalOutput").ap(),
            "xn3": nc.dram_tensor("xn3", [D, TOK], BF16, kind="ExternalOutput").ap(),
        }
        P = Prog(nc)
        consts = make_consts(P, [("eps_ln", LN_EPS), ("eps_rms", RMS_EPS)])
    else:
        P, consts = _P, _consts
    xn_in = _io["xn2"]
    wgu_in = _io["wgu"]
    wp_in = _io["wp"]
    x1_in = _io["x1"]
    x2_out = _io["x2"]
    xn3_out = _io["xn3"]
    NJ = H0 // 128

    xn_sb = P.sbuf("xn_sb", [128, NCH * 512], BF16)
    h_sb = P.sbuf("h_sb", [128, NJ * 512], BF16)
    wg_sb = P.sbuf("wg_sb", [128, 2 * 2 * NCH * 128], BF16)
    wp_sb = P.sbuf("wp_sb", [128, 2 * 32 * 128], BF16)
    x1_sb = P.sbuf("x1_sb", [128, NCH * 512], F32)
    y_sb = P.sbuf("y_sb", [128, NCH * 512], F32)
    sg_sb = P.sbuf("sg_sb", [128, 2 * 512], F32)
    sq_sb = P.sbuf("sq_sb", [128, 512], BF16)
    yb_sb = P.sbuf("yb_sb", [128, 2 * 512], BF16)
    racc = P.sbuf("racc", [1, 512], F32)
    acc1 = P.sbuf("acc1", [1, 512], F32)
    acc2 = P.sbuf("acc2", [1, 512], F32)
    rA = P.sbuf("rA", [1, 512], F32)
    rR = P.sbuf("rR", [1, 512], F32)
    Rb = P.sbuf("Rb", [128, 512], F32)
    Ab = P.sbuf("Ab", [128, 512], F32)
    bR = P.sbuf("bR", [128, 512], F32)
    psGs = [P.psum("psG", [128, 512]), P.psum("psGb", [128, 512])]
    psUs = [P.psum("psU", [128, 512]), P.psum("psUb", [128, 512])]
    psG = psGs[0]
    psU = psUs[0]
    psY = [P.psum("psY0", [128, 512]), P.psum("psY1", [128, 512])]
    ps_s1 = P.psum("ps_s1", [1, 512])
    ps_s2 = P.psum("ps_s2", [1, 512])
    ps_r = ps_s1
    ps_b = psGs[1]
    ldw = P.sem("ldw")
    ld = P.sem("ld")
    sto = P.sem("sto")

    P.wait('scalar', P.c['vector'], consts['_v'])

    for half in range(2):
        t0 = half * 512
        if half == 1:
            P.wait('sync', sto, prev_sto)
        lv = None
        for kc0 in range(0, NCH, 4):
            src = xn_in[kc0 * 128:(kc0 + 4) * 128, t0:t0 + 512].rearrange(
                "(kc p) n -> p kc n", p=128)
            dst = xn_sb[:, kc0 * 512:(kc0 + 4) * 512].rearrange("p (kc n) -> p kc n", kc=4)
            lv = P.dma('sync', dst, src, ld)
        xv = None
        for kc0 in range(0, NCH, 4):
            src = x1_in[kc0 * 128:(kc0 + 4) * 128, t0:t0 + 512].rearrange(
                "(kc p) n -> p kc n", p=128)
            dst = x1_sb[:, kc0 * 512:(kc0 + 4) * 512].rearrange("p (kc n) -> p kc n", kc=4)
            xv = P.dma('sync', dst, src, ld)
        if half == 1:
            P.wait('tensor', ld, lv)
            P.wait('tensor', P.c['vector'], P.c['vector'].n)
            P.wait('tensor', P.c['scalar'], P.c['scalar'].n)
        P.calln('vector', 'memset', ap=racc[:, :], constant=0.0)
        P.calln('vector', 'memset', ap=acc1[:, :], constant=0.0)
        P.calln('vector', 'memset', ap=acc2[:, :], constant=0.0)

        wv = {}
        wv[0] = load_w_cols(P, 'gpsimd', ldw, wgu_in, 0, 128, wg_sb, 0, D)
        wv[0] = load_w_cols(P, 'gpsimd', ldw, wgu_in, H0, 128, wg_sb, NCH * 128, D)
        gmm = {}
        hv = {}
        sqv = {}
        silu_j = {}
        for j in range(NJ):
            woff = (j % 2) * 2 * NCH * 128
            if j + 1 < NJ:
                woff2 = ((j + 1) % 2) * 2 * NCH * 128
                if j >= 1:
                    P.wait('gpsimd', P.c['tensor'], gmm[j - 1])
                wv[j + 1] = load_w_cols(P, 'gpsimd', ldw, wgu_in, (j + 1) * 128,
                                        128, wg_sb, woff2, D)
                wv[j + 1] = load_w_cols(P, 'gpsimd', ldw, wgu_in, H0 + (j + 1) * 128,
                                        128, wg_sb, woff2 + NCH * 128, D)
            P.wait('tensor', ldw, wv[j])
            if half == 0 and j == 0:
                P.wait('tensor', ld, lv)
            psGj = psGs[j % 2]
            psUj = psUs[j % 2]
            if j >= 2:
                P.wait('tensor', P.c['scalar'], silu_j[j - 2])
                P.wait('tensor', P.c['vector'], hv[j - 2])
            for kc in range(NCH):
                P.call('tensor', 'matmul', out=psGj[:, :],
                       lhsT=wg_sb[:, woff + kc * 128:woff + (kc + 1) * 128],
                       rhs=xn_sb[:, kc * 512:(kc + 1) * 512],
                       start=(kc == 0), stop=(kc == NCH - 1))
            for kc in range(NCH):
                P.call('tensor', 'matmul', out=psUj[:, :],
                       lhsT=wg_sb[:, woff + NCH * 128 + kc * 128:
                                  woff + NCH * 128 + (kc + 1) * 128],
                       rhs=xn_sb[:, kc * 512:(kc + 1) * 512],
                       start=(kc == 0), stop=(kc == NCH - 1))
            gmm[j] = P.c['tensor'].n
            P.wait('scalar', P.c['tensor'], gmm[j])
            if j >= 2:
                P.wait('scalar', P.c['vector'], hv[j - 2])
            silu_s = P.call('scalar', 'activation',
                            out=sg_sb[:, (j % 2) * 512:(j % 2) * 512 + 512],
                            in_=psGj[:, :],
                            func=AF.Silu, bias=consts['zero_b'][:, :], scale=1.0)
            silu_j[j] = silu_s
            P.wait('vector', P.c['scalar'], silu_s)
            hs = h_sb[:, j * 512:(j + 1) * 512]
            hv[j] = P.call('vector', 'tensor_mul', out=hs,
                           in0=sg_sb[:, (j % 2) * 512:(j % 2) * 512 + 512], in1=psUj[:, :])
            P.wait('scalar', P.c['vector'], hv[j])
            if j >= 1:
                P.wait('scalar', P.c['tensor'], rmm_t)
            sqv[j] = P.call('scalar', 'square', out=sq_sb[:, :], in_=hs)
            P.wait('tensor', P.c['scalar'], sqv[j])
            if j >= 1:
                P.wait('tensor', P.c['vector'], racc_v)
            P.call('tensor', 'matmul', out=ps_r[:, :], lhsT=consts['ones_bf'][:, :],
                   rhs=sq_sb[:, :], start=True, stop=True)
            rmm_t = P.c['tensor'].n
            P.wait('vector', P.c['tensor'], rmm_t)
            racc_v = P.call('vector', 'tensor_add', out=racc[:, :], in0=racc[:, :],
                            in1=ps_r[:, :])

        P.call('vector', 'tensor_scalar_mul', out=rR[:, :], in0=racc[:, :],
               scalar1=1.0 / H0)
        vw = P.c['vector'].n
        P.wait('scalar', P.c['vector'], vw)
        sw = P.call('scalar', 'activation', out=rA[:, :], in_=rR[:, :], func=AF.Sqrt,
                    bias=consts['eps_rms'][0:1, 0:1], scale=1.0)
        P.wait('vector', P.c['scalar'], sw)
        rv = P.call('vector', 'reciprocal', out=rR[:, :], in_=rA[:, :])
        P.wait('tensor', P.c['vector'], rv)
        P.call('tensor', 'matmul', out=ps_b[:, :], lhsT=consts['ones1_f32'][:, :],
               rhs=rR[0:1, :], start=True, stop=True)
        tb = P.c['tensor'].n
        P.wait('vector', P.c['tensor'], tb)
        brv = P.call('vector', 'tensor_copy', out=bR[:, :], in_=ps_b[:, :])

        dmm = {}
        yv = {}
        pieces = {}
        P.wait('tensor', P.c['vector'], hv[NJ - 1])

        def load_piece(oc, pc):
            dst_off = ((2 * oc + pc) % 2) * 32 * 128
            v = None
            for kk in range(0, 32, 4):
                ss = wp_in[pc * 32 * 128 + kk * 128: pc * 32 * 128 + (kk + 4) * 128,
                           oc * 128:(oc + 1) * 128].rearrange("(kc p) n -> p kc n", p=128)
                dd = wp_sb[:, dst_off + kk * 128:dst_off + (kk + 4) * 128].rearrange(
                    "p (kc n) -> p kc n", kc=4)
                v = P.dma('gpsimd', dd, ss, ldw)
            return v

        pieces[(0, 0)] = load_piece(0, 0)
        pseq = [(oc, pc) for oc in range(NCH) for pc in range(2)]
        for idx, (oc, pc) in enumerate(pseq):
            if idx + 1 < len(pseq):
                noc, npc = pseq[idx + 1]
                if idx >= 1:
                    P.wait('gpsimd', P.c['tensor'], dmm[pseq[idx - 1]])
                pieces[(noc, npc)] = load_piece(noc, npc)
            P.wait('tensor', ldw, pieces[(oc, pc)])
            if pc == 0 and oc >= 2:
                P.wait('tensor', P.c['vector'], yv[oc - 2])
            woff = (idx % 2) * 32 * 128
            bank = psY[oc % 2]
            for kk in range(32):
                j = pc * 32 + kk
                P.call('tensor', 'matmul', out=bank[:, :],
                       lhsT=wp_sb[:, woff + kk * 128:woff + (kk + 1) * 128],
                       rhs=h_sb[:, j * 512:(j + 1) * 512],
                       start=(j == 0), stop=(j == NJ - 1))
            dmm[(oc, pc)] = P.c['tensor'].n
            if pc == 1:
                P.wait('vector', P.c['tensor'], dmm[(oc, pc)])
                if oc == 0:
                    P.wait('vector', P.c['vector'], brv)
                ys = y_sb[:, oc * 512:(oc + 1) * 512]
                yv[oc] = P.call('vector', 'tensor_mul', out=ys, in0=bank[:, :], in1=bR[:, :])
                stats_chunk(P, consts, ys, yb_sb[:, 0:512], sq_sb, ps_s1, ps_s2,
                            acc1, acc2, 512)

        vv = ln_finalize(P, consts, D, acc1, acc2, rA, rR, "eps_ln", 512)
        P.wait('tensor', P.c['vector'], vv)
        P.call('tensor', 'matmul', out=psG[:, :], lhsT=consts['ones1_f32'][:, :],
               rhs=rR[0:1, :], start=True, stop=True)
        P.call('tensor', 'matmul', out=psU[:, :], lhsT=consts['ones1_f32'][:, :],
               rhs=rA[0:1, :], start=True, stop=True)
        tb = P.c['tensor'].n
        P.wait('vector', P.c['tensor'], tb)
        P.call('vector', 'tensor_copy', out=Rb[:, :], in_=psG[:, :])
        P.call('vector', 'tensor_copy', out=Ab[:, :], in_=psU[:, :])
        P.calln('vector', 'memset', ap=acc1[:, :], constant=0.0)
        P.calln('vector', 'memset', ap=acc2[:, :], constant=0.0)
        P.wait('vector', ld, xv)
        x2v = {}
        for oc in range(NCH):
            ys = y_sb[:, oc * 512:(oc + 1) * 512]
            xs = x1_sb[:, oc * 512:(oc + 1) * 512]
            P.call('vector', 'tensor_mul', out=ys, in0=ys, in1=Rb[:, :])
            P.call('vector', 'tensor_add', out=ys, in0=ys, in1=Ab[:, :])
            x2v[oc] = P.call('vector', 'tensor_add', out=xs, in0=xs, in1=ys)
            P.wait('sync', P.c['vector'], x2v[oc])
            P.dma('sync', x2_out[oc * 128:(oc + 1) * 128, t0:t0 + 512], xs, sto)
            stats_chunk(P, consts, xs, yb_sb[:, 0:512], sq_sb, ps_s1, ps_s2,
                        acc1, acc2, 512)
        vv = ln_finalize(P, consts, D, acc1, acc2, rA, rR, "eps_ln", 512)
        P.wait('tensor', P.c['vector'], vv)
        P.call('tensor', 'matmul', out=psG[:, :], lhsT=consts['ones1_f32'][:, :],
               rhs=rR[0:1, :], start=True, stop=True)
        P.call('tensor', 'matmul', out=psU[:, :], lhsT=consts['ones1_f32'][:, :],
               rhs=rA[0:1, :], start=True, stop=True)
        tb = P.c['tensor'].n
        P.wait('vector', P.c['tensor'], tb)
        P.call('vector', 'tensor_copy', out=Rb[:, :], in_=psG[:, :])
        P.call('vector', 'tensor_copy', out=Ab[:, :], in_=psU[:, :])
        xnb_d = {}
        for oc in range(NCH):
            xs = x1_sb[:, oc * 512:(oc + 1) * 512]
            xnb = yb_sb[:, (oc % 2) * 512:(oc % 2) * 512 + 512]
            if oc >= 2:
                P.wait('vector', sto, xnb_d[oc - 2])
            P.call('vector', 'tensor_mul', out=xs, in0=xs, in1=Rb[:, :])
            vv = P.call('vector', 'tensor_add', out=xnb, in0=xs, in1=Ab[:, :])
            P.wait('sync', P.c['vector'], vv)
            xnb_d[oc] = P.dma('sync', xn3_out[oc * 128:(oc + 1) * 128, t0:t0 + 512],
                              xnb, sto)
        prev_sto = sto.n
    P.wait('sync', sto, sto.n)
    if _P is None:
        return P.finish()


def build_ffn1(dbg=False, _P=None, _consts=None, _io=None):
    if _P is None:
        nc = new_nc()
        _io = {
            "xn4": nc.dram_tensor("xn4", [D, TOK], BF16, kind="ExternalInput").ap(),
            "up": nc.dram_tensor("up", [D, H1], BF16, kind="ExternalInput").ap(),
            "dn": nc.dram_tensor("dn", [H1, D], BF16, kind="ExternalInput").ap(),
            "upsc": nc.dram_tensor("upsc", [128, H1 // 128], F32, kind="ExternalInput").ap(),
            "dsc": nc.dram_tensor("dsc", [128, NCH], F32, kind="ExternalInput").ap(),
            "osc": nc.dram_tensor("osc", [1, 1], F32, kind="ExternalInput").ap(),
            "x3": nc.dram_tensor("x3", [D, TOK], F32, kind="ExternalInput").ap(),
            "x4": nc.dram_tensor("x4", [D, TOK], F32, kind="ExternalOutput").ap(),
        }
        if dbg:
            _io["h_dbg"] = nc.dram_tensor("h_dbg", [H1, 512], BF16, kind="ExternalOutput").ap()
            _io["y_dbg"] = nc.dram_tensor("y_dbg", [D, 512], F32, kind="ExternalOutput").ap()
            _io["s_dbg"] = nc.dram_tensor("s_dbg", [128, 17], F32, kind="ExternalOutput").ap()
        P = Prog(nc)
        consts = make_consts(P, [("eps_ln", LN_EPS)])
    else:
        P, consts = _P, _consts
        dbg = False
    xn_in = _io["xn4"]
    up_in = _io["up"]
    dn_in = _io["dn"]
    upsc_in = _io["upsc"]
    dsc_in = _io["dsc"]
    osc_in = _io["osc"]
    x3_in = _io["x3"]
    x4_out = _io["x4"]
    if dbg:
        h_dbg = _io["h_dbg"]
        y_dbg = _io["y_dbg"]
        s_dbg = _io["s_dbg"]
    NJ = H1 // 128

    xn_sb = P.sbuf("xn_sb", [128, NCH * 512], BF16)
    h_sb = P.sbuf("h_sb", [128, NJ * 512], BF16)
    wu_sb = P.sbuf("wu_sb", [128, 2 * NCH * 128], BF16)
    wd_sb = P.sbuf("wd_sb", [128, 2 * 32 * 128], BF16)
    x3_sb = P.sbuf("x3_sb", [128, NCH * 512], F32)
    y_sb = P.sbuf("y_sb", [128, NCH * 512], F32)
    sq_sb = P.sbuf("sq_sb", [128, 512], BF16)
    yb_sb = P.sbuf("yb_sb", [128, 2 * 512], BF16)
    upsc_sb = P.sbuf("upsc_sb", [128, NJ], F32)
    dsc_sb = P.sbuf("dsc_sb", [128, NCH], F32)
    osc_sb = P.sbuf("osc_sb", [1, 1], F32)
    osb = P.sbuf("osb", [128, 1], F32)
    dscm = P.sbuf("dscm", [128, NCH], F32)
    acc1 = P.sbuf("acc1", [1, 512], F32)
    acc2 = P.sbuf("acc2", [1, 512], F32)
    rA = P.sbuf("rA", [1, 512], F32)
    rR = P.sbuf("rR", [1, 512], F32)
    Rb = P.sbuf("Rb", [128, 512], F32)
    Ab = P.sbuf("Ab", [128, 512], F32)
    psU = P.psum("psU", [128, 512])
    psY = [P.psum("psY0", [128, 512]), P.psum("psY1", [128, 512])]
    ps_s1 = P.psum("ps_s1", [1, 512])
    ps_s2 = P.psum("ps_s2", [1, 512])
    ps_b = P.psum("ps_b", [128, 512])
    ps_b2 = P.psum("ps_b2", [128, 512])
    ldw = P.sem("ldw")
    ld = P.sem("ld")
    sto = P.sem("sto")

    P.wait('scalar', P.c['vector'], consts['_v'])

    P.dma('sync', upsc_sb[:, :], upsc_in[:, :], ld)
    P.dma('sync', dsc_sb[:, :], dsc_in[:, :], ld)
    scv = P.dma('sync', osc_sb[:, :], osc_in[:, :], ld)
    P.wait('tensor', ld, scv)
    P.wait('tensor', P.c['vector'], consts['_v'])
    P.call('tensor', 'matmul', out=ps_b[:, 0:1], lhsT=consts['ones1_f32'][:, :],
           rhs=osc_sb[:, :], start=True, stop=True)
    tb = P.c['tensor'].n
    P.wait('vector', P.c['tensor'], tb)
    P.wait('vector', ld, scv)
    cv = P.call('vector', 'tensor_copy', out=osb[:, :], in_=ps_b[:, 0:1])
    P.wait('vector', P.c['vector'], cv)
    dscmv = P.call('vector', 'tensor_scalar', out=dscm[:, :], in0=dsc_sb[:, :],
                   scalar1=osb[:, 0:1], scalar2=None, op0=ALU.mult)
    if dbg:
        P.wait('sync', P.c['vector'], dscmv)
        P.dma('sync', s_dbg[:, 0:1], osb[:, :], sto, allow_slow_non_contiguous=True)
        P.dma('sync', s_dbg[:, 1:17], dscm[:, :], sto, allow_slow_non_contiguous=True)

    for half in range(2):
        t0 = half * 512
        if half == 1:
            P.wait('sync', sto, prev_sto)
        lv = None
        for kc0 in range(0, NCH, 4):
            src = xn_in[kc0 * 128:(kc0 + 4) * 128, t0:t0 + 512].rearrange(
                "(kc p) n -> p kc n", p=128)
            dst = xn_sb[:, kc0 * 512:(kc0 + 4) * 512].rearrange("p (kc n) -> p kc n", kc=4)
            lv = P.dma('sync', dst, src, ld)
        xv = None
        for kc0 in range(0, NCH, 4):
            src = x3_in[kc0 * 128:(kc0 + 4) * 128, t0:t0 + 512].rearrange(
                "(kc p) n -> p kc n", p=128)
            dst = x3_sb[:, kc0 * 512:(kc0 + 4) * 512].rearrange("p (kc n) -> p kc n", kc=4)
            xv = P.dma('sync', dst, src, ld)
        if half == 1:
            P.wait('tensor', ld, lv)
            P.wait('tensor', P.c['vector'], P.c['vector'].n)
            P.wait('tensor', P.c['scalar'], P.c['scalar'].n)
        P.calln('vector', 'memset', ap=acc1[:, :], constant=0.0)
        P.calln('vector', 'memset', ap=acc2[:, :], constant=0.0)

        wv = {}
        wv[0] = load_w_cols(P, 'gpsimd', ldw, up_in, 0, 128, wu_sb, 0, D)
        umm = {}
        hs_s = {}
        for j in range(NJ):
            woff = (j % 2) * NCH * 128
            if j + 1 < NJ:
                if j >= 1:
                    P.wait('gpsimd', P.c['tensor'], umm[j - 1])
                wv[j + 1] = load_w_cols(P, 'gpsimd', ldw, up_in, (j + 1) * 128, 128,
                                        wu_sb, ((j + 1) % 2) * NCH * 128, D)
            P.wait('tensor', ldw, wv[j])
            if half == 0 and j == 0:
                P.wait('tensor', ld, lv)
            if j >= 1:
                P.wait('tensor', P.c['scalar'], hs_s[j - 1])
            for kc in range(NCH):
                P.call('tensor', 'matmul', out=psU[:, :],
                       lhsT=wu_sb[:, woff + kc * 128:woff + (kc + 1) * 128],
                       rhs=xn_sb[:, kc * 512:(kc + 1) * 512],
                       start=(kc == 0), stop=(kc == NCH - 1))
            umm[j] = P.c['tensor'].n
            P.wait('scalar', P.c['tensor'], umm[j])
            hs_s[j] = P.call('scalar', 'activation', out=h_sb[:, j * 512:(j + 1) * 512],
                             in_=psU[:, :], func=AF.Relu,
                             bias=consts['zero_b'][:, :], scale=upsc_sb[:, j:j + 1])
            if dbg and half == 0:
                P.wait('sync', P.c['scalar'], hs_s[j])
                P.dma('sync', h_dbg[j * 128:(j + 1) * 128, :],
                      h_sb[:, j * 512:(j + 1) * 512], sto)
        wpv = {}

        def load_piece1(oc, pc):
            dst_off = ((2 * oc + pc) % 2) * 32 * 128
            v = None
            for kk in range(0, 32, 4):
                ss = dn_in[pc * 32 * 128 + kk * 128: pc * 32 * 128 + (kk + 4) * 128,
                           oc * 128:(oc + 1) * 128].rearrange("(kc p) n -> p kc n", p=128)
                dd = wd_sb[:, dst_off + kk * 128:dst_off + (kk + 4) * 128].rearrange(
                    "p (kc n) -> p kc n", kc=4)
                v = P.dma('gpsimd', dd, ss, ldw)
            return v

        dmm = {}
        yv = {}
        wpv[(0, 0)] = load_piece1(0, 0)
        P.wait('tensor', P.c['scalar'], hs_s[NJ - 1])
        pseq = [(oc, pc) for oc in range(NCH) for pc in range(2)]
        for idx, (oc, pc) in enumerate(pseq):
            if idx + 1 < len(pseq):
                if idx >= 1:
                    P.wait('gpsimd', P.c['tensor'], dmm[pseq[idx - 1]])
                wpv[pseq[idx + 1]] = load_piece1(*pseq[idx + 1])
            P.wait('tensor', ldw, wpv[(oc, pc)])
            if pc == 0 and oc >= 2:
                P.wait('tensor', P.c['vector'], yv[oc - 2])
            woff = (idx % 2) * 32 * 128
            bank = psY[oc % 2]
            for kk in range(32):
                j = pc * 32 + kk
                P.call('tensor', 'matmul', out=bank[:, :],
                       lhsT=wd_sb[:, woff + kk * 128:woff + (kk + 1) * 128],
                       rhs=h_sb[:, j * 512:(j + 1) * 512],
                       start=(j == 0), stop=(j == NJ - 1))
            dmm[(oc, pc)] = P.c['tensor'].n
            if pc == 1:
                P.wait('vector', P.c['tensor'], dmm[(oc, pc)])
                if half == 0 and oc == 0:
                    P.wait('vector', P.c['vector'], dscmv)
                ys = y_sb[:, oc * 512:(oc + 1) * 512]
                yv[oc] = P.call('vector', 'tensor_scalar', out=ys, in0=bank[:, :],
                                scalar1=dscm[:, oc:oc + 1], scalar2=None, op0=ALU.mult)
                if dbg and half == 0:
                    P.wait('sync', P.c['vector'], yv[oc])
                    P.dma('sync', y_dbg[oc * 128:(oc + 1) * 128, :], ys, sto)
                stats_chunk(P, consts, ys, yb_sb[:, 0:512], sq_sb, ps_s1, ps_s2,
                            acc1, acc2, 512)

        vv = ln_finalize(P, consts, D, acc1, acc2, rA, rR, "eps_ln", 512)
        P.wait('tensor', P.c['vector'], vv)
        P.call('tensor', 'matmul', out=ps_b[:, :], lhsT=consts['ones1_f32'][:, :],
               rhs=rR[0:1, :], start=True, stop=True)
        P.call('tensor', 'matmul', out=ps_b2[:, :], lhsT=consts['ones1_f32'][:, :],
               rhs=rA[0:1, :], start=True, stop=True)
        tb = P.c['tensor'].n
        P.wait('vector', P.c['tensor'], tb)
        P.call('vector', 'tensor_copy', out=Rb[:, :], in_=ps_b[:, :])
        P.call('vector', 'tensor_copy', out=Ab[:, :], in_=ps_b2[:, :])
        P.wait('vector', ld, xv)
        for oc in range(NCH):
            ys = y_sb[:, oc * 512:(oc + 1) * 512]
            xs = x3_sb[:, oc * 512:(oc + 1) * 512]
            P.call('vector', 'tensor_mul', out=ys, in0=ys, in1=Rb[:, :])
            P.call('vector', 'tensor_add', out=ys, in0=ys, in1=Ab[:, :])
            vv = P.call('vector', 'tensor_add', out=xs, in0=xs, in1=ys)
            P.wait('sync', P.c['vector'], vv)
            P.dma('sync', x4_out[oc * 128:(oc + 1) * 128, t0:t0 + 512], xs, sto)
        prev_sto = sto.n
    P.wait('sync', sto, sto.n)
    if _P is None:
        return P.finish()


_BUILD_CACHE = {}


def _built(key, fn):
    if key not in _BUILD_CACHE:
        _BUILD_CACHE[key] = fn()
    return _BUILD_CACHE[key]


_CONST_CACHE = {}


def _host_consts():
    if _CONST_CACHE:
        return _CONST_CACHE
    C0, S0t, P0 = _rope_tables_l0()
    C1, S1t, P1 = _rope_tables_l1()
    _CONST_CACHE.update(dict(C0=C0, S0=S0t, P0=P0, C1=C1, S1=S1t, P1=P1,
                             M0=_masks_l0(), M1=_masks_l1()))
    Ebf, Ebc = _group_mats()
    _CONST_CACHE.update(dict(Ebf=Ebf, Ebc=Ebc))
    return _CONST_CACHE


def _ct(a):
    return np.ascontiguousarray(a)


FUSED = True


def kernel(**inp):
    if FUSED:
        return kernel_fused(inp)
    f32 = np.float32
    HC = _host_consts()
    xT = _ct(np.asarray(inp["L0_attn_x"], f32)[0].T)
    Wqkv = np.asarray(inp["L0_attn_Wqkv"], f32)
    bqkv = np.asarray(inp["L0_attn_bqkv"], f32)
    Wout = _ct(np.asarray(inp["L0_attn_Wout"], f32))
    bout = np.asarray(inp["L0_attn_bout"], f32)
    wgu_t = _ct(np.asarray(inp["L0_ffn_w_gate_up"], f32).T)
    wp_t = _ct(np.asarray(inp["L0_ffn_w_proj"], f32).T)
    w1q = np.asarray(inp["L1_attn_w_q"], f32)
    w1k = np.asarray(inp["L1_attn_w_k"], f32)
    w1v = np.asarray(inp["L1_attn_w_v"], f32)
    w1o = _ct(np.asarray(inp["L1_attn_w_o"], f32))
    up_t = _ct(np.asarray(inp["L1_ffn_up_weight"], f32).T)
    dn_t = _ct(np.asarray(inp["L1_ffn_down_weight"], f32).T)
    upsc = np.asarray(inp["L1_ffn_up_scale"], f32)
    dsc = np.asarray(inp["L1_ffn_down_scale"], f32)
    osc = np.asarray(inp["L1_ffn_output_scale"], f32).reshape(1, 1)

    cols = [slice(TOK * c, TOK * (c + 1)) for c in range(NCORES)]

    nc = _built("quant", build_quant)
    qres = run8(nc, [{
        "wgu_sh": _ct(wgu_t[256 * c:256 * (c + 1)]),
        "wp_sh": _ct(wp_t[1024 * c:1024 * (c + 1)]),
        "up_sh": _ct(up_t[256 * c:256 * (c + 1)]),
        "dn_sh": _ct(dn_t[1024 * c:1024 * (c + 1)]),
        "Ebf": HC["Ebf"], "Ebc": HC["Ebc"],
    } for c in range(NCORES)])
    WGUQ = np.concatenate([r["wgu_q"] for r in qres], axis=0)
    WPQ = np.concatenate([r["wp_q"] for r in qres], axis=0)
    UPQ = np.concatenate([r["up_q"] for r in qres], axis=0)
    DNQ = np.concatenate([r["dn_q"] for r in qres], axis=0)

    nc = _built("ln0", build_ln0)
    res = run8(nc, [{"x_own": _ct(xT[:, cols[c]])} for c in range(NCORES)])
    Xn1 = np.concatenate([r["xn"] for r in res], axis=1)

    nc = _built("qkv0", lambda: build_qkv(0))
    res = run8(nc, [{
        "xn_full": Xn1,
        "wq": _ct(Wqkv[:, 256 * c:256 * (c + 1)]),
        "wk": _ct(Wqkv[:, 2048 + 256 * c:2048 + 256 * (c + 1)]),
        "wv": _ct(Wqkv[:, 4096 + 256 * c:4096 + 256 * (c + 1)]),
        "bq": _ct(bqkv[256 * c:256 * (c + 1)].reshape(2, 128).T),
        "bk": _ct(bqkv[2048 + 256 * c:2048 + 256 * (c + 1)].reshape(2, 128).T),
        "bv": _ct(bqkv[4096 + 256 * c:4096 + 256 * (c + 1)].reshape(2, 128).T),
        "Ctab": HC["C0"], "Stab": HC["S0"], "Pmat": HC["P0"],
    } for c in range(NCORES)])
    q0 = [r["qt"] for r in res]
    k0 = [r["kt"] for r in res]
    Vt0 = np.concatenate([r["vt"] for r in res], axis=0)
    VH0 = _vaug_l0(Vt0)
    K0full = np.concatenate(k0, axis=0)

    nc = _built("attn0", build_attn0)
    res = run8(nc, [{
        "qt": q0[c], "kt": _ct(K0full[256 * c:256 * (c + 1)]),
        "vh": VH0[c], "m0": HC["M0"],
    } for c in range(NCORES)])
    A0 = np.concatenate([r["at"] for r in res], axis=0)

    nc = _built("proj_b", lambda: build_proj(True))
    res = run8(nc, [{
        "a_cols": _ct(A0[:, cols[c]]),
        "w": Wout,
        "b": _ct(bout.reshape(NCH, 128).T),
        "x_cols": _ct(xT[:, cols[c]]),
    } for c in range(NCORES)])
    x1 = [r["x1"] for r in res]
    xn2 = [r["xn"] for r in res]

    nc = _built("ffn0", build_ffn0)
    res = run8(nc, [{
        "xn2": xn2[c], "wgu": WGUQ, "wp": WPQ, "x1": x1[c],
    } for c in range(NCORES)])
    x2 = [r["x2"] for r in res]
    Xn3 = np.concatenate([r["xn3"] for r in res], axis=1)

    nc = _built("qkv1", lambda: build_qkv(1))
    res = run8(nc, [{
        "xn_full": Xn3,
        "wq": _ct(w1q[:, 256 * c:256 * (c + 1)]),
        "wk": _ct(w1k[:, 128 * (c // 2):128 * (c // 2) + 128]),
        "wv": _ct(w1v[:, 128 * (c // 2):128 * (c // 2) + 128]),
        "Ctab": HC["C1"], "Stab": HC["S1"], "Pmat": HC["P1"],
    } for c in range(NCORES)])
    q1 = [r["qt"] for r in res]
    k1 = [r["kt"] for r in res]
    v1 = [r["vt"] for r in res]

    nc = _built("attn1", build_attn1)
    in_maps = []
    for c in range(NCORES):
        vh = np.asarray(v1[c])
        vr = vh.reshape(128, NKT, 128)
        VA = np.ones((128, NKT, 65), NPBF16)
        VA[:, :, :64] = vr[:64].transpose(2, 1, 0)
        VB = _ct(vr[64:].transpose(2, 1, 0))
        in_maps.append({"qt": q1[c], "kt": k1[c], "va": VA, "vb": VB, "m1": HC["M1"]})
    res = run8(nc, in_maps)
    A1 = np.concatenate([r["at"] for r in res], axis=0)

    nc = _built("proj_nb", lambda: build_proj(False))
    res = run8(nc, [{
        "a_cols": _ct(A1[:, cols[c]]),
        "w": w1o,
        "x_cols": x2[c],
    } for c in range(NCORES)])
    x3 = [r["x1"] for r in res]
    xn4 = [r["xn"] for r in res]

    nc = _built("ffn1", build_ffn1)
    res = run8(nc, [{
        "xn4": xn4[c], "up": UPQ, "dn": DNQ,
        "upsc": _ct(upsc.reshape(H1 // 128, 128).T),
        "dsc": _ct(dsc.reshape(NCH, 128).T),
        "osc": osc,
        "x3": x3[c],
    } for c in range(NCORES)])

    out = np.empty((1, S, D), f32)
    for c in range(NCORES):
        out[0, TOK * c:TOK * (c + 1), :] = np.asarray(res[c]["x4"]).T
    return out


RG8 = [[0, 1, 2, 3, 4, 5, 6, 7]]


def _collective(P, kind, ins, outs, sem):
    h = sem.h

    def f(E, kind=kind, ins=ins, outs=outs, h=h):
        from concourse.aot_env import allow_ici_all_to_all
        with allow_ici_all_to_all():
            E.collective_compute(kind, ALU.bypass, replica_groups=RG8,
                                 ins=ins, outs=outs).then_inc(h, 1)

    P.q['gpsimd'].append(f)
    sem.n += 1
    return sem.n


def _load_v0_fused(v0t):
    def f(P, consts, v_sb, ldsem):
        vtok = P.sbuf("vtok", [128, NKT * 256], BF16)
        P.calln('vector', 'memset', ap=v_sb[:, :], constant=1.0)
        lv = ldsem.n
        for kt in range(NKT):
            lv = P.dma('sync', vtok[:, kt * 256:(kt + 1) * 256],
                       v0t[:, kt * 128:(kt + 1) * 128], ldsem, transpose=True)
        P.wait('vector', ldsem, lv)
        vv = None
        for hh in range(HPC0):
            dst = v_sb[:, hh * NKT * 65:(hh + 1) * NKT * 65].rearrange(
                "p (t c) -> p t c", c=65)[:, :, 0:64]
            src = vtok[:, :].rearrange("p (t c) -> p t c", c=256)[:, :, hh * 64:(hh + 1) * 64]
            vv = P.call('vector', 'tensor_copy', out=dst, in_=src)
        P.wait('tensor', P.c['vector'], vv)
        P.wait('scalar', P.c['vector'], vv)
        return ldsem.n
    return f


def _load_v1_fused(v1t):
    def f(P, consts, va_sb, vb_sb, ldsem):
        vtok = P.sbuf("vtok1", [128, NKT * 128], BF16)
        P.calln('vector', 'memset', ap=va_sb[:, :], constant=1.0)
        lv = ldsem.n
        for kt in range(NKT):
            lv = P.dma('sync', vtok[:, kt * 128:(kt + 1) * 128],
                       v1t[:, kt * 128:(kt + 1) * 128], ldsem, transpose=True)
        P.wait('vector', ldsem, lv)
        dst = va_sb[:, :].rearrange("p (t c) -> p t c", c=65)[:, :, 0:64]
        src = vtok[:, :].rearrange("p (t c) -> p t c", c=128)[:, :, 0:64]
        P.call('vector', 'tensor_copy', out=dst, in_=src)
        dstb = vb_sb[:, :].rearrange("p (t c) -> p t c", c=64)
        srcb = vtok[:, :].rearrange("p (t c) -> p t c", c=128)[:, :, 64:128]
        vv = P.call('vector', 'tensor_copy', out=dstb, in_=srcb)
        P.wait('tensor', P.c['vector'], vv)
        P.wait('scalar', P.c['vector'], vv)
        return ldsem.n
    return f


def build_fused(dbg=False):
    nc = new_nc()

    def EI(name, shape, dt):
        return nc.dram_tensor(name, shape, dt, kind="ExternalInput").ap()

    def INT(name, shape, dt):
        return nc.dram_tensor(name, shape, dt, kind="Internal").ap()

    def SHD(name, shape, dt):
        return nc.dram_tensor(name, shape, dt, kind="Internal", addr_space="Shared").ap()

    ei = {}
    for nm, shape, dt in [
        ("x_own", [D, TOK], F32),
        ("w0q", [D, 256], F32), ("w0k", [D, 256], F32), ("w0v", [D, 256], F32),
        ("b0q", [128, 2], F32), ("b0k", [128, 2], F32), ("b0v", [128, 2], F32),
        ("C0", [128, S], BF16), ("S0", [128, S], BF16), ("P0", [128, 128], BF16),
        ("C1", [128, S], BF16), ("S1", [128, S], BF16), ("P1", [128, 128], BF16),
        ("m0", [4, 128, 512], BF16), ("m1", [12, 128, 512], BF16),
        ("wout", [D, D], F32), ("bout", [128, NCH], F32),
        ("w1q", [D, 256], F32), ("w1k", [D, 128], F32), ("w1v", [D, 128], F32),
        ("w1o", [D, D], F32),
        ("wgu_sh", [D // 8, 2 * H0], F32), ("wp_sh", [H0 // 8, D], F32),
        ("up_sh", [D // 8, H1], F32), ("dn_sh", [H1 // 8, D], F32),
        ("Ebf", [128, 2], BF16), ("Ebc", [2, 128], F32),
        ("upsc", [128, H1 // 128], F32), ("dsc", [128, NCH], F32), ("osc", [1, 1], F32),
    ]:
        ei[nm] = EI(nm, shape, dt)
    x4_out = nc.dram_tensor("x4", [D, TOK], F32, kind="ExternalOutput").ap()

    DBG = (lambda name, shape, dt: nc.dram_tensor(name, shape, dt,
           kind="ExternalOutput").ap()) if dbg else INT
    xn1_own = INT("xn1_own", [D, TOK], BF16)
    Xn1g = SHD("Xn1g", [8 * D, TOK], BF16)
    q0t = DBG("q0t_d", [256, S], BF16)
    k0t = DBG("k0t_d", [256, S], BF16)
    v0t = DBG("v0t_d", [256, S], BF16)
    a2a0_in = INT("a2a0_in", [D, TOK], BF16)
    a2a0_out = INT("a2a0_out", [D, TOK], BF16)
    x1d = DBG("x1_d", [D, TOK], F32)
    xn2d = DBG("xn2_d", [D, TOK], BF16)
    wguq_sh = INT("wguq_sh", [D // 8, 2 * H0], BF16)
    wpq_sh = INT("wpq_sh", [H0 // 8, D], BF16)
    upq_sh = INT("upq_sh", [D // 8, H1], BF16)
    dnq_sh = INT("dnq_sh", [H1 // 8, D], BF16)
    WGUQg = SHD("WGUQg", [D, 2 * H0], BF16)
    WPQg = SHD("WPQg", [H0, D], BF16)
    UPQg = SHD("UPQg", [D, H1], BF16)
    DNQg = SHD("DNQg", [H1, D], BF16)
    x2d = DBG("x2_d", [D, TOK], F32)
    xn3_own = INT("xn3_own", [D, TOK], BF16)
    Xn3g = SHD("Xn3g", [8 * D, TOK], BF16)
    q1t = INT("q1t_d", [256, S], BF16)
    k1t = INT("k1t_d", [128, S], BF16)
    v1t = INT("v1t_d", [128, S], BF16)
    a2a1_in = INT("a2a1_in", [D, TOK], BF16)
    a2a1_out = INT("a2a1_out", [D, TOK], BF16)
    x3d = DBG("x3_d", [D, TOK], F32)
    xn4d = DBG("xn4_d", [D, TOK], BF16)
    if dbg:
        globals()['_dbg_extra'] = True

    P = Prog(nc)
    consts = make_consts(P, [("eps_ln", LN_EPS), ("eps_rms", RMS_EPS),
                             ("negC", -EXPC), ("mhalf", -0.5)])
    cc = {nm: P.sem("cc_" + nm) for nm in
          ("xn1", "a0", "xn3", "a1", "wgu", "wp", "up", "dn")}

    def xn_block_g(gbuf):
        def f(tb, kc0, kc1, _g=gbuf):
            r = tb // 2
            col = (tb % 2) * 512
            return _g[r * D + kc0 * 128: r * D + kc1 * 128, col:col + 512]
        return f

    sto = P.sem("sto")
    st = P.sem("st")
    with P.scope():
        build_quant(_P=P, _consts=consts, _io={
            "wgu_sh": ei["wgu_sh"], "wp_sh": ei["wp_sh"],
            "up_sh": ei["up_sh"], "dn_sh": ei["dn_sh"],
            "wgu_q": wguq_sh, "wp_q": wpq_sh, "up_q": upq_sh, "dn_q": dnq_sh,
            "Ebf": ei["Ebf"], "Ebc": ei["Ebc"],
        })
    P.wait('gpsimd', sto, sto.n)
    _collective(P, "AllGather", [wguq_sh[:, :]], [WGUQg[:, :]], cc["wgu"])
    _collective(P, "AllGather", [wpq_sh[:, :]], [WPQg[:, :]], cc["wp"])
    _collective(P, "AllGather", [upq_sh[:, :]], [UPQg[:, :]], cc["up"])
    _collective(P, "AllGather", [dnq_sh[:, :]], [DNQg[:, :]], cc["dn"])
    P.barrier()

    with P.scope():
        build_ln0(_P=P, _consts=consts, _io={"x_own": ei["x_own"], "xn": xn1_own})
    P.wait('gpsimd', st, st.n)
    _collective(P, "AllGather", [xn1_own[:, :]], [Xn1g[:, :]], cc["xn1"])
    P.barrier()

    P.wait('sync', cc["xn1"], 1)
    with P.scope():
        build_qkv(0, _P=P, _consts=consts, _io={
            "xn_block": xn_block_g(Xn1g),
            "wq": ei["w0q"], "wk": ei["w0k"], "wv": ei["w0v"],
            "bq": ei["b0q"], "bk": ei["b0k"], "bv": ei["b0v"],
            "qt": q0t, "kt": k0t, "vt": v0t,
            "Ctab": ei["C0"], "Stab": ei["S0"], "Pmat": ei["P0"],
        })
    P.barrier()

    with P.scope():
        build_attn0(_P=P, _consts=consts, _io={
            "qt": q0t, "kt": k0t, "m0": ei["m0"],
            "load_v": _load_v0_fused(v0t),
            "a_dst": (lambda j, hh, _a=a2a0_in:
                      _a[(j // 2) * 256 + hh * 64:(j // 2) * 256 + hh * 64 + 64,
                         (j % 2) * 512:(j % 2) * 512 + 512]),
        })
    P.wait('gpsimd', sto, sto.n)
    _collective(P, "AllToAll", [a2a0_in[:, :]], [a2a0_out[:, :]], cc["a0"])
    P.barrier()

    P.wait('sync', cc["a0"], 1)
    with P.scope():
        build_proj(True, _P=P, _consts=consts, _io={
            "a_cols": a2a0_out, "w": ei["wout"], "b": ei["bout"],
            "x_cols": ei["x_own"], "x1": x1d, "xn": xn2d,
        })
    P.barrier()

    P.wait('gpsimd', cc["wgu"], 1)
    P.wait('gpsimd', cc["wp"], 1)
    with P.scope():
        build_ffn0(_P=P, _consts=consts, _io={
            "xn2": xn2d, "wgu": WGUQg, "wp": WPQg, "x1": x1d,
            "x2": x2d, "xn3": xn3_own,
        })
    P.wait('gpsimd', sto, sto.n)
    _collective(P, "AllGather", [xn3_own[:, :]], [Xn3g[:, :]], cc["xn3"])
    P.barrier()

    P.wait('sync', cc["xn3"], 1)
    with P.scope():
        build_qkv(1, _P=P, _consts=consts, _io={
            "xn_block": xn_block_g(Xn3g),
            "wq": ei["w1q"], "wk": ei["w1k"], "wv": ei["w1v"],
            "qt": q1t, "kt": k1t, "vt": v1t,
            "Ctab": ei["C1"], "Stab": ei["S1"], "Pmat": ei["P1"],
        })
    P.barrier()

    with P.scope():
        build_attn1(_P=P, _consts=consts, _io={
            "qt": q1t, "kt": k1t, "m1": ei["m1"],
            "load_v1": _load_v1_fused(v1t),
            "a_dst": (lambda j, qh, _a=a2a1_in:
                      _a[(j // 2) * 256 + qh * 128:(j // 2) * 256 + qh * 128 + 128,
                         (j % 2) * 512:(j % 2) * 512 + 512]),
        })
    P.wait('gpsimd', sto, sto.n)
    _collective(P, "AllToAll", [a2a1_in[:, :]], [a2a1_out[:, :]], cc["a1"])
    P.barrier()

    P.wait('sync', cc["a1"], 1)
    with P.scope():
        build_proj(False, _P=P, _consts=consts, _io={
            "a_cols": a2a1_out, "w": ei["w1o"],
            "x_cols": x2d, "x1": x3d, "xn": xn4d,
        })
    P.barrier()

    P.wait('gpsimd', cc["up"], 1)
    P.wait('gpsimd', cc["dn"], 1)
    with P.scope():
        build_ffn1(_P=P, _consts=consts, _io={
            "xn4": xn4d, "up": UPQg, "dn": DNQg,
            "upsc": ei["upsc"], "dsc": ei["dsc"], "osc": ei["osc"],
            "x3": x3d, "x4": x4_out,
        })
    return P.finish()


def kernel_fused(inp):
    f32 = np.float32
    HC = _host_consts()
    xT = _ct(np.asarray(inp["L0_attn_x"], f32)[0].T)
    Wqkv = np.asarray(inp["L0_attn_Wqkv"], f32)
    bqkv = np.asarray(inp["L0_attn_bqkv"], f32)
    Wout = _ct(np.asarray(inp["L0_attn_Wout"], f32))
    bout = np.asarray(inp["L0_attn_bout"], f32)
    wgu_t = _ct(np.asarray(inp["L0_ffn_w_gate_up"], f32).T)
    wp_t = _ct(np.asarray(inp["L0_ffn_w_proj"], f32).T)
    w1q = np.asarray(inp["L1_attn_w_q"], f32)
    w1k = np.asarray(inp["L1_attn_w_k"], f32)
    w1v = np.asarray(inp["L1_attn_w_v"], f32)
    w1o = _ct(np.asarray(inp["L1_attn_w_o"], f32))
    up_t = _ct(np.asarray(inp["L1_ffn_up_weight"], f32).T)
    dn_t = _ct(np.asarray(inp["L1_ffn_down_weight"], f32).T)
    upsc = np.asarray(inp["L1_ffn_up_scale"], f32)
    dsc = np.asarray(inp["L1_ffn_down_scale"], f32)
    osc = np.asarray(inp["L1_ffn_output_scale"], f32).reshape(1, 1)

    nc = _built("fused", build_fused)
    in_maps = []
    for c in range(NCORES):
        g = c // 2
        in_maps.append({
            "x_own": _ct(xT[:, TOK * c:TOK * (c + 1)]),
            "w0q": _ct(Wqkv[:, 256 * c:256 * (c + 1)]),
            "w0k": _ct(Wqkv[:, 2048 + 256 * c:2048 + 256 * (c + 1)]),
            "w0v": _ct(Wqkv[:, 4096 + 256 * c:4096 + 256 * (c + 1)]),
            "b0q": _ct(bqkv[256 * c:256 * (c + 1)].reshape(2, 128).T),
            "b0k": _ct(bqkv[2048 + 256 * c:2048 + 256 * (c + 1)].reshape(2, 128).T),
            "b0v": _ct(bqkv[4096 + 256 * c:4096 + 256 * (c + 1)].reshape(2, 128).T),
            "C0": HC["C0"], "S0": HC["S0"], "P0": HC["P0"],
            "C1": HC["C1"], "S1": HC["S1"], "P1": HC["P1"],
            "m0": HC["M0"], "m1": HC["M1"],
            "wout": Wout, "bout": _ct(bout.reshape(NCH, 128).T),
            "w1q": _ct(w1q[:, 256 * c:256 * (c + 1)]),
            "w1k": _ct(w1k[:, 128 * g:128 * g + 128]),
            "w1v": _ct(w1v[:, 128 * g:128 * g + 128]),
            "w1o": w1o,
            "wgu_sh": _ct(wgu_t[256 * c:256 * (c + 1)]),
            "wp_sh": _ct(wp_t[1024 * c:1024 * (c + 1)]),
            "up_sh": _ct(up_t[256 * c:256 * (c + 1)]),
            "dn_sh": _ct(dn_t[1024 * c:1024 * (c + 1)]),
            "Ebf": HC["Ebf"], "Ebc": HC["Ebc"],
            "upsc": _ct(upsc.reshape(H1 // 128, 128).T),
            "dsc": _ct(dsc.reshape(NCH, 128).T),
            "osc": osc,
        })
    res = run8(nc, in_maps)
    globals()["_last_res"] = res
    out = np.empty((1, S, D), f32)
    for c in range(NCORES):
        out[0, TOK * c:TOK * (c + 1), :] = np.asarray(res[c]["x4"]).T
    return out
